# version 05
# baseline (speedup 1.0000x reference)
import numpy as np
import concourse.bass as bass
import concourse.mybir as mybir
from concourse.bass_utils import run_bass_kernel_spmd

F32 = mybir.dt.float32
AF = mybir.ActivationFunctionType
ALU = mybir.AluOpType
AX = mybir.AxisListType

C = 1024
H = 16
N = 64
LCH = 64
EPS_GN = 0.00064
EPS_LN = 1e-5
EPS_RMS = 1e-6
LATENT = 512
N_EXP = 8
EXP_DIM = 2048
SCL_W = -float(np.exp(-0.5))
CONST_VALS = [EPS_LN, EPS_GN, EPS_RMS, -4.0, 1e-12]


def host_consts():
    c = {}
    c["c_ident"] = np.eye(128, dtype=np.float32)
    c["c_i64b"] = np.tile(np.eye(64, dtype=np.float32), (2, 2))
    c["c_negI"] = (-1e9 * np.eye(128)).astype(np.float32)
    c["c_u01"] = np.tril(np.ones((128, 128), np.float32), -1)
    i_idx = (np.arange(128) % 64)[:, None]
    s_idx = np.tile(np.arange(64), 16)[None, :]
    c["c_mask_s"] = (s_idx < i_idx).astype(np.float32)
    c["c_maskT_s"] = (s_idx > i_idx).astype(np.float32)
    c["c_maskT_i"] = (s_idx >= i_idx).astype(np.float32)
    ind = np.zeros((128, 128), np.float32)
    for f in range(8):
        for p in range(128):
            ind[p, f * 16 + 2 * f + (1 if p >= 64 else 0)] = 1.0
    c["c_indgn"] = ind
    sel = np.zeros((16, 8 * 128), np.float32)
    for f in range(8):
        for m in range(128):
            sel[2 * f + (1 if m >= 64 else 0), f * 128 + m] = 1.0
    c["c_sel"] = sel
    c["c_iota8"] = np.tile(np.arange(8, dtype=np.float32)[None, :], (128, 1))
    c["c_iotar"] = 999.0 - c["c_iota8"]
    c["c_bias"] = np.tile(np.array(CONST_VALS, np.float32)[None, :], (128, 1))
    return c


_HC = host_consts()
CONST_NAMES = list(_HC.keys())

ATTN_VECS = ("mix_r", "mix_w", "mix_k", "mix_v", "mix_a", "mix_g", "w0", "a0",
             "k_k", "k_a", "r_k", "ln_g", "ln_b", "fg1", "fg2")


class K:
    def __init__(self, T=2048, SB=128, SD=512, SE=256):
        self.T, self.SB, self.SD, self.SE = T, SB, SD, SE
        assert T % SB == 0 and SB % LCH == 0 and T % SD == 0 and T % SE == 0
        self.nc = bass.Bass("TRN2", target_bir_lowering=False,
                            detect_race_conditions=False)
        self.sems = {}
        self.cnt = {}
        self.waited = {}

    def mark(self, inst, name, kk=1):
        inst.then_inc(self.sems[name], kk)
        self.cnt[name] = self.cnt.get(name, 0) + kk
        return (name, self.cnt[name])

    def ev(self, name):
        return (name, self.cnt.get(name, 0))

    def flush_dve(self, nc):
        e = self.mark(nc.vector.engine_nop(), "dve")
        self.wait(nc.scalar, e)
        e2 = self.mark(nc.scalar.nop(), "act")
        self.wait(nc.vector, e2)

    def wait(self, eng, *events):
        for event in events:
            if event is None:
                continue
            name, val = event
            if val <= 0:
                continue
            key = (id(eng), name)
            if self.waited.get(key, 0) >= val:
                continue
            eng.wait_ge(self.sems[name], val)
            self.waited[key] = val



class _PERec:

    def __init__(self, eng, k):
        self._e = eng
        self._k = k
        self.last = None
        self._last_ev = None
        self._last_ev_inst = None
        self._lastwin = None

    def ev_last(self):
        if self.last is None:
            return None
        if self._last_ev_inst is not self.last:
            self._last_ev = self._k.mark(self.last, "pe")
            self._last_ev_inst = self.last
        return self._last_ev

    def _guard(self, kbase, ksz):
        kwin = (kbase, ksz)
        if (self._lastwin is not None and ksz < 128 and self._lastwin[1] < 128
                and kwin != self._lastwin and self.last is not None):
            self._k.wait(self._e, self.ev_last())
        self._lastwin = kwin

    def matmul(self, out, lhsT, rhs, **kw):
        tp = kw.get("tile_position")
        kbase = tp[0] if tp is not None else 0
        self._guard(kbase, lhsT.shape[0])
        self.last = self._e.matmul(out, lhsT, rhs, **kw)
        return self.last

    def transpose(self, out, in_, ident, **kw):
        tp = kw.get("tile_position")
        kbase = tp[0] if tp is not None else 0
        self._guard(kbase, in_.shape[0])
        self.last = self._e.transpose(out, in_, ident, **kw)
        return self.last

    def __getattr__(self, nm):
        return getattr(self._e, nm)


def build_program(T=2048, SB=128, SD=512, SE=256, stages=("b", "d", "e"),
                  dbg_names=()):
    k = K(T=T, SB=SB, SD=SD, SE=SE)
    nc = k.nc
    ctx = nc.ctx

    din = {}

    def dram_in(name, shape):
        din[name] = nc.dram_tensor(name, list(shape), F32, kind="ExternalInput").ap()

    dram_in("xT", (C, T))
    for nm in CONST_NAMES:
        dram_in(nm, _HC[nm].shape)
    for nm, sh in [("w1", (C, 64)), ("w2", (64, C)), ("a1", (C, 64)), ("a2", (64, C)),
                   ("g1", (C, 128)), ("g2", (128, C)),
                   ("Wr", (C, C)), ("Wk", (C, C)), ("Wv", (C, C)), ("Wo", (C, C)),
                   ("Wq", (C, C)), ("Wdkv", (C, LATENT)), ("Wuk", (LATENT, C)),
                   ("Wuv", (LATENT, C)), ("Wof", (C, C)), ("gate_w", (C, N_EXP)),
                   ("we1", (N_EXP, C, EXP_DIM)), ("we2", (N_EXP, EXP_DIM, C))]:
        dram_in(nm, sh)
    for nm in ATTN_VECS:
        dram_in(nm, (C,))

    out = nc.dram_tensor("out", [T, C], F32, kind="ExternalOutput").ap()
    dbg = {}
    for nm, sh in dbg_names:
        dbg[nm] = nc.dram_tensor(nm, list(sh), F32, kind="ExternalOutput").ap()

    def scr(name, shape):
        return nc.dram_tensor(name, list(shape), F32, kind="Internal").ap()

    x2scr = scr("x2scr", (C, T))
    h1scr = scr("h1scr", (C, T))
    ybscr = scr("ybscr", (T, C))
    kscr = scr("kscr", (C, T))
    vscr = scr("vscr", (T // 128, 128, H * 65))

    for s in ("dS", "dG", "pe", "act", "dve", "gp"):
        k.sems[s] = ctx.enter_context(nc.semaphore(s))

    sbp = {}

    def S(name, shape):
        sbp[name] = ctx.enter_context(nc.sbuf_tensor(name, list(shape), F32)).ap()
        return sbp[name]

    for nm in CONST_NAMES:
        S("s" + nm, _HC[nm].shape)
    S("s_onescol", (128, 1))
    S("s_onesrow", (1, 128))
    S("s_vrow", (1, 1024))
    S("s_gamb", (128, 1024))
    S("s_betb", (128, 1024))
    for nm in ATTN_VECS:
        S("v_" + nm, (128, 8))
    S("v_ka1m", (128, 8))
    S("s_w1", (128, 8 * 64))
    S("s_w2", (64, 1024))
    S("s_a1", (128, 8 * 64))
    S("s_a2", (64, 1024))
    S("s_g1", (128, 8 * 128))
    S("s_g2", (128, 1024))
    S("s_gw", (128, 8 * 8))
    S("s_SS", (128, 8, 64))
    S("s_ln1last", (128, 8))
    S("s_e05n", (128, 1))

    for i, v in enumerate(CONST_VALS):
        nc.const_aps.register(F32, float(v), sbp["sc_bias"][:, i:i + 1])

    PSs = ctx.enter_context(nc.psum_tensor("PS", [128, 4096], F32)).ap()
    pb = [PSs[:, i * 512:(i + 1) * 512] for i in range(8)]
    pb.append(PSs)

    blk = ctx.enter_context(nc.Block())

    def body(sync):
        PE, ACT, DVE, GP = _PERec(nc.tensor, k), nc.scalar, nc.vector, nc.gpsimd
        for nm in CONST_NAMES:
            k.mark(sync.dma_start(out=sbp["s" + nm], in_=din[nm]), "dS", 16)
        for nm in ATTN_VECS:
            k.mark(sync.dma_start(out=sbp["v_" + nm],
                                  in_=din[nm].rearrange("(f p) -> p f", p=128),
                                  allow_slow_non_contiguous=True), "dS", 16)
        k.mark(sync.dma_start(out=sbp["s_w1"].rearrange("p (kc l) -> p kc l", l=64),
                              in_=din["w1"].rearrange("(kc p) l -> p kc l", p=128)), "dS", 16)
        k.mark(sync.dma_start(out=sbp["s_w2"], in_=din["w2"]), "dS", 16)
        k.mark(sync.dma_start(out=sbp["s_a1"].rearrange("p (kc l) -> p kc l", l=64),
                              in_=din["a1"].rearrange("(kc p) l -> p kc l", p=128)), "dS", 16)
        k.mark(sync.dma_start(out=sbp["s_a2"], in_=din["a2"]), "dS", 16)
        k.mark(sync.dma_start(out=sbp["s_g1"].rearrange("p (kc l) -> p kc l", l=128),
                              in_=din["g1"].rearrange("(kc p) l -> p kc l", p=128)), "dS", 16)
        k.mark(sync.dma_start(out=sbp["s_g2"], in_=din["g2"]), "dS", 16)
        ld = k.mark(sync.dma_start(out=sbp["s_gw"].rearrange("p (kc e) -> p kc e", e=8),
                                   in_=din["gate_w"].rearrange("(kc p) e -> p kc e", p=128)), "dS", 16)
        k.wait(DVE, ld)
        DVE.memset(sbp["s_onescol"], 1.0)
        DVE.memset(sbp["s_onesrow"], 1.0)
        DVE.tensor_scalar(sbp["v_ka1m"], sbp["v_k_a"], -1.0, 1.0, ALU.mult, ALU.add)
        DVE.memset(sbp["s_SS"], 0.0)
        DVE.memset(sbp["s_e05n"], -0.5)
        e = k.mark(DVE.memset(sbp["s_ln1last"], 0.0), "dve")
        k.wait(ACT, e)
        e = k.mark(ACT.activation(sbp["s_e05n"], sbp["s_e05n"], AF.Exp), "act")
        k.wait(DVE, e)
        e = k.mark(DVE.tensor_scalar_mul(sbp["s_e05n"], sbp["s_e05n"], -1.0), "dve")
        k.flush_dve(nc)
        for vec, dst in (("ln_g", "s_gamb"), ("ln_b", "s_betb")):
            k.wait(sync, k.ev("pe"))
            lv = k.mark(sync.dma_start(out=sbp["s_vrow"][0:1, :], in_=din[vec][None, :]), "dS", 16)
            k.wait(PE, lv, e)
            PE.matmul(pb[0][:, 0:512], sbp["s_onesrow"], sbp["s_vrow"][0:1, 0:512],
                      start=True, stop=True)
            PE.matmul(pb[1][:, 0:512], sbp["s_onesrow"],
                      sbp["s_vrow"][0:1, 512:1024], start=True, stop=True)
            ep = PE.ev_last()
            k.wait(DVE, ep)
            DVE.tensor_copy(sbp[dst][:, 0:512], pb[0][:, 0:512])
            e = k.mark(DVE.tensor_copy(sbp[dst][:, 512:1024], pb[1][:, 0:512]), "dve")

        if "b" in stages:
            phase_b(k, nc, din, dbg, sbp, pb, x2scr, sync)
        if "d" in stages:
            phase_d(k, nc, din, dbg, sbp, pb, x2scr, h1scr, ybscr, kscr, vscr, sync)
        if "e" in stages:
            phase_e(k, nc, din, dbg, sbp, pb, h1scr, ybscr, out, sync)

        GP.wait_ge(k.sems["dG"], k.cnt.get("dG", 0))
        sync.wait_ge(k.sems["dS"], k.cnt.get("dS", 0))
        PE.wait_ge(k.sems["dve"], k.cnt.get("dve", 0))
        PE.wait_ge(k.sems["act"], k.cnt.get("act", 0))

    @blk.sync
    def _(sync):
        body(sync)

    return nc, k


def phase_b(k, nc, din, dbg, sbp, pb, x2scr, sync):
    import os
    BSTEPS = int(os.environ.get("BSTEPS", "5"))
    T, SB = k.T, k.SB
    NSL = T // SB
    NCH = SB // LCH
    NTC = SB // 128
    PE, ACT, DVE, GP = _PERec(nc.tensor, k), nc.scalar, nc.vector, nc.gpsimd
    xT = din["xT"].rearrange("(f p) t -> p f t", p=128)
    x2v = x2scr.rearrange("(f p) t -> p f t", p=128)

    V = {nm: sbp["v_" + nm] for nm in ATTN_VECS}
    onescol, onesrow = sbp["s_onescol"], sbp["s_onesrow"]
    ident, i64b = sbp["sc_ident"], sbp["sc_i64b"]
    indgn, sel = sbp["sc_indgn"], sbp["sc_sel"]
    mask_s, maskT_s, maskT_i = sbp["sc_mask_s"], sbp["sc_maskT_s"], sbp["sc_maskT_i"]
    w1s, w2s, a1s, a2s = sbp["s_w1"], sbp["s_w2"], sbp["s_a1"], sbp["s_a2"]
    g1s, g2s = sbp["s_g1"], sbp["s_g2"]
    SS, ln1_last = sbp["s_SS"], sbp["s_ln1last"]
    gamb, betb = sbp["s_gamb"], sbp["s_betb"]
    ka1m = sbp["v_ka1m"]

    from contextlib import ExitStack
    es = ExitStack()

    def A(name, shape):
        return es.enter_context(nc.sbuf_tensor(name, list(shape), F32)).ap()

    assert SB == 128, "phase_b overlay pool assumes SB=128"
    pool1 = A("b_pool1", [128, 2048])
    x = A("b_x", [128, 8, SB])
    ln = pool1[:, 0:1024].rearrange("p (f t) -> p f t", t=SB)
    xx = pool1[:, 1024:2048].rearrange("p (f t) -> p f t", t=SB)
    mix = A("b_mix", [128, 8, SB])
    r_ = A("b_r", [128, 8, SB])
    kf = A("b_kf", [128, 8, SB])
    kkn = A("b_kkn", [128, 8, SB])
    bq = A("b_bq", [128, 8, SB])
    cc = A("b_cc", [128, 8, SB])
    vtm = A("b_vtm", [128, NTC, 1024])
    s1 = A("b_s1", [128, SB])
    sgb = A("b_sgb", [16, SB])
    wbuf = A("b_wbuf", [128, 2, 1024])
    lb = A("b_lb", [64, SB])
    smask = A("b_smask", [128, SB])
    rowm = A("b_rowm", [1, SB])
    rowr = A("b_rowr", [1, SB])
    rown = A("b_rown", [16, SB])
    E = A("b_E", [128, 8, LCH])
    Ei = A("b_Ei", [128, 8, LCH])
    Ep = A("b_Ep", [128, 8, LCH])
    EcL = A("b_EcL", [128, 8])
    rt = A("b_rt", [128, 8, LCH])
    at = A("b_at", [128, 8, LCH])
    bp_ = A("b_bp", [128, 8, LCH])
    kp = A("b_kp", [128, 8, LCH])
    ks = A("b_ks", [128, 8, LCH])
    bs = A("b_bs", [128, 8, LCH])
    Ap = A("b_Ap", [64, 1024])
    AkaT = A("b_AkaT", [64, 1024])
    ArbT = A("b_ArbT", [64, 1024])
    ArkT = A("b_ArkT", [64, 1024])
    U = A("b_U", [64, 1024])
    V2 = A("b_V2", [64, 1024])
    kst = A("b_kst", [64, 1024])
    bst = A("b_bst", [64, 1024])
    otm = A("b_otm", [128, NTC, 1024])
    gmu = A("b_gmu", [128, 64])[:, 0:16]
    gsc = A("b_gsc", [128, 64])[:, 0:16]
    grs = A("b_grs", [128, 64])[:, 0:16]
    sbT = A("b_sbT", [128, 64])[:, 0:16]
    ytm = pool1[:, 0:1024]
    zfm = pool1[:, 1024:2048].rearrange("p (f t) -> p f t", t=SB)
    Cl = [A("b_Cl%d" % i, [64, 1024]) for i in range(4)]
    Cl.append(pool1[0:64, 0:1024])
    Cl.append(pool1[0:64, 1024:2048])
    try:
        sq = mix

        DVE.memset(smask, 1.0)
        e_mask = k.mark(DVE.memset(
            smask.rearrange("p (c l) -> p c l", l=LCH)[:, :, 0:1], 0.0), "dve")

        def stream_W(name, kc, slot, pe_guard="auto"):
            k.wait(sync, k.ev("pe") if pe_guard == "auto" else pe_guard)
            return k.mark(sync.dma_start(out=wbuf[:, slot, :],
                                         in_=din[name][kc * 128:(kc + 1) * 128, :]), "dS", 16)

        for s in range(NSL):
            t0 = s * SB
            sl = slice(t0, t0 + SB)
            k.wait(ACT, k.ev("pe"), k.ev("dve"))
            k.wait(DVE, k.ev("pe"), k.ev("act"))
            k.wait(PE, k.ev("act"), k.ev("dve"))
            LNS = int(os.environ.get("LNSTEP", "9"))
            e_x = k.mark(sync.dma_start(out=x, in_=xT[:, :, sl]), "dS", 16)
            if LNS < 2:
                continue
            k.wait(ACT, e_x)
            e_sq = k.mark(ACT.activation(sq, x, AF.Square), "act")
            if LNS < 3:
                continue
            k.wait(PE, e_x, e_sq, k.ev("dve"))
            for f in range(8):
                PE.matmul(pb[6][0:1, 0:SB], onescol, x[:, f, :], start=(f == 0), stop=False)
            for f in range(8):
                PE.matmul(pb[6][0:1, SB:2 * SB], onescol, sq[:, f, :],
                          start=False, stop=(f == 7))
            e = PE.ev_last()
            if LNS < 4:
                continue
            LSUB = int(os.environ.get("LNSUB", "9"))
            k.wait(DVE, e)
            DVE.tensor_scalar_mul(rowm, pb[6][0:1, 0:SB], 1.0 / C)
            e = k.mark(DVE.tensor_scalar_mul(rowr, pb[6][0:1, SB:2 * SB], 1.0 / C), "dve")
            if LSUB >= 2:
                e = k.mark(DVE.tensor_mul(pb[6][0:1, 0:SB], rowm, rowm), "dve")
            if LSUB >= 3:
                DVE.tensor_sub(rowr, rowr, pb[6][0:1, 0:SB])
                e = k.mark(DVE.tensor_scalar_max(rowr, rowr, 0.0), "dve")
            if LSUB >= 4:
                k.wait(ACT, e)
                e = k.mark(ACT.activation(rowr, rowr, AF.Sqrt, bias=EPS_LN), "act")
            if LSUB >= 5:
                k.wait(DVE, e)
                e = k.mark(DVE.reciprocal(rowr, rowr), "dve")
            if LNS < 5:
                continue
            k.wait(PE, e)
            PE.matmul(pb[6][:, 0:SB], onesrow, rowm, start=True, stop=True)
            PE.matmul(pb[7][:, 0:SB], onesrow, rowr, start=True, stop=True)
            e = PE.ev_last()
            if LNS < 6:
                continue
            k.wait(DVE, e)
            for f in range(8):
                DVE.tensor_sub(ln[:, f, :], x[:, f, :], pb[6][:, 0:SB])
            for f in range(8):
                DVE.tensor_mul(ln[:, f, :], ln[:, f, :], pb[7][:, 0:SB])
            for f in range(8):
                DVE.tensor_sub(xx[:, f, 1:SB], ln[:, f, 0:SB - 1], ln[:, f, 1:SB])
                DVE.scalar_tensor_tensor(xx[:, f, 0:1], ln[:, f, 0:1], -1.0,
                                         ln1_last[:, f:f + 1], ALU.mult, ALU.add)
            e_ln = k.mark(DVE.tensor_copy(ln1_last, ln[:, :, SB - 1]), "dve")

            if BSTEPS < 2:
                continue
            def mk_mix(mvec, pe_guard):
                k.wait(DVE, pe_guard)
                for f in range(8):
                    DVE.scalar_tensor_tensor(mix[:, f, :], xx[:, f, :],
                                             mvec[:, f:f + 1], ln[:, f, :],
                                             ALU.mult, ALU.add)
                return k.mark(DVE.engine_nop(), "dve")

            def proj_fm(wname, evmix, evac):
                evw = [stream_W(wname, 0, 0), None]
                e_pe_prev = None
                for kc in range(8):
                    if kc + 1 < 8:
                        evw[(kc + 1) % 2] = stream_W(wname, kc + 1, (kc + 1) % 2,
                                                     pe_guard=e_pe_prev)
                    k.wait(PE, evw[kc % 2], evmix)
                    W = wbuf[:, kc % 2, :]
                    for oc in range(8):
                        bank = pb[oc // 2]
                        col = (oc % 2) * SB
                        PE.matmul(bank[:, col:col + SB], W[:, oc * 128:(oc + 1) * 128],
                                  mix[:, kc, :], start=(kc == 0 and oc % 2 == 0),
                                  stop=(kc == 7 and oc % 2 == 1))
                    e_pe_prev = PE.ev_last()
                for oc in range(8):
                    evac(oc, pb[oc // 2][:, (oc % 2) * SB:(oc % 2) * SB + SB], e_pe_prev)
                return e_pe_prev

            evm = mk_mix(V["mix_r"], k.ev("pe"))

            def ev_r(oc, ap, epe):
                k.wait(DVE, epe)
                k.mark(DVE.tensor_copy(r_[:, oc, :], ap), "dve")
            e_projr = proj_fm("Wr", evm, ev_r)

            evm = mk_mix(V["mix_k"], e_projr)

            def ev_k(oc, ap, epe):
                k.wait(DVE, epe)
                k.mark(DVE.tensor_copy(kf[:, oc, :], ap), "dve")
            e_projk = proj_fm("Wk", evm, ev_k)

            evm = mk_mix(V["mix_v"], e_projk)
            evw = [stream_W("Wv", 0, 0), None]
            e_pe_prev = None
            for kc in range(8):
                if kc + 1 < 8:
                    evw[(kc + 1) % 2] = stream_W("Wv", kc + 1, (kc + 1) % 2, pe_guard=e_pe_prev)
                k.wait(PE, evw[kc % 2], evm)
                W = wbuf[:, kc % 2, :]
                for tc in range(NTC):
                    for hf in range(2):
                        PE.matmul(pb[2 * tc + hf][:, 0:512],
                                  mix[:, kc, tc * 128:(tc + 1) * 128],
                                  W[:, hf * 512:(hf + 1) * 512],
                                  start=(kc == 0), stop=(kc == 7))
                e_pe_prev = PE.ev_last()
            k.wait(DVE, e_pe_prev)
            for tc in range(NTC):
                DVE.tensor_copy(vtm[:, tc, 0:512], pb[2 * tc][:, 0:512])
                k.mark(DVE.tensor_copy(vtm[:, tc, 512:1024], pb[2 * tc + 1][:, 0:512]), "dve")

            if BSTEPS < 3:
                continue
            e_kkt = None
            k.wait(DVE, k.ev("pe"))
            for f in range(8):
                DVE.tensor_scalar_mul(kkn[:, f, :], kf[:, f, :], V["k_k"][:, f:f + 1])
            e_kkt = k.mark(DVE.engine_nop(), "dve")
            k.wait(ACT, e_kkt)
            e_sq2 = k.mark(ACT.activation(mix, kkn, AF.Square), "act")
            k.wait(PE, e_sq2)
            for f in range(8):
                PE.matmul(pb[4][0:16, 0:SB], indgn[:, f * 16:(f + 1) * 16],
                          mix[:, f, :], start=(f == 0), stop=(f == 7))
            e = PE.ev_last()
            k.wait(ACT, e)
            e = k.mark(ACT.activation(rown, pb[4][0:16, 0:SB], AF.Sqrt), "act")
            k.wait(DVE, e)
            DVE.tensor_scalar_max(rown, rown, 1e-12)
            e = k.mark(DVE.reciprocal(rown, rown), "dve")
            k.wait(PE, e)
            for f in range(8):
                k.wait(PE, k.ev("dve"))
                PE.matmul(pb[4 + f % 2][:, 0:SB], sel[:, f * 128:(f + 1) * 128],
                          rown, start=True, stop=True)
                e = PE.ev_last()
                k.wait(DVE, e)
                k.mark(DVE.tensor_mul(kkn[:, f, :], kkn[:, f, :], pb[4 + f % 2][:, 0:SB]), "dve")

            evm = mk_mix(V["mix_w"], k.ev("pe"))
            k.wait(PE, evm)
            for kc in range(8):
                PE.matmul(pb[0][0:64, 0:SB], w1s[:, kc * 64:(kc + 1) * 64],
                          mix[:, kc, :], start=(kc == 0), stop=(kc == 7))
            e = PE.ev_last()
            k.wait(ACT, e)
            e_tanh = k.mark(ACT.activation(lb, pb[0][0:64, 0:SB], AF.Tanh), "act")
            k.wait(PE, e_tanh)
            for ocp in range(4):
                bank = pb[1 + ocp]
                for j in range(2):
                    oc = 2 * ocp + j
                    PE.matmul(bank[:, j * SB:(j + 1) * SB],
                              w2s[:, oc * 128:(oc + 1) * 128], lb,
                              start=(j == 0), stop=(j == 1))
                e = PE.ev_last()
                k.wait(ACT, e)
                ACT.activation(cc[:, 2 * ocp, :], bank[:, 0:SB], AF.Sigmoid,
                               bias=V["w0"][:, 2 * ocp:2 * ocp + 1])
                e_cc = k.mark(ACT.activation(cc[:, 2 * ocp + 1, :], bank[:, SB:2 * SB],
                                             AF.Sigmoid,
                                             bias=V["w0"][:, 2 * ocp + 1:2 * ocp + 2]), "act")

            evm = mk_mix(V["mix_a"], k.ev("pe"))
            k.wait(PE, evm)
            for kc in range(8):
                PE.matmul(pb[0][0:64, 0:SB], a1s[:, kc * 64:(kc + 1) * 64],
                          mix[:, kc, :], start=(kc == 0), stop=(kc == 7))
            e = PE.ev_last()
            k.wait(DVE, e)
            e_za = k.mark(DVE.tensor_copy(lb, pb[0][0:64, 0:SB]), "dve")
            k.wait(PE, e_za)
            for ocp in range(4):
                bank = pb[1 + ocp]
                for j in range(2):
                    oc = 2 * ocp + j
                    PE.matmul(bank[:, j * SB:(j + 1) * SB],
                              a2s[:, oc * 128:(oc + 1) * 128], lb,
                              start=(j == 0), stop=(j == 1))
                e = PE.ev_last()
                k.wait(ACT, e)
                ACT.activation(bq[:, 2 * ocp, :], bank[:, 0:SB], AF.Sigmoid,
                               bias=V["a0"][:, 2 * ocp:2 * ocp + 1])
                e_ag = k.mark(ACT.activation(bq[:, 2 * ocp + 1, :], bank[:, SB:2 * SB],
                                             AF.Sigmoid,
                                             bias=V["a0"][:, 2 * ocp + 1:2 * ocp + 2]), "act")

            evm = mk_mix(V["mix_g"], k.ev("pe"))
            k.wait(PE, evm)
            for kc in range(8):
                PE.matmul(pb[0][:, 0:SB], g1s[:, kc * 128:(kc + 1) * 128],
                          mix[:, kc, :], start=(kc == 0), stop=(kc == 7))
            e_g1pe = PE.ev_last()
            k.wait(ACT, e_g1pe)
            e_s1 = k.mark(ACT.activation(s1, pb[0][:, 0:SB], AF.Sigmoid), "act")

            k.wait(DVE, e_ag, e_cc, e_g1pe)
            for f in range(8):
                DVE.tensor_scalar(mix[:, f, :], bq[:, f, :], V["k_a"][:, f:f + 1],
                                  None, ALU.mult)
                DVE.tensor_scalar_add(mix[:, f, :], mix[:, f, :], ka1m[:, f:f + 1])
            DVE.tensor_mul(kf, kf, mix)
            for f in range(8):
                DVE.tensor_scalar_mul(mix[:, f, :], r_[:, f, :], V["r_k"][:, f:f + 1])
            e = k.mark(DVE.tensor_mul(mix, mix, kf), "dve")
            k.wait(PE, e)
            k.wait(PE, k.ev("dve"))
            for f in range(8):
                PE.matmul(pb[4][0:16, 0:SB], indgn[:, f * 16:(f + 1) * 16],
                          mix[:, f, :], start=(f == 0), stop=(f == 7))
            e = PE.ev_last()
            k.wait(DVE, e)
            k.mark(DVE.tensor_copy(sgb, pb[4][0:16, 0:SB]), "dve")
            e_bq = k.mark(DVE.tensor_mul(bq, kkn, bq), "dve")
            k.wait(DVE, e_cc, e_mask)
            for f in range(8):
                DVE.tensor_tensor_scan(cc[:, f, :], smask, cc[:, f, :], 0.0,
                                       ALU.mult, ALU.add)
            for f in range(8):
                DVE.tensor_scalar_mul(cc[:, f, :], cc[:, f, :],
                                      sbp["s_e05n"][:, 0:1])
            e_scan = k.mark(DVE.engine_nop(), "dve")

            if BSTEPS < 4:
                continue
            psW0 = pb[8][0:64, 0:1024]
            psW1 = pb[8][0:64, 1024:2048]
            psO = pb[8][:, 2048:3072]
            psSa = pb[8][0:64, 3072:3584]
            psSb = pb[8][64:128, 3584:4096]

            prev_e_rs = None
            prev_e_ss = None
            for ch in range(NCH):
                tc = ch // 2
                rb = (ch % 2) * 64
                csl = slice(ch * LCH, (ch + 1) * LCH)
                k.wait(ACT, e_scan, e_bq, prev_e_rs, prev_e_ss)
                ACT.activation(E, cc[:, :, csl], AF.Exp)
                e1 = k.mark(ACT.activation(Ei, cc[:, :, csl], AF.Exp, scale=-1.0), "act")
                e2 = k.mark(ACT.activation(EcL, cc[:, :, ch * LCH + LCH - 1],
                                           AF.Exp), "act")
                k.wait(DVE, e1, e2)
                DVE.memset(Ep[:, :, 0:1], 1.0)
                DVE.tensor_copy(Ep[:, :, 1:LCH], E[:, :, 0:LCH - 1])
                DVE.tensor_mul(rt, r_[:, :, csl], E)
                DVE.scalar_tensor_tensor(at, Ep, -1.0, kkn[:, :, csl], ALU.mult, ALU.mult)
                DVE.tensor_mul(bp_, bq[:, :, csl], Ei)
                DVE.tensor_mul(kp, kf[:, :, csl], Ei)
                for f in range(8):
                    DVE.tensor_scalar_mul(ks[:, f, :], Ei[:, f, :], EcL[:, f:f + 1])
                DVE.tensor_mul(bs, bq[:, :, csl], ks)
                e_rs = k.mark(DVE.tensor_mul(ks, kf[:, :, csl], ks), "dve")
                prev_e_rs = e_rs
                CST = int(os.environ.get("CSTEP", "9"))
                if CST < 2:
                    continue

                HEADS = [(h, h // 2, (h % 2) * 64) for h in range(16)]

                k.wait(PE, k.ev("dve"))
                for cb in range(2):
                    PE.matmul(psW0[:, cb * 512:(cb + 1) * 512],
                              i64b[rb:rb + 64, 0:64],
                              vtm[rb:rb + 64, tc, cb * 512:(cb + 1) * 512],
                              start=True, stop=True, tile_position=(rb, 0))
                e = PE.ev_last()
                k.wait(DVE, e)
                e_v2 = k.mark(DVE.tensor_copy(V2, psW0), "dve")

                if CST < 3:
                    continue
                specs = [
                    (Ap, at, bp_, mask_s),
                    (Cl[0], bp_, at, maskT_s),
                    (AkaT, kp, at, maskT_s),
                    (ArbT, bp_, rt, maskT_i),
                    (ArkT, kp, rt, maskT_i),
                ]
                for idx, (dst, lh, rh, msk) in enumerate(specs):
                    bank = psW1 if idx % 2 == 0 else psW0
                    k.wait(PE, k.ev("dve"))
                    for h, f, nh in HEADS:
                        PE.matmul(bank[:, h * 64:(h + 1) * 64],
                                  lh[nh:nh + 64, f, :], rh[nh:nh + 64, f, :],
                                  start=(h % 8 == 0), stop=(h % 8 == 7),
                                  tile_position=(nh, 0))
                    e = PE.ev_last()
                    k.wait(DVE, e)
                    k.mark(DVE.tensor_mul(dst, bank, msk[0:64, :]), "dve")

                if CST < 4:
                    continue
                for lev in range(1, 6):
                    k.wait(PE, k.ev("dve"))
                    for h, f, nh in HEADS:
                        PE.matmul(psW1[:, h * 64:(h + 1) * 64],
                                  Ap[:, h * 64:(h + 1) * 64],
                                  Cl[lev - 1][:, h * 64:(h + 1) * 64],
                                  start=(h % 8 == 0), stop=(h % 8 == 7),
                                  tile_position=(0, 0))
                    e_c2 = PE.ev_last()
                    if lev < 5:
                        for h, f, nh in HEADS:
                            PE.matmul(psW0[:, h * 64:(h + 1) * 64],
                                      Cl[lev - 1][:, h * 64:(h + 1) * 64],
                                      Ap[:, h * 64:(h + 1) * 64],
                                      start=(h % 8 == 0), stop=(h % 8 == 7),
                                      tile_position=(0, 0))
                        e_a2 = PE.ev_last()
                        k.wait(DVE, e_a2)
                        DVE.tensor_copy(Ap, psW0)
                    k.wait(DVE, e_c2)
                    k.mark(DVE.tensor_copy(Cl[lev], psW1), "dve")

                if CST < 5:
                    continue
                k.wait(PE, k.ev("dve"), e_v2)
                for h, f, nh in HEADS:
                    PE.matmul(psW0[:, h * 64:(h + 1) * 64],
                              AkaT[:, h * 64:(h + 1) * 64],
                              V2[:, h * 64:(h + 1) * 64],
                              start=(h % 8 == 0), stop=False, tile_position=(0, 0))
                for h, f, nh in HEADS:
                    PE.matmul(psW0[:, h * 64:(h + 1) * 64],
                              at[nh:nh + 64, f, :], SS[nh:nh + 64, f, :],
                              start=False, stop=(h % 8 == 7), tile_position=(nh, 0))
                e = PE.ev_last()
                k.wait(DVE, e)
                e_u = k.mark(DVE.tensor_copy(U, psW0), "dve")

                for lev in (5, 4, 3, 2, 1, 0):
                    k.wait(PE, e_u)
                    for h, f, nh in HEADS:
                        PE.matmul(psW0[:, h * 64:(h + 1) * 64],
                                  Cl[lev][:, h * 64:(h + 1) * 64],
                                  U[:, h * 64:(h + 1) * 64],
                                  start=(h % 8 == 0), stop=(h % 8 == 7),
                                  tile_position=(0, 0))
                    e = PE.ev_last()
                    k.wait(DVE, e)
                    e_u = k.mark(DVE.tensor_add(U, U, psW0), "dve")

                if CST < 6:
                    continue
                k.wait(PE, e_u)
                for h, f, nh in HEADS:
                    PE.matmul(psO[rb:rb + 64, h * 64:(h + 1) * 64],
                              rt[nh:nh + 64, f, :], SS[nh:nh + 64, f, :],
                              start=(h % 8 == 0), stop=False, tile_position=(nh, rb))
                    PE.matmul(psO[rb:rb + 64, h * 64:(h + 1) * 64],
                              ArbT[:, h * 64:(h + 1) * 64],
                              U[:, h * 64:(h + 1) * 64],
                              start=False, stop=False, tile_position=(0, rb))
                    PE.matmul(psO[rb:rb + 64, h * 64:(h + 1) * 64],
                              ArkT[:, h * 64:(h + 1) * 64],
                              V2[:, h * 64:(h + 1) * 64],
                              start=False, stop=(h % 8 == 7), tile_position=(0, rb))
                e_o = PE.ev_last()

                if CST < 7:
                    continue
                for h, f, nh in HEADS:
                    PE.matmul(psW0[:, h * 64:(h + 1) * 64],
                              ks[nh:nh + 64, f, :], i64b[nh:nh + 64, 0:64],
                              start=(h % 8 == 0), stop=(h % 8 == 7),
                              tile_position=(nh, 0))
                e_k2 = PE.ev_last()
                for h, f, nh in HEADS:
                    PE.matmul(psW1[:, h * 64:(h + 1) * 64],
                              bs[nh:nh + 64, f, :], i64b[nh:nh + 64, 0:64],
                              start=(h % 8 == 0), stop=(h % 8 == 7),
                              tile_position=(nh, 0))
                e_b2 = PE.ev_last()
                k.wait(DVE, e_k2, e_b2)
                DVE.tensor_copy(kst, psW0)
                e_t = k.mark(DVE.tensor_copy(bst, psW1), "dve")

                k.wait(PE, e_t)
                for par, bankS in ((0, psSa), (1, psSb)):
                    hs = [h for h in range(16) if h % 2 == par]
                    for i_h, h in enumerate(hs):
                        f = h // 2
                        nh = par * 64
                        PE.matmul(bankS[:, f * 64:(f + 1) * 64],
                                  bst[:, h * 64:(h + 1) * 64],
                                  U[:, h * 64:(h + 1) * 64],
                                  start=(i_h == 0), stop=False,
                                  tile_position=(0, nh))
                        PE.matmul(bankS[:, f * 64:(f + 1) * 64],
                                  kst[:, h * 64:(h + 1) * 64],
                                  V2[:, h * 64:(h + 1) * 64],
                                  start=False, stop=(i_h == 7),
                                  tile_position=(0, nh))
                e_st = PE.ev_last()
                k.wait(DVE, e_st)
                for f in range(8):
                    DVE.tensor_scalar_mul(SS[:, f, :], SS[:, f, :], EcL[:, f:f + 1])
                SSf = SS.rearrange("p a b -> p (a b)")
                DVE.tensor_add(SSf[0:64, :], SSf[0:64, :], psSa)
                e_ss = k.mark(DVE.tensor_add(SSf[64:128, :], SSf[64:128, :], psSb), "dve")
                prev_e_ss = e_ss

                if CST < 8:
                    continue
                k.wait(DVE, e_o)
                k.mark(DVE.tensor_copy(otm[rb:rb + 64, tc, :], psO[rb:rb + 64, :]), "dve")
                if "d_SS" in dbg:
                    gch_ = s * NCH + ch
                    k.wait(GP, e_ss)
                    k.mark(GP.dma_start(out=dbg["d_SS"][gch_ * 128:(gch_ + 1) * 128, 0:512],
                                        in_=SS.rearrange("p a b -> p (a b)")), "dG", 16)
                    k.wait(DVE, k.ev("dG"))
                    k.wait(PE, k.ev("dG"))

                if s == 0 and ch == 0:
                    for dnm, dt_ in (("d_Ap", Ap), ("d_Cl0", Cl[0]), ("d_Cl5", Cl[5]),
                                     ("d_AkaT", AkaT), ("d_ArbT", ArbT),
                                     ("d_ArkT", ArkT), ("d_U", U), ("d_V2", V2),
                                     ("d_kst", kst), ("d_bst", bst)):
                        if dnm in dbg:
                            k.wait(GP, k.ev("dve"))
                            k.mark(GP.dma_start(out=dbg[dnm], in_=dt_), "dG", 16)
                    if "d_at" in dbg:
                        k.wait(GP, k.ev("dve"))
                        for dnm, dt_ in (("d_at", at), ("d_bp", bp_), ("d_kp", kp),
                                         ("d_rt", rt), ("d_ks", ks), ("d_bs", bs),
                                         ("d_E", E), ("d_Ei", Ei), ("d_Ep", Ep)):
                            k.mark(GP.dma_start(
                                out=dbg[dnm],
                                in_=dt_.rearrange("p a b -> p (a b)")), "dG", 16)
                    k.wait(DVE, k.ev("dG"))
                    k.wait(PE, k.ev("dG"))
                    k.wait(ACT, k.ev("dG"))

            if BSTEPS < 5:
                continue
            evW = [stream_W("Wo", 0, 0), None]
            for tc in range(NTC):
                k.wait(PE, k.ev("dve"))
                PE.transpose(pb[4][0:128, 0:16],
                             sgb[0:16, tc * 128:(tc + 1) * 128],
                             ident[0:16, 0:16])
                e = PE.ev_last()
                k.wait(DVE, e)
                e_sbt = k.mark(DVE.tensor_copy(sbT, pb[4][0:128, 0:16]), "dve")
                k.flush_dve(nc)
                if s == 0 and tc == 0 and "d_otmB" in dbg:
                    k.wait(GP, k.ev("dve"))
                    k.mark(GP.dma_start(out=dbg["d_otmB"][0:128, :], in_=otm[:, 0, :]), "dG", 16)
                    k.wait(DVE, k.ev("dG"))
                for h in range(16):
                    DVE.tensor_reduce(gmu[:, h:h + 1],
                                      otm[:, tc, h * 64:(h + 1) * 64], AX.X, ALU.add)
                k.flush_dve(nc)
                e = k.mark(DVE.tensor_scalar_mul(gmu, gmu, 1.0 / 64), "dve")
                k.flush_dve(nc)
                for h in range(16):
                    DVE.tensor_scalar(ytm[:, h * 64:(h + 1) * 64],
                                      otm[:, tc, h * 64:(h + 1) * 64],
                                      gmu[:, h:h + 1], None, ALU.subtract)
                e = k.mark(DVE.engine_nop(), "dve")
                k.wait(ACT, e)
                zfl = zfm.rearrange("p a b -> p (a b)")
                e = k.mark(ACT.activation(zfl, ytm, AF.Square), "act")
                k.wait(DVE, e)
                for h in range(16):
                    DVE.tensor_reduce(grs[:, h:h + 1],
                                      zfl[:, h * 64:(h + 1) * 64], AX.X, ALU.add)
                k.flush_dve(nc)
                e = k.mark(DVE.tensor_scalar_mul(grs, grs, 1.0 / 64), "dve")
                k.wait(ACT, e)
                e = k.mark(ACT.activation(grs, grs, AF.Sqrt, bias=EPS_GN), "act")
                k.wait(DVE, e)
                e = k.mark(DVE.reciprocal(grs, grs), "dve")
                k.flush_dve(nc)
                if s == 0 and "d_gmu" in dbg:
                    k.wait(GP, e)
                    k.mark(GP.dma_start(out=dbg["d_gmu"][tc * 128:(tc + 1) * 128, 0:16],
                                        in_=gmu), "dG", 16)
                    k.mark(GP.dma_start(out=dbg["d_grs"][tc * 128:(tc + 1) * 128, 0:16],
                                        in_=grs), "dG", 16)
                    k.mark(GP.dma_start(out=dbg["d_sbT"][tc * 128:(tc + 1) * 128, 0:16],
                                        in_=sbT), "dG", 16)
                    k.wait(DVE, k.ev("dG"))
                for h in range(16):
                    DVE.tensor_scalar(ytm[:, h * 64:(h + 1) * 64],
                                      ytm[:, h * 64:(h + 1) * 64],
                                      grs[:, h:h + 1], None, ALU.mult)
                if s == 0 and tc == 0 and "d_yh" in dbg:
                    e_ = k.mark(DVE.engine_nop(), "dve")
                    k.wait(GP, e_)
                    k.mark(GP.dma_start(out=dbg["d_yh"][0:128, :], in_=ytm), "dG", 16)
                    k.mark(GP.dma_start(out=dbg["d_gam"][0:128, :], in_=gamb), "dG", 16)
                    k.mark(GP.dma_start(out=dbg["d_bet"][0:128, :], in_=betb), "dG", 16)
                    k.wait(DVE, k.ev("dG"))
                DVE.tensor_mul(ytm, ytm, gamb)
                DVE.tensor_add(ytm, ytm, betb)
                k.wait(DVE, e_sbt)
                for h in range(16):
                    DVE.scalar_tensor_tensor(ytm[:, h * 64:(h + 1) * 64],
                                             vtm[:, tc, h * 64:(h + 1) * 64],
                                             sbT[:, h:h + 1],
                                             ytm[:, h * 64:(h + 1) * 64],
                                             ALU.mult, ALU.add)
                e = k.mark(DVE.engine_nop(), "dve")
                if s == 0 and "d_y" in dbg:
                    k.wait(GP, e)
                    k.mark(GP.dma_start(out=dbg["d_y"][tc * 128:(tc + 1) * 128, :],
                                        in_=ytm), "dG", 16)
                    k.wait(DVE, k.ev("dG"))
                k.wait(PE, e_s1, e)
                PE.matmul(pb[4][:, 0:512], s1[:, tc * 128:(tc + 1) * 128],
                          g2s[:, 0:512], start=True, stop=True)
                PE.matmul(pb[5][:, 0:512], s1[:, tc * 128:(tc + 1) * 128],
                          g2s[:, 512:1024], start=True, stop=True)
                e = PE.ev_last()
                k.wait(DVE, e)
                DVE.tensor_mul(ytm[:, 0:512], ytm[:, 0:512], pb[4][:, 0:512])
                e = k.mark(DVE.tensor_mul(ytm[:, 512:1024], ytm[:, 512:1024],
                                          pb[5][:, 0:512]), "dve")
                if s == 0 and "d_z" in dbg:
                    k.wait(GP, e)
                    k.mark(GP.dma_start(out=dbg["d_z"][tc * 128:(tc + 1) * 128, :],
                                        in_=ytm), "dG", 16)
                    k.wait(DVE, k.ev("dG"))
                    k.wait(PE, k.ev("dG"))
                k.wait(PE, e)
                for oc in range(8):
                    PE.transpose(pb[4 + oc // 4][:, (oc % 4) * 128:(oc % 4 + 1) * 128],
                                 ytm[:, oc * 128:(oc + 1) * 128], ident)
                e = PE.ev_last()
                k.wait(DVE, e)
                for oc in range(8):
                    k.mark(DVE.tensor_copy(zfm[:, oc, tc * 128:(tc + 1) * 128],
                                           pb[4 + oc // 4][:, (oc % 4) * 128:(oc % 4 + 1) * 128]),
                           "dve")

            e_pe_prev = None
            for kc in range(8):
                if kc + 1 < 8:
                    evW[(kc + 1) % 2] = stream_W("Wo", kc + 1, (kc + 1) % 2,
                                                 pe_guard=e_pe_prev)
                k.wait(PE, evW[kc % 2], k.ev("dve"))
                W = wbuf[:, kc % 2, :]
                for oc in range(8):
                    bank = pb[oc // 2]
                    col = (oc % 2) * SB
                    PE.matmul(bank[:, col:col + SB], W[:, oc * 128:(oc + 1) * 128],
                              zfm[:, kc, :], start=(kc == 0 and oc % 2 == 0),
                              stop=(kc == 7 and oc % 2 == 1))
                e_pe_prev = PE.ev_last()
            k.wait(DVE, e_pe_prev)
            for oc in range(8):
                DVE.tensor_add(x[:, oc, :], x[:, oc, :],
                               pb[oc // 2][:, (oc % 2) * SB:(oc % 2) * SB + SB])
            e = k.mark(DVE.engine_nop(), "dve")
            k.wait(GP, e)
            ew = k.mark(GP.dma_start(out=x2v[:, :, sl], in_=x), "dG", 16)
            if "d_x2" in dbg:
                dv = dbg["d_x2"].rearrange("(f p) t -> p f t", p=128)
                k.mark(GP.dma_start(out=dv[:, :, sl], in_=x), "dG", 16)
            k.wait(sync, k.ev("dG"))

            if "d_ln1" in dbg:
                dv = dbg["d_ln1"].rearrange("(f p) t -> p f t", p=128)
                k.mark(GP.dma_start(out=dv[:, :, sl], in_=ln), "dG", 16)
            if "d_r" in dbg:
                dv = dbg["d_r"].rearrange("(f p) t -> p f t", p=128)
                k.mark(GP.dma_start(out=dv[:, :, sl], in_=r_), "dG", 16)
            if "d_kf" in dbg:
                dv = dbg["d_kf"].rearrange("(f p) t -> p f t", p=128)
                k.mark(GP.dma_start(out=dv[:, :, sl], in_=kf), "dG", 16)
            if "d_kkn" in dbg:
                dv = dbg["d_kkn"].rearrange("(f p) t -> p f t", p=128)
                k.mark(GP.dma_start(out=dv[:, :, sl], in_=kkn), "dG", 16)
            if "d_cc" in dbg:
                dv = dbg["d_cc"].rearrange("(f p) t -> p f t", p=128)
                k.mark(GP.dma_start(out=dv[:, :, sl], in_=cc), "dG", 16)
            if "d_otm" in dbg:
                dv = dbg["d_otm"]
                for tc in range(NTC):
                    gt = s * NTC + tc
                    k.mark(GP.dma_start(
                        out=dv[gt * 128:(gt + 1) * 128, 0:1024],
                        in_=otm[:, tc, :]), "dG", 16)
    finally:
        es.close()


def phase_d(k, nc, din, dbg, sbp, pb, x2scr, h1scr, ybscr, kscr, vscr, sync):
    T, SD = k.T, k.SD
    SH = SD // 2
    NSD = T // SD
    NTC = SD // 128
    PE, ACT, DVE, GP = _PERec(nc.tensor, k), nc.scalar, nc.vector, nc.gpsimd
    x2v = x2scr.rearrange("(f p) t -> p f t", p=128)
    h1v = h1scr.rearrange("(f p) t -> p f t", p=128)
    kv = kscr.rearrange("(f p) t -> p f t", p=128)
    ybv = ybscr.rearrange("(tc p) c -> p tc c", p=128)
    onescol, onesrow = sbp["s_onescol"], sbp["s_onesrow"]
    ident = sbp["sc_ident"]
    negI, u01 = sbp["sc_negI"], sbp["sc_u01"]
    fg1 = sbp["v_fg1"]

    from contextlib import ExitStack
    es = ExitStack()

    def A(name, shape):
        return es.enter_context(nc.sbuf_tensor(name, list(shape), F32)).ap()

    x2 = A("d_x2s", [128, 8, SD])
    h0 = A("d_h0", [128, 8, SD])
    t1 = A("ds_t1", [128, 8, SD])
    cdk = A("d_cdk", [128, 4, SD])
    q = A("d_q", [128, 8, SD])
    att = A("d_att", [128, 8, SD])
    kb = A("d_kb", [128, 2, 8, 128])
    vb = A("d_vb", [128, 2, 1040])
    Pb = A("d_Pb", [128, 512])
    wbuf = A("d_wbuf", [128, 2, 1024])
    rowm = A("d_rowm", [1, SD])
    rowr = A("d_rowr", [1, SD])
    zrow = A("d_zrow", [65, SD])
    zbc = A("d_zbc", [64, SD])
    ones65 = A("d_ones65", [65, 64])
    ytm = A("d_ytm", [128, 1024])

    try:
        e = k.mark(DVE.memset(ones65, 1.0), "dve")
        for s in range(NSD):
            t0 = s * SD
            sl = slice(t0, t0 + SD)
            k.wait(ACT, k.ev("pe"), k.ev("dve"))
            k.wait(DVE, k.ev("pe"), k.ev("act"))
            k.wait(PE, k.ev("act"), k.ev("dve"))
            k.wait(sync, k.ev("dG"))
            e_x = k.mark(sync.dma_start(out=x2, in_=x2v[:, :, sl]), "dS", 16)
            k.wait(ACT, e_x)
            e_sq = k.mark(ACT.activation(h0, x2, AF.Square), "act")
            k.wait(PE, e_x, e_sq)
            for f in range(8):
                PE.matmul(pb[6][0:1, 0:SD], onescol, x2[:, f, :], start=(f == 0),
                          stop=(f == 7))
            for f in range(8):
                PE.matmul(pb[7][0:1, 0:SD], onescol, h0[:, f, :], start=(f == 0),
                          stop=(f == 7))
            e = PE.ev_last()
            k.wait(DVE, e)
            DVE.tensor_scalar_mul(rowm, pb[6][0:1, 0:SD], 1.0 / C)
            DVE.tensor_scalar_mul(rowr, pb[7][0:1, 0:SD], 1.0 / C)
            DVE.tensor_mul(pb[6][0:1, 0:SD], rowm, rowm)
            DVE.tensor_sub(rowr, rowr, pb[6][0:1, 0:SD])
            e = k.mark(DVE.tensor_scalar_max(rowr, rowr, 0.0), "dve")
            k.wait(ACT, e)
            e = k.mark(ACT.activation(rowr, rowr, AF.Sqrt, bias=EPS_LN), "act")
            k.wait(DVE, e)
            e = k.mark(DVE.reciprocal(rowr, rowr), "dve")
            k.wait(PE, e)
            PE.matmul(pb[6][:, 0:SD], onesrow, rowm, start=True, stop=True)
            PE.matmul(pb[7][:, 0:SD], onesrow, rowr, start=True, stop=True)
            e = PE.ev_last()
            k.wait(DVE, e)
            for f in range(8):
                DVE.tensor_sub(h0[:, f, :], x2[:, f, :], pb[6][:, 0:SD])
            for f in range(8):
                DVE.tensor_mul(h0[:, f, :], h0[:, f, :], pb[7][:, 0:SD])
            e_h0 = k.mark(DVE.engine_nop(), "dve")
            k.wait(ACT, e_h0)
            e_sq = k.mark(ACT.activation(t1, h0, AF.Square), "act")
            k.wait(PE, e_sq)
            for f in range(8):
                PE.matmul(pb[6][0:1, 0:SD], onescol, t1[:, f, :], start=(f == 0),
                          stop=(f == 7))
            e = PE.ev_last()
            k.wait(DVE, e)
            e = k.mark(DVE.tensor_scalar(rowr, pb[6][0:1, 0:SD], 1.0 / C, EPS_RMS,
                                         ALU.mult, ALU.add), "dve")
            k.wait(ACT, e)
            e = k.mark(ACT.activation(rowr, rowr, AF.Sqrt), "act")
            k.wait(DVE, e)
            e = k.mark(DVE.reciprocal(rowr, rowr), "dve")
            k.wait(PE, e)
            PE.matmul(pb[6][:, 0:SD], onesrow, rowr, start=True, stop=True)
            e = PE.ev_last()
            k.wait(DVE, e)
            for f in range(8):
                DVE.scalar_tensor_tensor(t1[:, f, :], h0[:, f, :], fg1[:, f:f + 1],
                                         pb[6][:, 0:SD], ALU.mult, ALU.mult)
            e_t1 = k.mark(DVE.engine_nop(), "dve")

            def stream_W(name, kc, slot):
                ncols = din[name].shape[1]
                k.wait(sync, k.ev("pe"))
                return k.mark(sync.dma_start(out=wbuf[:, slot, 0:ncols],
                                             in_=din[name][kc * 128:(kc + 1) * 128, :]),
                              "dS", 16)

            evw = [stream_W("Wdkv", 0, 0), None]
            e_pe_prev = None
            for kc in range(8):
                if kc + 1 < 8:
                    evw[(kc + 1) % 2] = stream_W("Wdkv", kc + 1, (kc + 1) % 2)
                k.wait(PE, evw[kc % 2], e_t1)
                W = wbuf[:, kc % 2, 0:512]
                for oc in range(4):
                    PE.matmul(pb[oc][:, 0:SD], W[:, oc * 128:(oc + 1) * 128],
                              t1[:, kc, :], start=(kc == 0), stop=(kc == 7))
                e_pe_prev = PE.ev_last()
            k.wait(DVE, e_pe_prev)
            for oc in range(4):
                k.mark(DVE.tensor_copy(cdk[:, oc, :], pb[oc][:, 0:SD]), "dve")
            evw = [stream_W("Wq", 0, 0), None]
            for kc in range(8):
                if kc + 1 < 8:
                    evw[(kc + 1) % 2] = stream_W("Wq", kc + 1, (kc + 1) % 2)
                k.wait(PE, evw[kc % 2])
                W = wbuf[:, kc % 2, :]
                for oc in range(8):
                    bank = pb[4 + oc // 2]
                    col = (oc % 2) * SH
                    PE.matmul(bank[:, col:col + SH],
                              W[:, oc * 128:(oc + 1) * 128],
                              t1[:, kc, 0:SH], start=(kc == 0 and oc % 2 == 0),
                              stop=(kc == 7 and oc % 2 == 1))
                for oc in range(8):
                    bank = pb[oc // 2]
                    col = (oc % 2) * SH
                    PE.matmul(bank[:, col:col + SH],
                              W[:, oc * 128:(oc + 1) * 128],
                              t1[:, kc, SH:2 * SH], start=(kc == 0 and oc % 2 == 0),
                              stop=(kc == 7 and oc % 2 == 1))
                e_pe_prev = PE.ev_last()
            k.wait(ACT, e_pe_prev)
            for oc in range(8):
                ACT.activation(q[:, oc, 0:SH],
                               pb[4 + oc // 2][:, (oc % 2) * SH:(oc % 2) * SH + SH],
                               AF.Identity, scale=0.125)
            for oc in range(8):
                ACT.activation(q[:, oc, SH:2 * SH],
                               pb[oc // 2][:, (oc % 2) * SH:(oc % 2) * SH + SH],
                               AF.Identity, scale=0.125)
            e_q = k.mark(ACT.nop(), "act")

            if s == 0 and "d_t1" in dbg:
                k.wait(GP, e_t1, e_q)
                dv = dbg["d_t1"].rearrange("(f p) t -> p f t", p=128)
                k.mark(GP.dma_start(out=dv[:, :, sl], in_=t1), "dG", 16)
                dv = dbg["d_qd"].rearrange("(f p) t -> p f t", p=128)
                k.mark(GP.dma_start(out=dv[:, :, sl], in_=q), "dG", 16)
                k.wait(DVE, k.ev("dG"))
                k.wait(PE, k.ev("dG"))
            evw = [stream_W("Wuk", 0, 0), None]
            for kc in range(4):
                if kc + 1 < 4:
                    evw[(kc + 1) % 2] = stream_W("Wuk", kc + 1, (kc + 1) % 2)
                k.wait(PE, evw[kc % 2], k.ev("dve"))
                W = wbuf[:, kc % 2, :]
                for oc in range(8):
                    bank = pb[4 + oc // 2]
                    col = (oc % 2) * SH
                    PE.matmul(bank[:, col:col + SH], W[:, oc * 128:(oc + 1) * 128],
                              cdk[:, kc, 0:SH], start=(kc == 0 and oc % 2 == 0),
                              stop=(kc == 3 and oc % 2 == 1))
                for oc in range(8):
                    bank = pb[oc // 2]
                    col = (oc % 2) * SH
                    PE.matmul(bank[:, col:col + SH], W[:, oc * 128:(oc + 1) * 128],
                              cdk[:, kc, SH:2 * SH], start=(kc == 0 and oc % 2 == 0),
                              stop=(kc == 3 and oc % 2 == 1))
                e_pe_prev = PE.ev_last()
            k.wait(DVE, e_pe_prev)
            for oc in range(8):
                DVE.tensor_copy(att[:, oc, 0:SH],
                                pb[4 + oc // 2][:, (oc % 2) * SH:(oc % 2) * SH + SH])
                DVE.tensor_copy(att[:, oc, SH:2 * SH],
                                pb[oc // 2][:, (oc % 2) * SH:(oc % 2) * SH + SH])
            e_kev = k.mark(DVE.engine_nop(), "dve")
            k.wait(GP, e_kev)
            e_kw = k.mark(GP.dma_start(out=kv[:, :, sl], in_=att), "dG", 16)

            evw = [stream_W("Wuv", 0, 0), None]
            for kc in range(4):
                if kc + 1 < 4:
                    evw[(kc + 1) % 2] = stream_W("Wuv", kc + 1, (kc + 1) % 2)
                k.wait(PE, evw[kc % 2])
                W = wbuf[:, kc % 2, :]
                for tc in range(NTC):
                    for hf in range(2):
                        PE.matmul(pb[2 * tc + hf][:, 0:512],
                                  cdk[:, kc, tc * 128:(tc + 1) * 128],
                                  W[:, hf * 512:(hf + 1) * 512],
                                  start=(kc == 0), stop=(kc == 3))
                e_pe_prev = PE.ev_last()
            k.wait(DVE, e_pe_prev)
            for tc in range(NTC):
                vbt = vb[:, 0, :] if tc % 2 == 0 else vb[:, 1, :]
                vb3 = vbt.rearrange("p (h n) -> p h n", n=65)
                DVE.memset(vb3[:, :, 64:65], 1.0)
                DVE.tensor_copy(vb3[:, 0:8, 0:64],
                                pb[2 * tc][:, 0:512].rearrange("p (h n) -> p h n", n=64))
                e = k.mark(DVE.tensor_copy(vb3[:, 8:16, 0:64],
                                           pb[2 * tc + 1][:, 0:512].rearrange("p (h n) -> p h n", n=64)),
                           "dve")
                k.wait(GP, e)
                e_vw = k.mark(GP.dma_start(out=vscr[s * NTC + tc, :, :], in_=vbt), "dG", 16)
                k.wait(DVE, k.ev("dG"))

            nkt = (s + 1) * NTC
            k.wait(sync, k.ev("dG"))
            for h in range(16):
                f = h // 2
                nh = (h % 2) * 64
                for kt in range(nkt):
                    slot = kt % 2
                    k.wait(sync, k.ev("pe"))
                    e_kt = k.mark(sync.dma_start(out=kb[:, slot, :, :],
                                                 in_=kv[:, :, kt * 128:(kt + 1) * 128]), "dS", 16)
                    e_vt = k.mark(sync.dma_start(out=vb[:, slot, :],
                                                 in_=vscr[kt, :, :]), "dS", 16)
                    qoff = max(0, kt * 128 - t0)
                    diag = kt * 128 >= t0
                    k.wait(PE, e_kt, e_q, k.ev("act"))
                    PE.matmul(pb[4][:, qoff:SD], kb[nh:nh + 64, slot, f, :],
                              q[nh:nh + 64, f, qoff:SD], start=True, stop=not diag,
                              tile_position=(nh, 0))
                    if diag:
                        PE.matmul(pb[4][:, qoff:qoff + 128], negI, u01,
                                  start=False, stop=True, tile_position=(0, 0))
                    e_sc = PE.ev_last()
                    k.wait(ACT, e_sc)
                    e_p = k.mark(ACT.activation(Pb[:, qoff:SD], pb[4][:, qoff:SD],
                                                AF.Exp, bias=-4.0), "act")
                    k.wait(PE, e_p, e_vt)
                    PE.matmul(pb[5][0:65, qoff:SD],
                              vb[:, slot, 65 * h:65 * h + 65],
                              Pb[:, qoff:SD], start=(kt == 0), stop=(kt == nkt - 1),
                              tile_position=(0, 0))
                    e_pv = PE.ev_last()
                k.wait(DVE, e_pv)
                e_z = k.mark(DVE.reciprocal(zrow[64:65, 0:SD], pb[5][64:65, 0:SD]), "dve")
                k.wait(PE, e_z)
                PE.matmul(pb[6][0:64, 0:SD], ones65[64:65, 0:64],
                          zrow[64:65, 0:SD], start=True, stop=True,
                          tile_position=(64, 0))
                e_zb = PE.ev_last()
                k.wait(DVE, e_zb)
                DVE.tensor_copy(zbc, pb[6][0:64, 0:SD])
                e_att = k.mark(DVE.tensor_tensor(att[nh:nh + 64, f, :],
                                                 pb[5][0:64, 0:SD], zbc,
                                                 ALU.mult), "dve")
                k.wait(PE, e_att)

            if s == 0 and "d_attn" in dbg:
                k.wait(GP, k.ev("dve"))
                dv = dbg["d_attn"].rearrange("(f p) t -> p f t", p=128)
                k.mark(GP.dma_start(out=dv[:, :, sl], in_=att), "dG", 16)
                k.wait(DVE, k.ev("dG"))
                k.wait(PE, k.ev("dG"))
            evw = [stream_W("Wof", 0, 0), None]
            for kc in range(8):
                if kc + 1 < 8:
                    evw[(kc + 1) % 2] = stream_W("Wof", kc + 1, (kc + 1) % 2)
                k.wait(PE, evw[kc % 2], k.ev("dve"))
                W = wbuf[:, kc % 2, :]
                for oc in range(8):
                    bank = pb[4 + oc // 2]
                    col = (oc % 2) * SH
                    PE.matmul(bank[:, col:col + SH], W[:, oc * 128:(oc + 1) * 128],
                              att[:, kc, 0:SH], start=(kc == 0 and oc % 2 == 0),
                              stop=(kc == 7 and oc % 2 == 1))
                for oc in range(8):
                    bank = pb[oc // 2]
                    col = (oc % 2) * SH
                    PE.matmul(bank[:, col:col + SH], W[:, oc * 128:(oc + 1) * 128],
                              att[:, kc, SH:2 * SH], start=(kc == 0 and oc % 2 == 0),
                              stop=(kc == 7 and oc % 2 == 1))
                e_pe_prev = PE.ev_last()
            k.wait(DVE, e_pe_prev)
            for oc in range(8):
                DVE.tensor_add(h0[:, oc, 0:SH], h0[:, oc, 0:SH],
                               pb[4 + oc // 2][:, (oc % 2) * SH:(oc % 2) * SH + SH])
                DVE.tensor_add(h0[:, oc, SH:2 * SH], h0[:, oc, SH:2 * SH],
                               pb[oc // 2][:, (oc % 2) * SH:(oc % 2) * SH + SH])
            e_h1 = k.mark(DVE.tensor_add(x2, x2, h0), "dve")
            k.wait(GP, e_h1)
            e_h1w = k.mark(GP.dma_start(out=h1v[:, :, sl], in_=h0), "dG", 16)
            if "d_h1" in dbg:
                dv = dbg["d_h1"].rearrange("(f p) t -> p f t", p=128)
                k.mark(GP.dma_start(out=dv[:, :, sl], in_=h0), "dG", 16)
            for tc in range(NTC):
                k.wait(PE, e_h1, k.ev("dve"))
                for oc in range(8):
                    PE.transpose(pb[4 + oc // 4][:, (oc % 4) * 128:(oc % 4 + 1) * 128],
                                 x2[:, oc, tc * 128:(tc + 1) * 128], ident)
                e = PE.ev_last()
                k.wait(DVE, e)
                for oc in range(8):
                    DVE.tensor_copy(ytm[:, oc * 128:(oc + 1) * 128],
                                    pb[4 + oc // 4][:, (oc % 4) * 128:(oc % 4 + 1) * 128])
                e = k.mark(DVE.engine_nop(), "dve")
                k.wait(GP, e)
                k.mark(GP.dma_start(out=ybv[:, s * NTC + tc, :], in_=ytm), "dG", 16)
                if "d_yb" in dbg:
                    k.mark(GP.dma_start(
                        out=dbg["d_yb"][(s * NTC + tc) * 128:(s * NTC + tc + 1) * 128, :],
                        in_=ytm), "dG", 16)
                k.wait(DVE, k.ev("dG"))
                k.wait(PE, k.ev("dG"))
    finally:
        es.close()


def phase_e(k, nc, din, dbg, sbp, pb, h1scr, ybscr, out, sync):
    T, SE = k.T, k.SE
    NSE = T // SE
    NTC = SE // 128
    PE, ACT, DVE, GP = _PERec(nc.tensor, k), nc.scalar, nc.vector, nc.gpsimd
    h1v = h1scr.rearrange("(f p) t -> p f t", p=128)
    ybv = ybscr.rearrange("(tc p) c -> p tc c", p=128)
    outv = out.rearrange("(tc p) c -> p tc c", p=128)
    onescol, onesrow = sbp["s_onescol"], sbp["s_onesrow"]
    fg2 = sbp["v_fg2"]
    gws = sbp["s_gw"]
    iotar = sbp["sc_iotar"]

    from contextlib import ExitStack
    es = ExitStack()

    def A(name, shape):
        return es.enter_context(nc.sbuf_tensor(name, list(shape), F32)).ap()

    h1 = A("e_h1", [128, 8, SE])
    t2 = A("e_t2", [128, 8, SE])
    yba = A("e_yba", [128, NTC, 1024])
    hid = A("e_hid", [128, 2, SE])
    w1b = A("e_w1b", [128, 2, 8, 256])
    w2b = A("e_w2b", [128, 2, 2, 1024])
    rowm = A("e_rowm", [1, SE])
    v1c = A("e_v1c", [128, NTC])
    v2c = A("e_v2c", [128, NTC])
    i1c = A("e_i1c", [128, NTC])
    i2c = A("e_i2c", [128, NTC])
    g1c = A("e_g1c", [128, NTC])
    g2c = A("e_g2c", [128, NTC])
    eqs = A("e_eqs", [128, NTC * 8])
    wcols = A("e_wcols", [128, NTC, 8])

    try:
        for s in range(NSE):
            t0 = s * SE
            sl = slice(t0, t0 + SE)
            k.wait(ACT, k.ev("pe"), k.ev("dve"))
            k.wait(DVE, k.ev("pe"), k.ev("act"))
            k.wait(PE, k.ev("act"), k.ev("dve"))
            k.wait(sync, k.ev("dG"))
            e_h = k.mark(sync.dma_start(out=h1, in_=h1v[:, :, sl]), "dS", 16)
            e_yb = k.mark(sync.dma_start(out=yba,
                                         in_=ybv[:, s * NTC:(s + 1) * NTC, :]), "dS", 16)
            k.wait(ACT, e_h)
            e_sq = k.mark(ACT.activation(t2, h1, AF.Square), "act")
            k.wait(PE, e_sq, e_h)
            for f in range(8):
                PE.matmul(pb[6][0:1, 0:SE], onescol, t2[:, f, :], start=(f == 0),
                          stop=(f == 7))
            e = PE.ev_last()
            k.wait(DVE, e)
            e = k.mark(DVE.tensor_scalar(rowm, pb[6][0:1, 0:SE], 1.0 / C, EPS_RMS,
                                         ALU.mult, ALU.add), "dve")
            k.wait(ACT, e)
            e = k.mark(ACT.activation(rowm, rowm, AF.Sqrt), "act")
            k.wait(DVE, e)
            e = k.mark(DVE.reciprocal(rowm, rowm), "dve")
            k.wait(PE, e)
            PE.matmul(pb[6][:, 0:SE], onesrow, rowm, start=True, stop=True)
            e = PE.ev_last()
            k.wait(DVE, e)
            for f in range(8):
                DVE.scalar_tensor_tensor(t2[:, f, :], h1[:, f, :], fg2[:, f:f + 1],
                                         pb[6][:, 0:SE], ALU.mult, ALU.mult)
            e_t2 = k.mark(DVE.engine_nop(), "dve")

            k.wait(PE, e_t2)
            for tc in range(NTC):
                for kc in range(8):
                    PE.matmul(pb[7][:, tc * 8:(tc + 1) * 8],
                              t2[:, kc, tc * 128:(tc + 1) * 128],
                              gws[:, kc * 8:(kc + 1) * 8],
                              start=(kc == 0), stop=(kc == 7))
            e = PE.ev_last()
            k.wait(DVE, e)
            Ls = [pb[7][:, tc * 8:(tc + 1) * 8] for tc in range(NTC)]
            eqt = [eqs[:, tc * 8:(tc + 1) * 8] for tc in range(NTC)]
            for tc in range(NTC):
                DVE.tensor_reduce(v1c[:, tc:tc + 1], Ls[tc], AX.X, ALU.max)
            k.flush_dve(nc)
            for tc in range(NTC):
                DVE.tensor_scalar(eqt[tc], Ls[tc], v1c[:, tc:tc + 1], None, ALU.is_equal)
            k.flush_dve(nc)
            for tc in range(NTC):
                DVE.tensor_mul(eqt[tc], eqt[tc], iotar)
            k.flush_dve(nc)
            for tc in range(NTC):
                DVE.tensor_reduce(i1c[:, tc:tc + 1], eqt[tc], AX.X, ALU.max)
            k.flush_dve(nc)
            for tc in range(NTC):
                DVE.tensor_scalar(eqt[tc], iotar, i1c[:, tc:tc + 1], None, ALU.is_equal)
            k.flush_dve(nc)
            for tc in range(NTC):
                DVE.scalar_tensor_tensor(eqt[tc], eqt[tc], -1e30, Ls[tc],
                                         ALU.mult, ALU.add)
            k.flush_dve(nc)
            for tc in range(NTC):
                DVE.tensor_reduce(v2c[:, tc:tc + 1], eqt[tc], AX.X, ALU.max)
            k.flush_dve(nc)
            for tc in range(NTC):
                DVE.tensor_scalar(eqt[tc], eqt[tc], v2c[:, tc:tc + 1], None, ALU.is_equal)
            k.flush_dve(nc)
            for tc in range(NTC):
                DVE.tensor_mul(eqt[tc], eqt[tc], iotar)
            k.flush_dve(nc)
            for tc in range(NTC):
                DVE.tensor_reduce(i2c[:, tc:tc + 1], eqt[tc], AX.X, ALU.max)
            k.flush_dve(nc)
            e = k.mark(DVE.tensor_sub(g2c, v2c, v1c), "dve")
            k.wait(ACT, e)
            ACT.activation(g1c, g2c, AF.Sigmoid, scale=-1.0)
            e_gt = k.mark(ACT.activation(g2c, g2c, AF.Sigmoid), "act")
            k.wait(DVE, e_gt)
            wq = wcols.rearrange("p a b -> p (a b)")
            for tc in range(NTC):
                for e_i in range(N_EXP):
                    DVE.tensor_scalar(wq[:, tc * 8 + e_i:tc * 8 + e_i + 1],
                                      i1c[:, tc:tc + 1], float(999 - e_i), None,
                                      ALU.is_equal)
                    DVE.tensor_scalar(eqt[tc][:, e_i:e_i + 1],
                                      i2c[:, tc:tc + 1], float(999 - e_i), None,
                                      ALU.is_equal)
            k.flush_dve(nc)
            for tc in range(NTC):
                DVE.tensor_scalar(wq[:, tc * 8:(tc + 1) * 8],
                                  wq[:, tc * 8:(tc + 1) * 8], g1c[:, tc:tc + 1],
                                  None, ALU.mult)
                DVE.tensor_scalar(eqt[tc], eqt[tc], g2c[:, tc:tc + 1], None, ALU.mult)
            k.flush_dve(nc)
            for tc in range(NTC):
                DVE.tensor_add(wq[:, tc * 8:(tc + 1) * 8],
                               wq[:, tc * 8:(tc + 1) * 8], eqt[tc])
            k.flush_dve(nc)
            e_wc = k.mark(DVE.engine_nop(), "dve")

            if s == 0 and "d_yba0" in dbg:
                k.wait(GP, e_wc, e_yb)
                k.mark(GP.dma_start(out=dbg["d_yba0"][0:128, :], in_=yba[:, 0, :]), "dG", 16)
                k.mark(GP.dma_start(out=dbg["d_wc"][0:128, 0:N_EXP * NTC],
                                    in_=wcols.rearrange("p a b -> p (a b)")), "dG", 16)
                k.mark(GP.dma_start(out=dbg["d_t2"][0:128, :],
                                    in_=t2.rearrange("p a b -> p (a b)")[:, 0:1024]), "dG", 16)
                k.mark(GP.dma_start(out=dbg["d_tv"][0:128, 0:NTC],
                                    in_=v1c), "dG", 16)
                k.mark(GP.dma_start(out=dbg["d_tv"][0:128, NTC:2 * NTC],
                                    in_=v2c), "dG", 16)
                k.mark(GP.dma_start(out=dbg["d_tv"][0:128, 2 * NTC:3 * NTC],
                                    in_=i1c), "dG", 16)
                k.mark(GP.dma_start(out=dbg["d_tv"][0:128, 3 * NTC:4 * NTC],
                                    in_=i2c), "dG", 16)
                k.mark(GP.dma_start(out=dbg["d_tv"][0:128, 4 * NTC:5 * NTC],
                                    in_=g1c), "dG", 16)
                k.mark(GP.dma_start(out=dbg["d_tv"][0:128, 5 * NTC:6 * NTC],
                                    in_=g2c), "dG", 16)
                k.wait(DVE, k.ev("dG"))
            we1 = [din["we1"][e_i].rearrange("(kc p) hh -> p kc hh", p=128)
                   for e_i in range(N_EXP)]
            we2 = [din["we2"][e_i].rearrange("(kc p) cc -> p kc cc", p=128)
                   for e_i in range(N_EXP)]
            for e_i in range(N_EXP):
                first = (e_i == 0)
                for hq in range(8):
                    slot = hq % 2
                    k.wait(sync, k.ev("pe"))
                    e_w1 = k.mark(sync.dma_start(
                        out=w1b[:, slot, :, :],
                        in_=we1[e_i][:, :, hq * 256:(hq + 1) * 256]), "dS", 16)
                    e_w2 = k.mark(sync.dma_start(
                        out=w2b[:, slot, :, :],
                        in_=we2[e_i][:, hq * 2:hq * 2 + 2, :]), "dS", 16)
                    k.wait(PE, e_w1, e_wc)
                    for oc2 in range(2):
                        for kc in range(8):
                            PE.matmul(pb[6][:, oc2 * SE:(oc2 + 1) * SE][:, 0:SE],
                                      w1b[:, slot, kc, oc2 * 128:(oc2 + 1) * 128],
                                      t2[:, kc, :],
                                      start=(kc == 0 and oc2 == 0),
                                      stop=(kc == 7 and oc2 == 1))
                    e = PE.ev_last()
                    k.wait(ACT, e)
                    e_hid = k.mark(ACT.activation(hid, pb[6][:, 0:2 * SE], AF.Silu), "act")
                    k.wait(PE, e_hid, e_w2)
                    for tc in range(NTC):
                        for kc2 in range(2):
                            for half in range(2):
                                PE.matmul(pb[2 * tc + half][:, 0:512],
                                          hid[:, kc2, tc * 128:(tc + 1) * 128],
                                          w2b[:, slot, kc2, half * 512:(half + 1) * 512],
                                          start=(hq == 0 and kc2 == 0),
                                          stop=(hq == 7 and kc2 == 1))
                    e_ye = PE.ev_last()
                k.wait(DVE, e_ye)
                for tc in range(NTC):
                    DVE.scalar_tensor_tensor(yba[:, tc, 0:512], pb[2 * tc][:, 0:512],
                                             wcols[:, tc, e_i:e_i + 1],
                                             yba[:, tc, 0:512], ALU.mult, ALU.add)
                    DVE.scalar_tensor_tensor(yba[:, tc, 512:1024], pb[2 * tc + 1][:, 0:512],
                                             wcols[:, tc, e_i:e_i + 1],
                                             yba[:, tc, 512:1024], ALU.mult, ALU.add)
                e_cmb = k.mark(DVE.engine_nop(), "dve")
                k.wait(PE, e_cmb)
                if s == 0 and e_i == 0 and "d_yba1" in dbg:
                    k.wait(GP, e_cmb)
                    k.mark(GP.dma_start(out=dbg["d_yba1"][0:128, :], in_=yba[:, 0, :]), "dG", 16)
                    k.wait(DVE, k.ev("dG"))
            k.wait(GP, e_cmb, e_yb)
            k.mark(GP.dma_start(out=outv[:, s * NTC:(s + 1) * NTC, :],
                                in_=yba), "dG", 16)
            k.wait(DVE, k.ev("dG"))
    finally:
        es.close()


_BUILT = {}


def _get_program():
    key = "full"
    if key not in _BUILT:
        _BUILT[key] = build_program()
    return _BUILT[key]


def make_in_map(inputs, core, T=2048):
    g = lambda n: np.ascontiguousarray(np.asarray(inputs[n], dtype=np.float32))
    m = dict()
    m["xT"] = np.ascontiguousarray(g("L0_attn_x")[core].T)
    for nm, arr in host_consts().items():
        m[nm] = arr
    m["mix_r"] = g("L0_attn_x_r").ravel()
    m["mix_w"] = g("L0_attn_x_w").ravel()
    m["mix_k"] = g("L0_attn_x_k").ravel()
    m["mix_v"] = g("L0_attn_x_v").ravel()
    m["mix_a"] = g("L0_attn_x_a").ravel()
    m["mix_g"] = g("L0_attn_x_g").ravel()
    m["w0"] = g("L0_attn_w0").ravel()
    m["a0"] = g("L0_attn_a0").ravel()
    m["k_k"] = g("L0_attn_k_k").ravel()
    m["k_a"] = g("L0_attn_k_a").ravel()
    m["r_k"] = g("L0_attn_r_k").ravel()
    m["ln_g"] = g("L0_attn_ln_gamma").ravel()
    m["ln_b"] = g("L0_attn_ln_beta").ravel()
    m["w1"] = g("L0_attn_w1"); m["w2"] = g("L0_attn_w2")
    m["a1"] = g("L0_attn_a1"); m["a2"] = g("L0_attn_a2")
    m["g1"] = g("L0_attn_g1"); m["g2"] = g("L0_attn_g2")
    m["Wr"] = g("L0_attn_Wr"); m["Wk"] = g("L0_attn_Wk")
    m["Wv"] = g("L0_attn_Wv"); m["Wo"] = g("L0_attn_Wo")
    m["fg1"] = g("L0_ffn_g1"); m["fg2"] = g("L0_ffn_g2")
    m["Wq"] = g("L0_ffn_Wq"); m["Wdkv"] = g("L0_ffn_Wdkv")
    m["Wuk"] = g("L0_ffn_Wuk"); m["Wuv"] = g("L0_ffn_Wuv")
    m["Wof"] = g("L0_ffn_Wo"); m["gate_w"] = g("L0_ffn_gate_w")
    m["we1"] = g("L0_ffn_w1"); m["we2"] = g("L0_ffn_w2")
    return m


def kernel(**inputs):
    nc, _ = _get_program()
    in_maps = [make_in_map(inputs, c) for c in range(8)]
    res = run_bass_kernel_spmd(nc, in_maps, core_ids=list(range(8)))
    return np.stack([res.results[c]["out"] for c in range(8)], axis=0)


# version 06
# speedup vs baseline: 1.0525x; 1.0525x over previous
import numpy as np
import concourse.bass as bass
import concourse.mybir as mybir
from concourse.bass_utils import run_bass_kernel_spmd

F32 = mybir.dt.float32
AF = mybir.ActivationFunctionType
ALU = mybir.AluOpType
AX = mybir.AxisListType

C = 1024
H = 16
N = 64
LCH = 64
EPS_GN = 0.00064
EPS_LN = 1e-5
EPS_RMS = 1e-6
LATENT = 512
N_EXP = 8
EXP_DIM = 2048
SCL_W = -float(np.exp(-0.5))
CONST_VALS = [EPS_LN, EPS_GN, EPS_RMS, -4.0, 1e-12]


def host_consts():
    c = {}
    c["c_ident"] = np.eye(128, dtype=np.float32)
    c["c_i64b"] = np.tile(np.eye(64, dtype=np.float32), (2, 2))
    c["c_negI"] = (-1e9 * np.eye(128)).astype(np.float32)
    c["c_u01"] = np.tril(np.ones((128, 128), np.float32), -1)
    i_idx = (np.arange(128) % 64)[:, None]
    s_idx = np.tile(np.arange(64), 16)[None, :]
    c["c_mask_s"] = (s_idx < i_idx).astype(np.float32)
    c["c_maskT_s"] = (s_idx > i_idx).astype(np.float32)
    c["c_maskT_i"] = (s_idx >= i_idx).astype(np.float32)
    ind = np.zeros((128, 128), np.float32)
    for f in range(8):
        for p in range(128):
            ind[p, f * 16 + 2 * f + (1 if p >= 64 else 0)] = 1.0
    c["c_indgn"] = ind
    sel = np.zeros((16, 8 * 128), np.float32)
    for f in range(8):
        for m in range(128):
            sel[2 * f + (1 if m >= 64 else 0), f * 128 + m] = 1.0
    c["c_sel"] = sel
    c["c_iota8"] = np.tile(np.arange(8, dtype=np.float32)[None, :], (128, 1))
    c["c_iotar"] = 999.0 - c["c_iota8"]
    c["c_bias"] = np.tile(np.array(CONST_VALS, np.float32)[None, :], (128, 1))
    return c


_HC = host_consts()
CONST_NAMES = list(_HC.keys())

ATTN_VECS = ("mix_r", "mix_w", "mix_k", "mix_v", "mix_a", "mix_g", "w0", "a0",
             "k_k", "k_a", "r_k", "ln_g", "ln_b", "fg1", "fg2")


class K:
    def __init__(self, T=2048, SB=128, SD=512, SE=256):
        self.T, self.SB, self.SD, self.SE = T, SB, SD, SE
        assert T % SB == 0 and SB % LCH == 0 and T % SD == 0 and T % SE == 0
        self.nc = bass.Bass("TRN2", target_bir_lowering=False,
                            detect_race_conditions=False)
        self.sems = {}
        self.cnt = {}
        self.waited = {}

    def mark(self, inst, name, kk=1):
        inst.then_inc(self.sems[name], kk)
        self.cnt[name] = self.cnt.get(name, 0) + kk
        return (name, self.cnt[name])

    def ev(self, name):
        return (name, self.cnt.get(name, 0))

    def flush_dve(self, nc):
        e = self.mark(nc.vector.engine_nop(), "dve")
        self.wait(nc.scalar, e)
        e2 = self.mark(nc.scalar.nop(), "act")
        self.wait(nc.vector, e2)

    def wait(self, eng, *events):
        for event in events:
            if event is None:
                continue
            name, val = event
            if val <= 0:
                continue
            key = (id(eng), name)
            if self.waited.get(key, 0) >= val:
                continue
            eng.wait_ge(self.sems[name], val)
            self.waited[key] = val



class _PERec:

    def __init__(self, eng, k):
        self._e = eng
        self._k = k
        self.last = None
        self._last_ev = None
        self._last_ev_inst = None
        self._lastwin = None

    def ev_last(self):
        if self.last is None:
            return None
        if self._last_ev_inst is not self.last:
            self._last_ev = self._k.mark(self.last, "pe")
            self._last_ev_inst = self.last
        return self._last_ev

    def _guard(self, kbase, ksz):
        kwin = (kbase, ksz)
        if (self._lastwin is not None and ksz < 128 and self._lastwin[1] < 128
                and kwin != self._lastwin and self.last is not None):
            self._k.wait(self._e, self.ev_last())
        self._lastwin = kwin

    def matmul(self, out, lhsT, rhs, **kw):
        tp = kw.get("tile_position")
        kbase = tp[0] if tp is not None else 0
        self._guard(kbase, lhsT.shape[0])
        self.last = self._e.matmul(out, lhsT, rhs, **kw)
        return self.last

    def transpose(self, out, in_, ident, **kw):
        tp = kw.get("tile_position")
        kbase = tp[0] if tp is not None else 0
        self._guard(kbase, in_.shape[0])
        self.last = self._e.transpose(out, in_, ident, **kw)
        return self.last

    def __getattr__(self, nm):
        return getattr(self._e, nm)


def build_program(T=2048, SB=128, SD=512, SE=512, stages=("b", "d", "e"),
                  dbg_names=()):
    k = K(T=T, SB=SB, SD=SD, SE=SE)
    nc = k.nc
    ctx = nc.ctx

    din = {}

    def dram_in(name, shape):
        din[name] = nc.dram_tensor(name, list(shape), F32, kind="ExternalInput").ap()

    dram_in("xT", (C, T))
    for nm in CONST_NAMES:
        dram_in(nm, _HC[nm].shape)
    for nm, sh in [("w1", (C, 64)), ("w2", (64, C)), ("a1", (C, 64)), ("a2", (64, C)),
                   ("g1", (C, 128)), ("g2", (128, C)),
                   ("Wr", (C, C)), ("Wk", (C, C)), ("Wv", (C, C)), ("Wo", (C, C)),
                   ("Wq", (C, C)), ("Wdkv", (C, LATENT)), ("Wuk", (LATENT, C)),
                   ("Wuv", (LATENT, C)), ("Wof", (C, C)), ("gate_w", (C, N_EXP)),
                   ("we1", (N_EXP, C, EXP_DIM)), ("we2", (N_EXP, EXP_DIM, C))]:
        dram_in(nm, sh)
    for nm in ATTN_VECS:
        dram_in(nm, (C,))

    out = nc.dram_tensor("out", [T, C], F32, kind="ExternalOutput").ap()
    dbg = {}
    for nm, sh in dbg_names:
        dbg[nm] = nc.dram_tensor(nm, list(sh), F32, kind="ExternalOutput").ap()

    def scr(name, shape):
        return nc.dram_tensor(name, list(shape), F32, kind="Internal").ap()

    x2scr = scr("x2scr", (C, T))
    h1scr = scr("h1scr", (C, T))
    ybscr = scr("ybscr", (T, C))
    kscr = scr("kscr", (C, T))
    vscr = scr("vscr", (T // 128, 128, H * 65))

    for s in ("dS", "dG", "pe", "act", "dve", "gp"):
        k.sems[s] = ctx.enter_context(nc.semaphore(s))

    sbp = {}

    def S(name, shape):
        sbp[name] = ctx.enter_context(nc.sbuf_tensor(name, list(shape), F32)).ap()
        return sbp[name]

    for nm in CONST_NAMES:
        S("s" + nm, _HC[nm].shape)
    S("s_onescol", (128, 1))
    S("s_onesrow", (1, 128))
    S("s_vrow", (1, 1024))
    S("s_gamb", (128, 1024))
    S("s_betb", (128, 1024))
    for nm in ATTN_VECS:
        S("v_" + nm, (128, 8))
    S("v_ka1m", (128, 8))
    S("s_w1", (128, 8 * 64))
    S("s_w2", (64, 1024))
    S("s_a1", (128, 8 * 64))
    S("s_a2", (64, 1024))
    S("s_g1", (128, 8 * 128))
    S("s_g2", (128, 1024))
    S("s_gw", (128, 8 * 8))
    S("s_SS", (128, 8, 64))
    S("s_ln1last", (128, 8))
    S("s_e05n", (128, 1))

    for i, v in enumerate(CONST_VALS):
        nc.const_aps.register(F32, float(v), sbp["sc_bias"][:, i:i + 1])

    PSs = ctx.enter_context(nc.psum_tensor("PS", [128, 4096], F32)).ap()
    pb = [PSs[:, i * 512:(i + 1) * 512] for i in range(8)]
    pb.append(PSs)

    blk = ctx.enter_context(nc.Block())

    def body(sync):
        PE, ACT, DVE, GP = _PERec(nc.tensor, k), nc.scalar, nc.vector, nc.gpsimd
        for nm in CONST_NAMES:
            k.mark(sync.dma_start(out=sbp["s" + nm], in_=din[nm]), "dS", 16)
        for nm in ATTN_VECS:
            k.mark(sync.dma_start(out=sbp["v_" + nm],
                                  in_=din[nm].rearrange("(f p) -> p f", p=128),
                                  allow_slow_non_contiguous=True), "dS", 16)
        k.mark(sync.dma_start(out=sbp["s_w1"].rearrange("p (kc l) -> p kc l", l=64),
                              in_=din["w1"].rearrange("(kc p) l -> p kc l", p=128)), "dS", 16)
        k.mark(sync.dma_start(out=sbp["s_w2"], in_=din["w2"]), "dS", 16)
        k.mark(sync.dma_start(out=sbp["s_a1"].rearrange("p (kc l) -> p kc l", l=64),
                              in_=din["a1"].rearrange("(kc p) l -> p kc l", p=128)), "dS", 16)
        k.mark(sync.dma_start(out=sbp["s_a2"], in_=din["a2"]), "dS", 16)
        k.mark(sync.dma_start(out=sbp["s_g1"].rearrange("p (kc l) -> p kc l", l=128),
                              in_=din["g1"].rearrange("(kc p) l -> p kc l", p=128)), "dS", 16)
        k.mark(sync.dma_start(out=sbp["s_g2"], in_=din["g2"]), "dS", 16)
        ld = k.mark(sync.dma_start(out=sbp["s_gw"].rearrange("p (kc e) -> p kc e", e=8),
                                   in_=din["gate_w"].rearrange("(kc p) e -> p kc e", p=128)), "dS", 16)
        k.wait(DVE, ld)
        DVE.memset(sbp["s_onescol"], 1.0)
        DVE.memset(sbp["s_onesrow"], 1.0)
        DVE.tensor_scalar(sbp["v_ka1m"], sbp["v_k_a"], -1.0, 1.0, ALU.mult, ALU.add)
        DVE.memset(sbp["s_SS"], 0.0)
        DVE.memset(sbp["s_e05n"], -0.5)
        e = k.mark(DVE.memset(sbp["s_ln1last"], 0.0), "dve")
        k.wait(ACT, e)
        e = k.mark(ACT.activation(sbp["s_e05n"], sbp["s_e05n"], AF.Exp), "act")
        k.wait(DVE, e)
        e = k.mark(DVE.tensor_scalar_mul(sbp["s_e05n"], sbp["s_e05n"], -1.0), "dve")
        k.flush_dve(nc)
        for vec, dst in (("ln_g", "s_gamb"), ("ln_b", "s_betb")):
            k.wait(sync, k.ev("pe"))
            lv = k.mark(sync.dma_start(out=sbp["s_vrow"][0:1, :], in_=din[vec][None, :]), "dS", 16)
            k.wait(PE, lv, e)
            PE.matmul(pb[0][:, 0:512], sbp["s_onesrow"], sbp["s_vrow"][0:1, 0:512],
                      start=True, stop=True)
            PE.matmul(pb[1][:, 0:512], sbp["s_onesrow"],
                      sbp["s_vrow"][0:1, 512:1024], start=True, stop=True)
            ep = PE.ev_last()
            k.wait(DVE, ep)
            DVE.tensor_copy(sbp[dst][:, 0:512], pb[0][:, 0:512])
            e = k.mark(DVE.tensor_copy(sbp[dst][:, 512:1024], pb[1][:, 0:512]), "dve")

        if "b" in stages:
            phase_b(k, nc, din, dbg, sbp, pb, x2scr, sync)
        if "d" in stages:
            phase_d(k, nc, din, dbg, sbp, pb, x2scr, h1scr, ybscr, kscr, vscr, sync)
        if "e" in stages:
            phase_e(k, nc, din, dbg, sbp, pb, h1scr, ybscr, out, sync)

        GP.wait_ge(k.sems["dG"], k.cnt.get("dG", 0))
        sync.wait_ge(k.sems["dS"], k.cnt.get("dS", 0))
        PE.wait_ge(k.sems["dve"], k.cnt.get("dve", 0))
        PE.wait_ge(k.sems["act"], k.cnt.get("act", 0))

    @blk.sync
    def _(sync):
        body(sync)

    return nc, k


def phase_b(k, nc, din, dbg, sbp, pb, x2scr, sync):
    import os
    BSTEPS = int(os.environ.get("BSTEPS", "5"))
    T, SB = k.T, k.SB
    NSL = T // SB
    NCH = SB // LCH
    NTC = SB // 128
    PE, ACT, DVE, GP = _PERec(nc.tensor, k), nc.scalar, nc.vector, nc.gpsimd
    xT = din["xT"].rearrange("(f p) t -> p f t", p=128)
    x2v = x2scr.rearrange("(f p) t -> p f t", p=128)

    V = {nm: sbp["v_" + nm] for nm in ATTN_VECS}
    onescol, onesrow = sbp["s_onescol"], sbp["s_onesrow"]
    ident, i64b = sbp["sc_ident"], sbp["sc_i64b"]
    indgn, sel = sbp["sc_indgn"], sbp["sc_sel"]
    mask_s, maskT_s, maskT_i = sbp["sc_mask_s"], sbp["sc_maskT_s"], sbp["sc_maskT_i"]
    w1s, w2s, a1s, a2s = sbp["s_w1"], sbp["s_w2"], sbp["s_a1"], sbp["s_a2"]
    g1s, g2s = sbp["s_g1"], sbp["s_g2"]
    SS, ln1_last = sbp["s_SS"], sbp["s_ln1last"]
    gamb, betb = sbp["s_gamb"], sbp["s_betb"]
    ka1m = sbp["v_ka1m"]

    from contextlib import ExitStack
    es = ExitStack()

    def A(name, shape):
        return es.enter_context(nc.sbuf_tensor(name, list(shape), F32)).ap()

    assert SB == 128, "phase_b overlay pool assumes SB=128"
    pool1 = A("b_pool1", [128, 2048])
    x = A("b_x", [128, 8, SB])
    ln = pool1[:, 0:1024].rearrange("p (f t) -> p f t", t=SB)
    xx = pool1[:, 1024:2048].rearrange("p (f t) -> p f t", t=SB)
    mix = A("b_mix", [128, 8, SB])
    r_ = A("b_r", [128, 8, SB])
    kf = A("b_kf", [128, 8, SB])
    kkn = A("b_kkn", [128, 8, SB])
    bq = A("b_bq", [128, 8, SB])
    cc = A("b_cc", [128, 8, SB])
    vtm = A("b_vtm", [128, NTC, 1024])
    s1 = A("b_s1", [128, SB])
    sgb = A("b_sgb", [16, SB])
    wbuf = A("b_wbuf", [128, 2, 1024])
    lb = A("b_lb", [64, SB])
    smask = A("b_smask", [128, SB])
    rowm = A("b_rowm", [1, SB])
    rowr = A("b_rowr", [1, SB])
    rown = A("b_rown", [16, SB])
    E = A("b_E", [128, 8, LCH])
    Ei = A("b_Ei", [128, 8, LCH])
    Ep = A("b_Ep", [128, 8, LCH])
    EcL = A("b_EcL", [128, 8])
    rt = A("b_rt", [128, 8, LCH])
    at = A("b_at", [128, 8, LCH])
    bp_ = A("b_bp", [128, 8, LCH])
    kp = A("b_kp", [128, 8, LCH])
    ks = A("b_ks", [128, 8, LCH])
    bs = A("b_bs", [128, 8, LCH])
    Ap = A("b_Ap", [64, 1024])
    AkaT = A("b_AkaT", [64, 1024])
    ArbT = A("b_ArbT", [64, 1024])
    ArkT = A("b_ArkT", [64, 1024])
    U = A("b_U", [64, 1024])
    V2 = A("b_V2", [64, 1024])
    kst = A("b_kst", [64, 1024])
    bst = A("b_bst", [64, 1024])
    otm = A("b_otm", [128, NTC, 1024])
    gmu = A("b_gmu", [128, 64])[:, 0:16]
    gsc = A("b_gsc", [128, 64])[:, 0:16]
    grs = A("b_grs", [128, 64])[:, 0:16]
    sbT = A("b_sbT", [128, 64])[:, 0:16]
    ytm = pool1[:, 0:1024]
    zfm = pool1[:, 1024:2048].rearrange("p (f t) -> p f t", t=SB)
    Cl = [A("b_Cl%d" % i, [64, 1024]) for i in range(4)]
    Cl.append(pool1[0:64, 0:1024])
    Cl.append(pool1[0:64, 1024:2048])
    try:
        sq = mix

        DVE.memset(smask, 1.0)
        e_mask = k.mark(DVE.memset(
            smask.rearrange("p (c l) -> p c l", l=LCH)[:, :, 0:1], 0.0), "dve")

        def stream_W(name, kc, slot, pe_guard="auto"):
            k.wait(sync, k.ev("pe") if pe_guard == "auto" else pe_guard)
            return k.mark(sync.dma_start(out=wbuf[:, slot, :],
                                         in_=din[name][kc * 128:(kc + 1) * 128, :]), "dS", 16)

        for s in range(NSL):
            t0 = s * SB
            sl = slice(t0, t0 + SB)
            k.wait(ACT, k.ev("pe"), k.ev("dve"))
            k.wait(DVE, k.ev("pe"), k.ev("act"))
            k.wait(PE, k.ev("act"), k.ev("dve"))
            LNS = int(os.environ.get("LNSTEP", "9"))
            e_x = k.mark(sync.dma_start(out=x, in_=xT[:, :, sl]), "dS", 16)
            if LNS < 2:
                continue
            k.wait(ACT, e_x)
            e_sq = k.mark(ACT.activation(sq, x, AF.Square), "act")
            if LNS < 3:
                continue
            k.wait(PE, e_x, e_sq, k.ev("dve"))
            for f in range(8):
                PE.matmul(pb[6][0:1, 0:SB], onescol, x[:, f, :], start=(f == 0), stop=False)
            for f in range(8):
                PE.matmul(pb[6][0:1, SB:2 * SB], onescol, sq[:, f, :],
                          start=False, stop=(f == 7))
            e = PE.ev_last()
            if LNS < 4:
                continue
            LSUB = int(os.environ.get("LNSUB", "9"))
            k.wait(DVE, e)
            DVE.tensor_scalar_mul(rowm, pb[6][0:1, 0:SB], 1.0 / C)
            e = k.mark(DVE.tensor_scalar_mul(rowr, pb[6][0:1, SB:2 * SB], 1.0 / C), "dve")
            if LSUB >= 2:
                e = k.mark(DVE.tensor_mul(pb[6][0:1, 0:SB], rowm, rowm), "dve")
            if LSUB >= 3:
                DVE.tensor_sub(rowr, rowr, pb[6][0:1, 0:SB])
                e = k.mark(DVE.tensor_scalar_max(rowr, rowr, 0.0), "dve")
            if LSUB >= 4:
                k.wait(ACT, e)
                e = k.mark(ACT.activation(rowr, rowr, AF.Sqrt, bias=EPS_LN), "act")
            if LSUB >= 5:
                k.wait(DVE, e)
                e = k.mark(DVE.reciprocal(rowr, rowr), "dve")
            if LNS < 5:
                continue
            k.wait(PE, e)
            PE.matmul(pb[6][:, 0:SB], onesrow, rowm, start=True, stop=True)
            PE.matmul(pb[7][:, 0:SB], onesrow, rowr, start=True, stop=True)
            e = PE.ev_last()
            if LNS < 6:
                continue
            k.wait(DVE, e)
            for f in range(8):
                DVE.tensor_sub(ln[:, f, :], x[:, f, :], pb[6][:, 0:SB])
            for f in range(8):
                DVE.tensor_mul(ln[:, f, :], ln[:, f, :], pb[7][:, 0:SB])
            for f in range(8):
                DVE.tensor_sub(xx[:, f, 1:SB], ln[:, f, 0:SB - 1], ln[:, f, 1:SB])
                DVE.scalar_tensor_tensor(xx[:, f, 0:1], ln[:, f, 0:1], -1.0,
                                         ln1_last[:, f:f + 1], ALU.mult, ALU.add)
            e_ln = k.mark(DVE.tensor_copy(ln1_last, ln[:, :, SB - 1]), "dve")

            if BSTEPS < 2:
                continue
            def mk_mix(mvec, pe_guard):
                k.wait(DVE, pe_guard)
                for f in range(8):
                    DVE.scalar_tensor_tensor(mix[:, f, :], xx[:, f, :],
                                             mvec[:, f:f + 1], ln[:, f, :],
                                             ALU.mult, ALU.add)
                return k.mark(DVE.engine_nop(), "dve")

            def proj_fm(wname, evmix, evac):
                evw = [stream_W(wname, 0, 0), None]
                e_pe_prev = None
                for kc in range(8):
                    if kc + 1 < 8:
                        evw[(kc + 1) % 2] = stream_W(wname, kc + 1, (kc + 1) % 2,
                                                     pe_guard=e_pe_prev)
                    k.wait(PE, evw[kc % 2], evmix)
                    W = wbuf[:, kc % 2, :]
                    for oc in range(8):
                        bank = pb[oc // 2]
                        col = (oc % 2) * SB
                        PE.matmul(bank[:, col:col + SB], W[:, oc * 128:(oc + 1) * 128],
                                  mix[:, kc, :], start=(kc == 0 and oc % 2 == 0),
                                  stop=(kc == 7 and oc % 2 == 1))
                    e_pe_prev = PE.ev_last()
                for oc in range(8):
                    evac(oc, pb[oc // 2][:, (oc % 2) * SB:(oc % 2) * SB + SB], e_pe_prev)
                return e_pe_prev

            evm = mk_mix(V["mix_r"], k.ev("pe"))

            def ev_r(oc, ap, epe):
                k.wait(DVE, epe)
                k.mark(DVE.tensor_copy(r_[:, oc, :], ap), "dve")
            e_projr = proj_fm("Wr", evm, ev_r)

            evm = mk_mix(V["mix_k"], e_projr)

            def ev_k(oc, ap, epe):
                k.wait(DVE, epe)
                k.mark(DVE.tensor_copy(kf[:, oc, :], ap), "dve")
            e_projk = proj_fm("Wk", evm, ev_k)

            evm = mk_mix(V["mix_v"], e_projk)
            evw = [stream_W("Wv", 0, 0), None]
            e_pe_prev = None
            for kc in range(8):
                if kc + 1 < 8:
                    evw[(kc + 1) % 2] = stream_W("Wv", kc + 1, (kc + 1) % 2, pe_guard=e_pe_prev)
                k.wait(PE, evw[kc % 2], evm)
                W = wbuf[:, kc % 2, :]
                for tc in range(NTC):
                    for hf in range(2):
                        PE.matmul(pb[2 * tc + hf][:, 0:512],
                                  mix[:, kc, tc * 128:(tc + 1) * 128],
                                  W[:, hf * 512:(hf + 1) * 512],
                                  start=(kc == 0), stop=(kc == 7))
                e_pe_prev = PE.ev_last()
            k.wait(DVE, e_pe_prev)
            for tc in range(NTC):
                DVE.tensor_copy(vtm[:, tc, 0:512], pb[2 * tc][:, 0:512])
                k.mark(DVE.tensor_copy(vtm[:, tc, 512:1024], pb[2 * tc + 1][:, 0:512]), "dve")

            if BSTEPS < 3:
                continue
            e_kkt = None
            k.wait(DVE, k.ev("pe"))
            for f in range(8):
                DVE.tensor_scalar_mul(kkn[:, f, :], kf[:, f, :], V["k_k"][:, f:f + 1])
            e_kkt = k.mark(DVE.engine_nop(), "dve")
            k.wait(ACT, e_kkt)
            e_sq2 = k.mark(ACT.activation(mix, kkn, AF.Square), "act")
            k.wait(PE, e_sq2)
            for f in range(8):
                PE.matmul(pb[4][0:16, 0:SB], indgn[:, f * 16:(f + 1) * 16],
                          mix[:, f, :], start=(f == 0), stop=(f == 7))
            e = PE.ev_last()
            k.wait(ACT, e)
            e = k.mark(ACT.activation(rown, pb[4][0:16, 0:SB], AF.Sqrt), "act")
            k.wait(DVE, e)
            DVE.tensor_scalar_max(rown, rown, 1e-12)
            e = k.mark(DVE.reciprocal(rown, rown), "dve")
            k.wait(PE, e)
            for f in range(8):
                k.wait(PE, k.ev("dve"))
                PE.matmul(pb[4 + f % 2][:, 0:SB], sel[:, f * 128:(f + 1) * 128],
                          rown, start=True, stop=True)
                e = PE.ev_last()
                k.wait(DVE, e)
                k.mark(DVE.tensor_mul(kkn[:, f, :], kkn[:, f, :], pb[4 + f % 2][:, 0:SB]), "dve")

            evm = mk_mix(V["mix_w"], k.ev("pe"))
            k.wait(PE, evm)
            for kc in range(8):
                PE.matmul(pb[0][0:64, 0:SB], w1s[:, kc * 64:(kc + 1) * 64],
                          mix[:, kc, :], start=(kc == 0), stop=(kc == 7))
            e = PE.ev_last()
            k.wait(ACT, e)
            e_tanh = k.mark(ACT.activation(lb, pb[0][0:64, 0:SB], AF.Tanh), "act")
            k.wait(PE, e_tanh)
            for ocp in range(4):
                bank = pb[1 + ocp]
                for j in range(2):
                    oc = 2 * ocp + j
                    PE.matmul(bank[:, j * SB:(j + 1) * SB],
                              w2s[:, oc * 128:(oc + 1) * 128], lb,
                              start=(j == 0), stop=(j == 1))
                e = PE.ev_last()
                k.wait(ACT, e)
                ACT.activation(cc[:, 2 * ocp, :], bank[:, 0:SB], AF.Sigmoid,
                               bias=V["w0"][:, 2 * ocp:2 * ocp + 1])
                e_cc = k.mark(ACT.activation(cc[:, 2 * ocp + 1, :], bank[:, SB:2 * SB],
                                             AF.Sigmoid,
                                             bias=V["w0"][:, 2 * ocp + 1:2 * ocp + 2]), "act")

            evm = mk_mix(V["mix_a"], k.ev("pe"))
            k.wait(PE, evm)
            for kc in range(8):
                PE.matmul(pb[0][0:64, 0:SB], a1s[:, kc * 64:(kc + 1) * 64],
                          mix[:, kc, :], start=(kc == 0), stop=(kc == 7))
            e = PE.ev_last()
            k.wait(DVE, e)
            e_za = k.mark(DVE.tensor_copy(lb, pb[0][0:64, 0:SB]), "dve")
            k.wait(PE, e_za)
            for ocp in range(4):
                bank = pb[1 + ocp]
                for j in range(2):
                    oc = 2 * ocp + j
                    PE.matmul(bank[:, j * SB:(j + 1) * SB],
                              a2s[:, oc * 128:(oc + 1) * 128], lb,
                              start=(j == 0), stop=(j == 1))
                e = PE.ev_last()
                k.wait(ACT, e)
                ACT.activation(bq[:, 2 * ocp, :], bank[:, 0:SB], AF.Sigmoid,
                               bias=V["a0"][:, 2 * ocp:2 * ocp + 1])
                e_ag = k.mark(ACT.activation(bq[:, 2 * ocp + 1, :], bank[:, SB:2 * SB],
                                             AF.Sigmoid,
                                             bias=V["a0"][:, 2 * ocp + 1:2 * ocp + 2]), "act")

            evm = mk_mix(V["mix_g"], k.ev("pe"))
            k.wait(PE, evm)
            for kc in range(8):
                PE.matmul(pb[0][:, 0:SB], g1s[:, kc * 128:(kc + 1) * 128],
                          mix[:, kc, :], start=(kc == 0), stop=(kc == 7))
            e_g1pe = PE.ev_last()
            k.wait(ACT, e_g1pe)
            e_s1 = k.mark(ACT.activation(s1, pb[0][:, 0:SB], AF.Sigmoid), "act")

            k.wait(DVE, e_ag, e_cc, e_g1pe)
            for f in range(8):
                DVE.tensor_scalar(mix[:, f, :], bq[:, f, :], V["k_a"][:, f:f + 1],
                                  None, ALU.mult)
                DVE.tensor_scalar_add(mix[:, f, :], mix[:, f, :], ka1m[:, f:f + 1])
            DVE.tensor_mul(kf, kf, mix)
            for f in range(8):
                DVE.tensor_scalar_mul(mix[:, f, :], r_[:, f, :], V["r_k"][:, f:f + 1])
            e = k.mark(DVE.tensor_mul(mix, mix, kf), "dve")
            k.wait(PE, e)
            k.wait(PE, k.ev("dve"))
            for f in range(8):
                PE.matmul(pb[4][0:16, 0:SB], indgn[:, f * 16:(f + 1) * 16],
                          mix[:, f, :], start=(f == 0), stop=(f == 7))
            e = PE.ev_last()
            k.wait(DVE, e)
            k.mark(DVE.tensor_copy(sgb, pb[4][0:16, 0:SB]), "dve")
            e_bq = k.mark(DVE.tensor_mul(bq, kkn, bq), "dve")
            k.wait(DVE, e_cc, e_mask)
            for f in range(8):
                DVE.tensor_tensor_scan(cc[:, f, :], smask, cc[:, f, :], 0.0,
                                       ALU.mult, ALU.add)
            for f in range(8):
                DVE.tensor_scalar_mul(cc[:, f, :], cc[:, f, :],
                                      sbp["s_e05n"][:, 0:1])
            e_scan = k.mark(DVE.engine_nop(), "dve")

            if BSTEPS < 4:
                continue
            psW0 = pb[8][0:64, 0:1024]
            psW1 = pb[8][0:64, 1024:2048]
            psO = pb[8][:, 2048:3072]
            psSa = pb[8][0:64, 3072:3584]
            psSb = pb[8][64:128, 3584:4096]

            prev_e_rs = None
            prev_e_ss = None
            for ch in range(NCH):
                tc = ch // 2
                rb = (ch % 2) * 64
                csl = slice(ch * LCH, (ch + 1) * LCH)
                k.wait(ACT, e_scan, e_bq, prev_e_rs, prev_e_ss)
                ACT.activation(E, cc[:, :, csl], AF.Exp)
                e1 = k.mark(ACT.activation(Ei, cc[:, :, csl], AF.Exp, scale=-1.0), "act")
                e2 = k.mark(ACT.activation(EcL, cc[:, :, ch * LCH + LCH - 1],
                                           AF.Exp), "act")
                k.wait(DVE, e1, e2)
                DVE.memset(Ep[:, :, 0:1], 1.0)
                DVE.tensor_copy(Ep[:, :, 1:LCH], E[:, :, 0:LCH - 1])
                DVE.tensor_mul(rt, r_[:, :, csl], E)
                DVE.scalar_tensor_tensor(at, Ep, -1.0, kkn[:, :, csl], ALU.mult, ALU.mult)
                DVE.tensor_mul(bp_, bq[:, :, csl], Ei)
                DVE.tensor_mul(kp, kf[:, :, csl], Ei)
                for f in range(8):
                    DVE.tensor_scalar_mul(ks[:, f, :], Ei[:, f, :], EcL[:, f:f + 1])
                DVE.tensor_mul(bs, bq[:, :, csl], ks)
                e_rs = k.mark(DVE.tensor_mul(ks, kf[:, :, csl], ks), "dve")
                prev_e_rs = e_rs
                CST = int(os.environ.get("CSTEP", "9"))
                if CST < 2:
                    continue

                HEADS = [(h, h // 2, (h % 2) * 64) for h in range(16)]

                k.wait(PE, k.ev("dve"))
                for cb in range(2):
                    PE.matmul(psW0[:, cb * 512:(cb + 1) * 512],
                              i64b[rb:rb + 64, 0:64],
                              vtm[rb:rb + 64, tc, cb * 512:(cb + 1) * 512],
                              start=True, stop=True, tile_position=(rb, 0))
                e = PE.ev_last()
                k.wait(DVE, e)
                e_v2 = k.mark(DVE.tensor_copy(V2, psW0), "dve")

                if CST < 3:
                    continue
                specs = [
                    (Ap, at, bp_, mask_s),
                    (Cl[0], bp_, at, maskT_s),
                    (AkaT, kp, at, maskT_s),
                    (ArbT, bp_, rt, maskT_i),
                    (ArkT, kp, rt, maskT_i),
                ]
                for idx, (dst, lh, rh, msk) in enumerate(specs):
                    bank = psW1 if idx % 2 == 0 else psW0
                    k.wait(PE, k.ev("dve"))
                    for h, f, nh in HEADS:
                        PE.matmul(bank[:, h * 64:(h + 1) * 64],
                                  lh[nh:nh + 64, f, :], rh[nh:nh + 64, f, :],
                                  start=(h % 8 == 0), stop=(h % 8 == 7),
                                  tile_position=(nh, 0))
                    e = PE.ev_last()
                    k.wait(DVE, e)
                    k.mark(DVE.tensor_mul(dst, bank, msk[0:64, :]), "dve")

                if CST < 4:
                    continue
                for lev in range(1, 6):
                    k.wait(PE, k.ev("dve"))
                    for h, f, nh in HEADS:
                        PE.matmul(psW1[:, h * 64:(h + 1) * 64],
                                  Ap[:, h * 64:(h + 1) * 64],
                                  Cl[lev - 1][:, h * 64:(h + 1) * 64],
                                  start=(h % 8 == 0), stop=(h % 8 == 7),
                                  tile_position=(0, 0))
                    e_c2 = PE.ev_last()
                    if lev < 5:
                        for h, f, nh in HEADS:
                            PE.matmul(psW0[:, h * 64:(h + 1) * 64],
                                      Cl[lev - 1][:, h * 64:(h + 1) * 64],
                                      Ap[:, h * 64:(h + 1) * 64],
                                      start=(h % 8 == 0), stop=(h % 8 == 7),
                                      tile_position=(0, 0))
                        e_a2 = PE.ev_last()
                        k.wait(DVE, e_a2)
                        DVE.tensor_copy(Ap, psW0)
                    k.wait(DVE, e_c2)
                    k.mark(DVE.tensor_copy(Cl[lev], psW1), "dve")

                if CST < 5:
                    continue
                k.wait(PE, k.ev("dve"), e_v2)
                for h, f, nh in HEADS:
                    PE.matmul(psW0[:, h * 64:(h + 1) * 64],
                              AkaT[:, h * 64:(h + 1) * 64],
                              V2[:, h * 64:(h + 1) * 64],
                              start=(h % 8 == 0), stop=False, tile_position=(0, 0))
                for h, f, nh in HEADS:
                    PE.matmul(psW0[:, h * 64:(h + 1) * 64],
                              at[nh:nh + 64, f, :], SS[nh:nh + 64, f, :],
                              start=False, stop=(h % 8 == 7), tile_position=(nh, 0))
                e = PE.ev_last()
                k.wait(DVE, e)
                e_u = k.mark(DVE.tensor_copy(U, psW0), "dve")

                for lev in (5, 4, 3, 2, 1, 0):
                    k.wait(PE, e_u)
                    for h, f, nh in HEADS:
                        PE.matmul(psW0[:, h * 64:(h + 1) * 64],
                                  Cl[lev][:, h * 64:(h + 1) * 64],
                                  U[:, h * 64:(h + 1) * 64],
                                  start=(h % 8 == 0), stop=(h % 8 == 7),
                                  tile_position=(0, 0))
                    e = PE.ev_last()
                    k.wait(DVE, e)
                    e_u = k.mark(DVE.tensor_add(U, U, psW0), "dve")

                if CST < 6:
                    continue
                k.wait(PE, e_u)
                for h, f, nh in HEADS:
                    PE.matmul(psO[rb:rb + 64, h * 64:(h + 1) * 64],
                              rt[nh:nh + 64, f, :], SS[nh:nh + 64, f, :],
                              start=(h % 8 == 0), stop=False, tile_position=(nh, rb))
                    PE.matmul(psO[rb:rb + 64, h * 64:(h + 1) * 64],
                              ArbT[:, h * 64:(h + 1) * 64],
                              U[:, h * 64:(h + 1) * 64],
                              start=False, stop=False, tile_position=(0, rb))
                    PE.matmul(psO[rb:rb + 64, h * 64:(h + 1) * 64],
                              ArkT[:, h * 64:(h + 1) * 64],
                              V2[:, h * 64:(h + 1) * 64],
                              start=False, stop=(h % 8 == 7), tile_position=(0, rb))
                e_o = PE.ev_last()

                if CST < 7:
                    continue
                for h, f, nh in HEADS:
                    PE.matmul(psW0[:, h * 64:(h + 1) * 64],
                              ks[nh:nh + 64, f, :], i64b[nh:nh + 64, 0:64],
                              start=(h % 8 == 0), stop=(h % 8 == 7),
                              tile_position=(nh, 0))
                e_k2 = PE.ev_last()
                for h, f, nh in HEADS:
                    PE.matmul(psW1[:, h * 64:(h + 1) * 64],
                              bs[nh:nh + 64, f, :], i64b[nh:nh + 64, 0:64],
                              start=(h % 8 == 0), stop=(h % 8 == 7),
                              tile_position=(nh, 0))
                e_b2 = PE.ev_last()
                k.wait(DVE, e_k2, e_b2)
                DVE.tensor_copy(kst, psW0)
                e_t = k.mark(DVE.tensor_copy(bst, psW1), "dve")

                k.wait(PE, e_t)
                for par, bankS in ((0, psSa), (1, psSb)):
                    hs = [h for h in range(16) if h % 2 == par]
                    for i_h, h in enumerate(hs):
                        f = h // 2
                        nh = par * 64
                        PE.matmul(bankS[:, f * 64:(f + 1) * 64],
                                  bst[:, h * 64:(h + 1) * 64],
                                  U[:, h * 64:(h + 1) * 64],
                                  start=(i_h == 0), stop=False,
                                  tile_position=(0, nh))
                        PE.matmul(bankS[:, f * 64:(f + 1) * 64],
                                  kst[:, h * 64:(h + 1) * 64],
                                  V2[:, h * 64:(h + 1) * 64],
                                  start=False, stop=(i_h == 7),
                                  tile_position=(0, nh))
                e_st = PE.ev_last()
                k.wait(DVE, e_st)
                for f in range(8):
                    DVE.tensor_scalar_mul(SS[:, f, :], SS[:, f, :], EcL[:, f:f + 1])
                SSf = SS.rearrange("p a b -> p (a b)")
                DVE.tensor_add(SSf[0:64, :], SSf[0:64, :], psSa)
                e_ss = k.mark(DVE.tensor_add(SSf[64:128, :], SSf[64:128, :], psSb), "dve")
                prev_e_ss = e_ss

                if CST < 8:
                    continue
                k.wait(DVE, e_o)
                k.mark(DVE.tensor_copy(otm[rb:rb + 64, tc, :], psO[rb:rb + 64, :]), "dve")
                if "d_SS" in dbg:
                    gch_ = s * NCH + ch
                    k.wait(GP, e_ss)
                    k.mark(GP.dma_start(out=dbg["d_SS"][gch_ * 128:(gch_ + 1) * 128, 0:512],
                                        in_=SS.rearrange("p a b -> p (a b)")), "dG", 16)
                    k.wait(DVE, k.ev("dG"))
                    k.wait(PE, k.ev("dG"))

                if s == 0 and ch == 0:
                    for dnm, dt_ in (("d_Ap", Ap), ("d_Cl0", Cl[0]), ("d_Cl5", Cl[5]),
                                     ("d_AkaT", AkaT), ("d_ArbT", ArbT),
                                     ("d_ArkT", ArkT), ("d_U", U), ("d_V2", V2),
                                     ("d_kst", kst), ("d_bst", bst)):
                        if dnm in dbg:
                            k.wait(GP, k.ev("dve"))
                            k.mark(GP.dma_start(out=dbg[dnm], in_=dt_), "dG", 16)
                    if "d_at" in dbg:
                        k.wait(GP, k.ev("dve"))
                        for dnm, dt_ in (("d_at", at), ("d_bp", bp_), ("d_kp", kp),
                                         ("d_rt", rt), ("d_ks", ks), ("d_bs", bs),
                                         ("d_E", E), ("d_Ei", Ei), ("d_Ep", Ep)):
                            k.mark(GP.dma_start(
                                out=dbg[dnm],
                                in_=dt_.rearrange("p a b -> p (a b)")), "dG", 16)
                    k.wait(DVE, k.ev("dG"))
                    k.wait(PE, k.ev("dG"))
                    k.wait(ACT, k.ev("dG"))

            if BSTEPS < 5:
                continue
            evW = [stream_W("Wo", 0, 0), None]
            for tc in range(NTC):
                k.wait(PE, k.ev("dve"))
                PE.transpose(pb[4][0:128, 0:16],
                             sgb[0:16, tc * 128:(tc + 1) * 128],
                             ident[0:16, 0:16])
                e = PE.ev_last()
                k.wait(DVE, e)
                e_sbt = k.mark(DVE.tensor_copy(sbT, pb[4][0:128, 0:16]), "dve")
                k.flush_dve(nc)
                if s == 0 and tc == 0 and "d_otmB" in dbg:
                    k.wait(GP, k.ev("dve"))
                    k.mark(GP.dma_start(out=dbg["d_otmB"][0:128, :], in_=otm[:, 0, :]), "dG", 16)
                    k.wait(DVE, k.ev("dG"))
                for h in range(16):
                    DVE.tensor_reduce(gmu[:, h:h + 1],
                                      otm[:, tc, h * 64:(h + 1) * 64], AX.X, ALU.add)
                k.flush_dve(nc)
                e = k.mark(DVE.tensor_scalar_mul(gmu, gmu, 1.0 / 64), "dve")
                k.flush_dve(nc)
                for h in range(16):
                    DVE.tensor_scalar(ytm[:, h * 64:(h + 1) * 64],
                                      otm[:, tc, h * 64:(h + 1) * 64],
                                      gmu[:, h:h + 1], None, ALU.subtract)
                e = k.mark(DVE.engine_nop(), "dve")
                k.wait(ACT, e)
                zfl = zfm.rearrange("p a b -> p (a b)")
                e = k.mark(ACT.activation(zfl, ytm, AF.Square), "act")
                k.wait(DVE, e)
                for h in range(16):
                    DVE.tensor_reduce(grs[:, h:h + 1],
                                      zfl[:, h * 64:(h + 1) * 64], AX.X, ALU.add)
                k.flush_dve(nc)
                e = k.mark(DVE.tensor_scalar_mul(grs, grs, 1.0 / 64), "dve")
                k.wait(ACT, e)
                e = k.mark(ACT.activation(grs, grs, AF.Sqrt, bias=EPS_GN), "act")
                k.wait(DVE, e)
                e = k.mark(DVE.reciprocal(grs, grs), "dve")
                k.flush_dve(nc)
                if s == 0 and "d_gmu" in dbg:
                    k.wait(GP, e)
                    k.mark(GP.dma_start(out=dbg["d_gmu"][tc * 128:(tc + 1) * 128, 0:16],
                                        in_=gmu), "dG", 16)
                    k.mark(GP.dma_start(out=dbg["d_grs"][tc * 128:(tc + 1) * 128, 0:16],
                                        in_=grs), "dG", 16)
                    k.mark(GP.dma_start(out=dbg["d_sbT"][tc * 128:(tc + 1) * 128, 0:16],
                                        in_=sbT), "dG", 16)
                    k.wait(DVE, k.ev("dG"))
                for h in range(16):
                    DVE.tensor_scalar(ytm[:, h * 64:(h + 1) * 64],
                                      ytm[:, h * 64:(h + 1) * 64],
                                      grs[:, h:h + 1], None, ALU.mult)
                if s == 0 and tc == 0 and "d_yh" in dbg:
                    e_ = k.mark(DVE.engine_nop(), "dve")
                    k.wait(GP, e_)
                    k.mark(GP.dma_start(out=dbg["d_yh"][0:128, :], in_=ytm), "dG", 16)
                    k.mark(GP.dma_start(out=dbg["d_gam"][0:128, :], in_=gamb), "dG", 16)
                    k.mark(GP.dma_start(out=dbg["d_bet"][0:128, :], in_=betb), "dG", 16)
                    k.wait(DVE, k.ev("dG"))
                DVE.tensor_mul(ytm, ytm, gamb)
                DVE.tensor_add(ytm, ytm, betb)
                k.wait(DVE, e_sbt)
                for h in range(16):
                    DVE.scalar_tensor_tensor(ytm[:, h * 64:(h + 1) * 64],
                                             vtm[:, tc, h * 64:(h + 1) * 64],
                                             sbT[:, h:h + 1],
                                             ytm[:, h * 64:(h + 1) * 64],
                                             ALU.mult, ALU.add)
                e = k.mark(DVE.engine_nop(), "dve")
                if s == 0 and "d_y" in dbg:
                    k.wait(GP, e)
                    k.mark(GP.dma_start(out=dbg["d_y"][tc * 128:(tc + 1) * 128, :],
                                        in_=ytm), "dG", 16)
                    k.wait(DVE, k.ev("dG"))
                k.wait(PE, e_s1, e)
                PE.matmul(pb[4][:, 0:512], s1[:, tc * 128:(tc + 1) * 128],
                          g2s[:, 0:512], start=True, stop=True)
                PE.matmul(pb[5][:, 0:512], s1[:, tc * 128:(tc + 1) * 128],
                          g2s[:, 512:1024], start=True, stop=True)
                e = PE.ev_last()
                k.wait(DVE, e)
                DVE.tensor_mul(ytm[:, 0:512], ytm[:, 0:512], pb[4][:, 0:512])
                e = k.mark(DVE.tensor_mul(ytm[:, 512:1024], ytm[:, 512:1024],
                                          pb[5][:, 0:512]), "dve")
                if s == 0 and "d_z" in dbg:
                    k.wait(GP, e)
                    k.mark(GP.dma_start(out=dbg["d_z"][tc * 128:(tc + 1) * 128, :],
                                        in_=ytm), "dG", 16)
                    k.wait(DVE, k.ev("dG"))
                    k.wait(PE, k.ev("dG"))
                k.wait(PE, e)
                for oc in range(8):
                    PE.transpose(pb[4 + oc // 4][:, (oc % 4) * 128:(oc % 4 + 1) * 128],
                                 ytm[:, oc * 128:(oc + 1) * 128], ident)
                e = PE.ev_last()
                k.wait(DVE, e)
                for oc in range(8):
                    k.mark(DVE.tensor_copy(zfm[:, oc, tc * 128:(tc + 1) * 128],
                                           pb[4 + oc // 4][:, (oc % 4) * 128:(oc % 4 + 1) * 128]),
                           "dve")

            e_pe_prev = None
            for kc in range(8):
                if kc + 1 < 8:
                    evW[(kc + 1) % 2] = stream_W("Wo", kc + 1, (kc + 1) % 2,
                                                 pe_guard=e_pe_prev)
                k.wait(PE, evW[kc % 2], k.ev("dve"))
                W = wbuf[:, kc % 2, :]
                for oc in range(8):
                    bank = pb[oc // 2]
                    col = (oc % 2) * SB
                    PE.matmul(bank[:, col:col + SB], W[:, oc * 128:(oc + 1) * 128],
                              zfm[:, kc, :], start=(kc == 0 and oc % 2 == 0),
                              stop=(kc == 7 and oc % 2 == 1))
                e_pe_prev = PE.ev_last()
            k.wait(DVE, e_pe_prev)
            for oc in range(8):
                DVE.tensor_add(x[:, oc, :], x[:, oc, :],
                               pb[oc // 2][:, (oc % 2) * SB:(oc % 2) * SB + SB])
            e = k.mark(DVE.engine_nop(), "dve")
            k.wait(GP, e)
            ew = k.mark(GP.dma_start(out=x2v[:, :, sl], in_=x), "dG", 16)
            if "d_x2" in dbg:
                dv = dbg["d_x2"].rearrange("(f p) t -> p f t", p=128)
                k.mark(GP.dma_start(out=dv[:, :, sl], in_=x), "dG", 16)
            k.wait(sync, k.ev("dG"))

            if "d_ln1" in dbg:
                dv = dbg["d_ln1"].rearrange("(f p) t -> p f t", p=128)
                k.mark(GP.dma_start(out=dv[:, :, sl], in_=ln), "dG", 16)
            if "d_r" in dbg:
                dv = dbg["d_r"].rearrange("(f p) t -> p f t", p=128)
                k.mark(GP.dma_start(out=dv[:, :, sl], in_=r_), "dG", 16)
            if "d_kf" in dbg:
                dv = dbg["d_kf"].rearrange("(f p) t -> p f t", p=128)
                k.mark(GP.dma_start(out=dv[:, :, sl], in_=kf), "dG", 16)
            if "d_kkn" in dbg:
                dv = dbg["d_kkn"].rearrange("(f p) t -> p f t", p=128)
                k.mark(GP.dma_start(out=dv[:, :, sl], in_=kkn), "dG", 16)
            if "d_cc" in dbg:
                dv = dbg["d_cc"].rearrange("(f p) t -> p f t", p=128)
                k.mark(GP.dma_start(out=dv[:, :, sl], in_=cc), "dG", 16)
            if "d_otm" in dbg:
                dv = dbg["d_otm"]
                for tc in range(NTC):
                    gt = s * NTC + tc
                    k.mark(GP.dma_start(
                        out=dv[gt * 128:(gt + 1) * 128, 0:1024],
                        in_=otm[:, tc, :]), "dG", 16)
    finally:
        es.close()


def phase_d(k, nc, din, dbg, sbp, pb, x2scr, h1scr, ybscr, kscr, vscr, sync):
    T, SD = k.T, k.SD
    SH = SD // 2
    NSD = T // SD
    NTC = SD // 128
    PE, ACT, DVE, GP = _PERec(nc.tensor, k), nc.scalar, nc.vector, nc.gpsimd
    x2v = x2scr.rearrange("(f p) t -> p f t", p=128)
    h1v = h1scr.rearrange("(f p) t -> p f t", p=128)
    kv = kscr.rearrange("(f p) t -> p f t", p=128)
    ybv = ybscr.rearrange("(tc p) c -> p tc c", p=128)
    onescol, onesrow = sbp["s_onescol"], sbp["s_onesrow"]
    ident = sbp["sc_ident"]
    negI, u01 = sbp["sc_negI"], sbp["sc_u01"]
    fg1 = sbp["v_fg1"]

    from contextlib import ExitStack
    es = ExitStack()

    def A(name, shape):
        return es.enter_context(nc.sbuf_tensor(name, list(shape), F32)).ap()

    x2 = A("d_x2s", [128, 8, SD])
    h0 = A("d_h0", [128, 8, SD])
    t1 = A("ds_t1", [128, 8, SD])
    cdk = A("d_cdk", [128, 4, SD])
    q = A("d_q", [128, 8, SD])
    att = A("d_att", [128, 8, SD])
    kb = A("d_kb", [128, 2, 8, 128])
    vb = A("d_vb", [128, 2, 1040])
    Pb = A("d_Pb", [128, 512])
    wbuf = A("d_wbuf", [128, 2, 1024])
    rowm = A("d_rowm", [1, SD])
    rowr = A("d_rowr", [1, SD])
    zrow = A("d_zrow", [65, SD])
    zbc = A("d_zbc", [64, SD])
    ones65 = A("d_ones65", [65, 64])
    ytm = A("d_ytm", [128, 1024])

    try:
        e = k.mark(DVE.memset(ones65, 1.0), "dve")
        for s in range(NSD):
            t0 = s * SD
            sl = slice(t0, t0 + SD)
            k.wait(ACT, k.ev("pe"), k.ev("dve"))
            k.wait(DVE, k.ev("pe"), k.ev("act"))
            k.wait(PE, k.ev("act"), k.ev("dve"))
            k.wait(sync, k.ev("dG"))
            e_x = k.mark(sync.dma_start(out=x2, in_=x2v[:, :, sl]), "dS", 16)
            k.wait(ACT, e_x)
            e_sq = k.mark(ACT.activation(h0, x2, AF.Square), "act")
            k.wait(PE, e_x, e_sq)
            for f in range(8):
                PE.matmul(pb[6][0:1, 0:SD], onescol, x2[:, f, :], start=(f == 0),
                          stop=(f == 7))
            for f in range(8):
                PE.matmul(pb[7][0:1, 0:SD], onescol, h0[:, f, :], start=(f == 0),
                          stop=(f == 7))
            e = PE.ev_last()
            k.wait(DVE, e)
            DVE.tensor_scalar_mul(rowm, pb[6][0:1, 0:SD], 1.0 / C)
            DVE.tensor_scalar_mul(rowr, pb[7][0:1, 0:SD], 1.0 / C)
            DVE.tensor_mul(pb[6][0:1, 0:SD], rowm, rowm)
            DVE.tensor_sub(rowr, rowr, pb[6][0:1, 0:SD])
            e = k.mark(DVE.tensor_scalar_max(rowr, rowr, 0.0), "dve")
            k.wait(ACT, e)
            e = k.mark(ACT.activation(rowr, rowr, AF.Sqrt, bias=EPS_LN), "act")
            k.wait(DVE, e)
            e = k.mark(DVE.reciprocal(rowr, rowr), "dve")
            k.wait(PE, e)
            PE.matmul(pb[6][:, 0:SD], onesrow, rowm, start=True, stop=True)
            PE.matmul(pb[7][:, 0:SD], onesrow, rowr, start=True, stop=True)
            e = PE.ev_last()
            k.wait(DVE, e)
            for f in range(8):
                DVE.tensor_sub(h0[:, f, :], x2[:, f, :], pb[6][:, 0:SD])
            for f in range(8):
                DVE.tensor_mul(h0[:, f, :], h0[:, f, :], pb[7][:, 0:SD])
            e_h0 = k.mark(DVE.engine_nop(), "dve")
            k.wait(ACT, e_h0)
            e_sq = k.mark(ACT.activation(t1, h0, AF.Square), "act")
            k.wait(PE, e_sq)
            for f in range(8):
                PE.matmul(pb[6][0:1, 0:SD], onescol, t1[:, f, :], start=(f == 0),
                          stop=(f == 7))
            e = PE.ev_last()
            k.wait(DVE, e)
            e = k.mark(DVE.tensor_scalar(rowr, pb[6][0:1, 0:SD], 1.0 / C, EPS_RMS,
                                         ALU.mult, ALU.add), "dve")
            k.wait(ACT, e)
            e = k.mark(ACT.activation(rowr, rowr, AF.Sqrt), "act")
            k.wait(DVE, e)
            e = k.mark(DVE.reciprocal(rowr, rowr), "dve")
            k.wait(PE, e)
            PE.matmul(pb[6][:, 0:SD], onesrow, rowr, start=True, stop=True)
            e = PE.ev_last()
            k.wait(DVE, e)
            for f in range(8):
                DVE.scalar_tensor_tensor(t1[:, f, :], h0[:, f, :], fg1[:, f:f + 1],
                                         pb[6][:, 0:SD], ALU.mult, ALU.mult)
            e_t1 = k.mark(DVE.engine_nop(), "dve")

            def stream_W(name, kc, slot):
                ncols = din[name].shape[1]
                k.wait(sync, k.ev("pe"))
                return k.mark(sync.dma_start(out=wbuf[:, slot, 0:ncols],
                                             in_=din[name][kc * 128:(kc + 1) * 128, :]),
                              "dS", 16)

            evw = [stream_W("Wdkv", 0, 0), None]
            e_pe_prev = None
            for kc in range(8):
                if kc + 1 < 8:
                    evw[(kc + 1) % 2] = stream_W("Wdkv", kc + 1, (kc + 1) % 2)
                k.wait(PE, evw[kc % 2], e_t1)
                W = wbuf[:, kc % 2, 0:512]
                for oc in range(4):
                    PE.matmul(pb[oc][:, 0:SD], W[:, oc * 128:(oc + 1) * 128],
                              t1[:, kc, :], start=(kc == 0), stop=(kc == 7))
                e_pe_prev = PE.ev_last()
            k.wait(DVE, e_pe_prev)
            for oc in range(4):
                k.mark(DVE.tensor_copy(cdk[:, oc, :], pb[oc][:, 0:SD]), "dve")
            evw = [stream_W("Wq", 0, 0), None]
            for kc in range(8):
                if kc + 1 < 8:
                    evw[(kc + 1) % 2] = stream_W("Wq", kc + 1, (kc + 1) % 2)
                k.wait(PE, evw[kc % 2], k.ev("dve"))
                W = wbuf[:, kc % 2, :]
                for oc in range(8):
                    PE.matmul(pb[oc][:, 0:SD],
                              W[:, oc * 128:(oc + 1) * 128],
                              t1[:, kc, :], start=(kc == 0), stop=(kc == 7))
                e_pe_prev = PE.ev_last()
            k.wait(ACT, e_pe_prev)
            for oc in range(8):
                ACT.activation(q[:, oc, :], pb[oc][:, 0:SD],
                               AF.Identity, scale=0.125)
            e_q = k.mark(ACT.nop(), "act")

            if s == 0 and "d_t1" in dbg:
                k.wait(GP, e_t1, e_q)
                dv = dbg["d_t1"].rearrange("(f p) t -> p f t", p=128)
                k.mark(GP.dma_start(out=dv[:, :, sl], in_=t1), "dG", 16)
                dv = dbg["d_qd"].rearrange("(f p) t -> p f t", p=128)
                k.mark(GP.dma_start(out=dv[:, :, sl], in_=q), "dG", 16)
                k.wait(DVE, k.ev("dG"))
                k.wait(PE, k.ev("dG"))
            evw = [stream_W("Wuk", 0, 0), None]
            for kc in range(4):
                if kc + 1 < 4:
                    evw[(kc + 1) % 2] = stream_W("Wuk", kc + 1, (kc + 1) % 2)
                k.wait(PE, evw[kc % 2], k.ev("act"))
                W = wbuf[:, kc % 2, :]
                for oc in range(8):
                    PE.matmul(pb[oc][:, 0:SD], W[:, oc * 128:(oc + 1) * 128],
                              cdk[:, kc, :], start=(kc == 0), stop=(kc == 3))
                e_pe_prev = PE.ev_last()
            k.wait(DVE, e_pe_prev)
            for oc in range(8):
                DVE.tensor_copy(att[:, oc, :], pb[oc][:, 0:SD])
            e_kev = k.mark(DVE.engine_nop(), "dve")
            k.wait(GP, e_kev)
            e_kw = k.mark(GP.dma_start(out=kv[:, :, sl], in_=att), "dG", 16)

            evw = [stream_W("Wuv", 0, 0), None]
            for kc in range(4):
                if kc + 1 < 4:
                    evw[(kc + 1) % 2] = stream_W("Wuv", kc + 1, (kc + 1) % 2)
                k.wait(PE, evw[kc % 2], k.ev("dve"))
                W = wbuf[:, kc % 2, :]
                for tc in range(NTC):
                    for hf in range(2):
                        PE.matmul(pb[2 * tc + hf][:, 0:512],
                                  cdk[:, kc, tc * 128:(tc + 1) * 128],
                                  W[:, hf * 512:(hf + 1) * 512],
                                  start=(kc == 0), stop=(kc == 3))
                e_pe_prev = PE.ev_last()
            k.wait(DVE, e_pe_prev)
            for tc in range(NTC):
                vbt = vb[:, 0, :] if tc % 2 == 0 else vb[:, 1, :]
                vb3 = vbt.rearrange("p (h n) -> p h n", n=65)
                DVE.memset(vb3[:, :, 64:65], 1.0)
                DVE.tensor_copy(vb3[:, 0:8, 0:64],
                                pb[2 * tc][:, 0:512].rearrange("p (h n) -> p h n", n=64))
                e = k.mark(DVE.tensor_copy(vb3[:, 8:16, 0:64],
                                           pb[2 * tc + 1][:, 0:512].rearrange("p (h n) -> p h n", n=64)),
                           "dve")
                k.wait(GP, e)
                e_vw = k.mark(GP.dma_start(out=vscr[s * NTC + tc, :, :], in_=vbt), "dG", 16)
                k.wait(DVE, k.ev("dG"))

            nkt = (s + 1) * NTC
            k.wait(sync, k.ev("dG"))
            for h in range(16):
                f = h // 2
                nh = (h % 2) * 64
                for kt in range(nkt):
                    slot = kt % 2
                    k.wait(sync, k.ev("pe"))
                    e_kt = k.mark(sync.dma_start(out=kb[:, slot, :, :],
                                                 in_=kv[:, :, kt * 128:(kt + 1) * 128]), "dS", 16)
                    e_vt = k.mark(sync.dma_start(out=vb[:, slot, :],
                                                 in_=vscr[kt, :, :]), "dS", 16)
                    qoff = max(0, kt * 128 - t0)
                    diag = kt * 128 >= t0
                    k.wait(PE, e_kt, e_q, k.ev("act"))
                    PE.matmul(pb[4][:, qoff:SD], kb[nh:nh + 64, slot, f, :],
                              q[nh:nh + 64, f, qoff:SD], start=True, stop=not diag,
                              tile_position=(nh, 0))
                    if diag:
                        PE.matmul(pb[4][:, qoff:qoff + 128], negI, u01,
                                  start=False, stop=True, tile_position=(0, 0))
                    e_sc = PE.ev_last()
                    k.wait(ACT, e_sc)
                    e_p = k.mark(ACT.activation(Pb[:, qoff:SD], pb[4][:, qoff:SD],
                                                AF.Exp, bias=-4.0), "act")
                    k.wait(PE, e_p, e_vt)
                    PE.matmul(pb[5][0:65, qoff:SD],
                              vb[:, slot, 65 * h:65 * h + 65],
                              Pb[:, qoff:SD], start=(kt == 0), stop=(kt == nkt - 1),
                              tile_position=(0, 0))
                    e_pv = PE.ev_last()
                k.wait(DVE, e_pv)
                e_z = k.mark(DVE.reciprocal(zrow[64:65, 0:SD], pb[5][64:65, 0:SD]), "dve")
                k.wait(PE, e_z)
                PE.matmul(pb[6][0:64, 0:SD], ones65[64:65, 0:64],
                          zrow[64:65, 0:SD], start=True, stop=True,
                          tile_position=(64, 0))
                e_zb = PE.ev_last()
                k.wait(DVE, e_zb)
                DVE.tensor_copy(zbc, pb[6][0:64, 0:SD])
                e_att = k.mark(DVE.tensor_tensor(att[nh:nh + 64, f, :],
                                                 pb[5][0:64, 0:SD], zbc,
                                                 ALU.mult), "dve")
                k.wait(PE, e_att)

            if s == 0 and "d_attn" in dbg:
                k.wait(GP, k.ev("dve"))
                dv = dbg["d_attn"].rearrange("(f p) t -> p f t", p=128)
                k.mark(GP.dma_start(out=dv[:, :, sl], in_=att), "dG", 16)
                k.wait(DVE, k.ev("dG"))
                k.wait(PE, k.ev("dG"))
            evw = [stream_W("Wof", 0, 0), None]
            for kc in range(8):
                if kc + 1 < 8:
                    evw[(kc + 1) % 2] = stream_W("Wof", kc + 1, (kc + 1) % 2)
                k.wait(PE, evw[kc % 2], k.ev("dve"))
                W = wbuf[:, kc % 2, :]
                for oc in range(8):
                    bank = pb[4 + oc // 2]
                    col = (oc % 2) * SH
                    PE.matmul(bank[:, col:col + SH], W[:, oc * 128:(oc + 1) * 128],
                              att[:, kc, 0:SH], start=(kc == 0 and oc % 2 == 0),
                              stop=(kc == 7 and oc % 2 == 1))
                for oc in range(8):
                    bank = pb[oc // 2]
                    col = (oc % 2) * SH
                    PE.matmul(bank[:, col:col + SH], W[:, oc * 128:(oc + 1) * 128],
                              att[:, kc, SH:2 * SH], start=(kc == 0 and oc % 2 == 0),
                              stop=(kc == 7 and oc % 2 == 1))
                e_pe_prev = PE.ev_last()
            k.wait(DVE, e_pe_prev)
            for oc in range(8):
                DVE.tensor_add(h0[:, oc, 0:SH], h0[:, oc, 0:SH],
                               pb[4 + oc // 2][:, (oc % 2) * SH:(oc % 2) * SH + SH])
                DVE.tensor_add(h0[:, oc, SH:2 * SH], h0[:, oc, SH:2 * SH],
                               pb[oc // 2][:, (oc % 2) * SH:(oc % 2) * SH + SH])
            e_h1 = k.mark(DVE.tensor_add(x2, x2, h0), "dve")
            k.wait(GP, e_h1)
            e_h1w = k.mark(GP.dma_start(out=h1v[:, :, sl], in_=h0), "dG", 16)
            if "d_h1" in dbg:
                dv = dbg["d_h1"].rearrange("(f p) t -> p f t", p=128)
                k.mark(GP.dma_start(out=dv[:, :, sl], in_=h0), "dG", 16)
            for tc in range(NTC):
                k.wait(PE, e_h1, k.ev("dve"))
                for oc in range(8):
                    PE.transpose(pb[4 + oc // 4][:, (oc % 4) * 128:(oc % 4 + 1) * 128],
                                 x2[:, oc, tc * 128:(tc + 1) * 128], ident)
                e = PE.ev_last()
                k.wait(DVE, e)
                for oc in range(8):
                    DVE.tensor_copy(ytm[:, oc * 128:(oc + 1) * 128],
                                    pb[4 + oc // 4][:, (oc % 4) * 128:(oc % 4 + 1) * 128])
                e = k.mark(DVE.engine_nop(), "dve")
                k.wait(GP, e)
                k.mark(GP.dma_start(out=ybv[:, s * NTC + tc, :], in_=ytm), "dG", 16)
                if "d_yb" in dbg:
                    k.mark(GP.dma_start(
                        out=dbg["d_yb"][(s * NTC + tc) * 128:(s * NTC + tc + 1) * 128, :],
                        in_=ytm), "dG", 16)
                k.wait(DVE, k.ev("dG"))
                k.wait(PE, k.ev("dG"))
    finally:
        es.close()


def phase_e(k, nc, din, dbg, sbp, pb, h1scr, ybscr, out, sync):
    T, SE = k.T, k.SE
    NSE = T // SE
    NTC = SE // 128
    PE, ACT, DVE, GP = _PERec(nc.tensor, k), nc.scalar, nc.vector, nc.gpsimd
    h1v = h1scr.rearrange("(f p) t -> p f t", p=128)
    ybv = ybscr.rearrange("(tc p) c -> p tc c", p=128)
    outv = out.rearrange("(tc p) c -> p tc c", p=128)
    onescol, onesrow = sbp["s_onescol"], sbp["s_onesrow"]
    fg2 = sbp["v_fg2"]
    gws = sbp["s_gw"]
    iotar = sbp["sc_iotar"]

    from contextlib import ExitStack
    es = ExitStack()

    def A(name, shape):
        return es.enter_context(nc.sbuf_tensor(name, list(shape), F32)).ap()

    h1 = A("e_h1", [128, 8, SE])
    t2 = A("e_t2", [128, 8, SE])
    yba = A("e_yba", [128, NTC, 1024])
    hid = A("e_hid", [128, 16, SE])
    w1b = A("e_w1b", [128, 2, 8, 256])
    w2b = A("e_w2b", [128, 2, 4, 512])
    rowm = A("e_rowm", [1, SE])
    v1c = A("e_v1c", [128, NTC])
    v2c = A("e_v2c", [128, NTC])
    i1c = A("e_i1c", [128, NTC])
    i2c = A("e_i2c", [128, NTC])
    g1c = A("e_g1c", [128, NTC])
    g2c = A("e_g2c", [128, NTC])
    eqs = A("e_eqs", [128, NTC * 8])
    wcols = A("e_wcols", [128, NTC, 8])

    try:
        for s in range(NSE):
            t0 = s * SE
            sl = slice(t0, t0 + SE)
            k.wait(ACT, k.ev("pe"), k.ev("dve"))
            k.wait(DVE, k.ev("pe"), k.ev("act"))
            k.wait(PE, k.ev("act"), k.ev("dve"))
            k.wait(sync, k.ev("dG"))
            e_h = k.mark(sync.dma_start(out=h1, in_=h1v[:, :, sl]), "dS", 16)
            e_yb = k.mark(sync.dma_start(out=yba,
                                         in_=ybv[:, s * NTC:(s + 1) * NTC, :]), "dS", 16)
            k.wait(ACT, e_h)
            e_sq = k.mark(ACT.activation(t2, h1, AF.Square), "act")
            k.wait(PE, e_sq, e_h)
            for f in range(8):
                PE.matmul(pb[6][0:1, 0:SE], onescol, t2[:, f, :], start=(f == 0),
                          stop=(f == 7))
            e = PE.ev_last()
            k.wait(DVE, e)
            e = k.mark(DVE.tensor_scalar(rowm, pb[6][0:1, 0:SE], 1.0 / C, EPS_RMS,
                                         ALU.mult, ALU.add), "dve")
            k.wait(ACT, e)
            e = k.mark(ACT.activation(rowm, rowm, AF.Sqrt), "act")
            k.wait(DVE, e)
            e = k.mark(DVE.reciprocal(rowm, rowm), "dve")
            k.wait(PE, e)
            PE.matmul(pb[6][:, 0:SE], onesrow, rowm, start=True, stop=True)
            e = PE.ev_last()
            k.wait(DVE, e)
            for f in range(8):
                DVE.scalar_tensor_tensor(t2[:, f, :], h1[:, f, :], fg2[:, f:f + 1],
                                         pb[6][:, 0:SE], ALU.mult, ALU.mult)
            e_t2 = k.mark(DVE.engine_nop(), "dve")

            k.wait(PE, e_t2)
            for tc in range(NTC):
                for kc in range(8):
                    PE.matmul(pb[7][:, tc * 8:(tc + 1) * 8],
                              t2[:, kc, tc * 128:(tc + 1) * 128],
                              gws[:, kc * 8:(kc + 1) * 8],
                              start=(kc == 0), stop=(kc == 7))
            e = PE.ev_last()
            k.wait(DVE, e)
            Ls = [pb[7][:, tc * 8:(tc + 1) * 8] for tc in range(NTC)]
            eqt = [eqs[:, tc * 8:(tc + 1) * 8] for tc in range(NTC)]
            for tc in range(NTC):
                DVE.tensor_reduce(v1c[:, tc:tc + 1], Ls[tc], AX.X, ALU.max)
            k.flush_dve(nc)
            for tc in range(NTC):
                DVE.tensor_scalar(eqt[tc], Ls[tc], v1c[:, tc:tc + 1], None, ALU.is_equal)
            k.flush_dve(nc)
            for tc in range(NTC):
                DVE.tensor_mul(eqt[tc], eqt[tc], iotar)
            k.flush_dve(nc)
            for tc in range(NTC):
                DVE.tensor_reduce(i1c[:, tc:tc + 1], eqt[tc], AX.X, ALU.max)
            k.flush_dve(nc)
            for tc in range(NTC):
                DVE.tensor_scalar(eqt[tc], iotar, i1c[:, tc:tc + 1], None, ALU.is_equal)
            k.flush_dve(nc)
            for tc in range(NTC):
                DVE.scalar_tensor_tensor(eqt[tc], eqt[tc], -1e30, Ls[tc],
                                         ALU.mult, ALU.add)
            k.flush_dve(nc)
            for tc in range(NTC):
                DVE.tensor_reduce(v2c[:, tc:tc + 1], eqt[tc], AX.X, ALU.max)
            k.flush_dve(nc)
            for tc in range(NTC):
                DVE.tensor_scalar(eqt[tc], eqt[tc], v2c[:, tc:tc + 1], None, ALU.is_equal)
            k.flush_dve(nc)
            for tc in range(NTC):
                DVE.tensor_mul(eqt[tc], eqt[tc], iotar)
            k.flush_dve(nc)
            for tc in range(NTC):
                DVE.tensor_reduce(i2c[:, tc:tc + 1], eqt[tc], AX.X, ALU.max)
            k.flush_dve(nc)
            e = k.mark(DVE.tensor_sub(g2c, v2c, v1c), "dve")
            k.wait(ACT, e)
            ACT.activation(g1c, g2c, AF.Sigmoid, scale=-1.0)
            e_gt = k.mark(ACT.activation(g2c, g2c, AF.Sigmoid), "act")
            k.wait(DVE, e_gt)
            wq = wcols.rearrange("p a b -> p (a b)")
            for tc in range(NTC):
                for e_i in range(N_EXP):
                    DVE.tensor_scalar(wq[:, tc * 8 + e_i:tc * 8 + e_i + 1],
                                      i1c[:, tc:tc + 1], float(999 - e_i), None,
                                      ALU.is_equal)
                    DVE.tensor_scalar(eqt[tc][:, e_i:e_i + 1],
                                      i2c[:, tc:tc + 1], float(999 - e_i), None,
                                      ALU.is_equal)
            k.flush_dve(nc)
            for tc in range(NTC):
                DVE.tensor_scalar(wq[:, tc * 8:(tc + 1) * 8],
                                  wq[:, tc * 8:(tc + 1) * 8], g1c[:, tc:tc + 1],
                                  None, ALU.mult)
                DVE.tensor_scalar(eqt[tc], eqt[tc], g2c[:, tc:tc + 1], None, ALU.mult)
            k.flush_dve(nc)
            for tc in range(NTC):
                DVE.tensor_add(wq[:, tc * 8:(tc + 1) * 8],
                               wq[:, tc * 8:(tc + 1) * 8], eqt[tc])
            k.flush_dve(nc)
            e_wc = k.mark(DVE.engine_nop(), "dve")

            if s == 0 and "d_yba0" in dbg:
                k.wait(GP, e_wc, e_yb)
                k.mark(GP.dma_start(out=dbg["d_yba0"][0:128, :], in_=yba[:, 0, :]), "dG", 16)
                k.mark(GP.dma_start(out=dbg["d_wc"][0:128, 0:N_EXP * NTC],
                                    in_=wcols.rearrange("p a b -> p (a b)")), "dG", 16)
                k.mark(GP.dma_start(out=dbg["d_t2"][0:128, :],
                                    in_=t2.rearrange("p a b -> p (a b)")[:, 0:1024]), "dG", 16)
                k.mark(GP.dma_start(out=dbg["d_tv"][0:128, 0:NTC],
                                    in_=v1c), "dG", 16)
                k.mark(GP.dma_start(out=dbg["d_tv"][0:128, NTC:2 * NTC],
                                    in_=v2c), "dG", 16)
                k.mark(GP.dma_start(out=dbg["d_tv"][0:128, 2 * NTC:3 * NTC],
                                    in_=i1c), "dG", 16)
                k.mark(GP.dma_start(out=dbg["d_tv"][0:128, 3 * NTC:4 * NTC],
                                    in_=i2c), "dG", 16)
                k.mark(GP.dma_start(out=dbg["d_tv"][0:128, 4 * NTC:5 * NTC],
                                    in_=g1c), "dG", 16)
                k.mark(GP.dma_start(out=dbg["d_tv"][0:128, 5 * NTC:6 * NTC],
                                    in_=g2c), "dG", 16)
                k.wait(DVE, k.ev("dG"))
            we1 = [din["we1"][e_i].rearrange("(kc p) hh -> p kc hh", p=128)
                   for e_i in range(N_EXP)]
            we2 = [din["we2"][e_i].rearrange("(kc p) cc -> p kc cc", p=128)
                   for e_i in range(N_EXP)]
            for e_i in range(N_EXP):
                e_hid = None
                for hq in range(8):
                    slot = hq % 2
                    k.wait(sync, k.ev("pe"))
                    e_w1 = k.mark(sync.dma_start(
                        out=w1b[:, slot, :, :],
                        in_=we1[e_i][:, :, hq * 256:(hq + 1) * 256]), "dS", 16)
                    k.wait(PE, e_w1, e_wc, k.ev("act"))
                    for oc2 in range(2):
                        for kc in range(8):
                            PE.matmul(pb[6 + oc2][:, 0:SE],
                                      w1b[:, slot, kc, oc2 * 128:(oc2 + 1) * 128],
                                      t2[:, kc, :],
                                      start=(kc == 0), stop=(kc == 7))
                    e = PE.ev_last()
                    k.wait(ACT, e)
                    ACT.activation(hid[:, 2 * hq, :], pb[6][:, 0:SE], AF.Silu)
                    e_hid = k.mark(ACT.activation(hid[:, 2 * hq + 1, :],
                                                  pb[7][:, 0:SE], AF.Silu), "act")
                for ch2 in range(2):
                    for kg in range(4):
                        slot = kg % 2
                        k.wait(sync, k.ev("pe"))
                        e_w2 = k.mark(sync.dma_start(
                            out=w2b[:, slot, :, :],
                            in_=we2[e_i][:, kg * 4:kg * 4 + 4,
                                         ch2 * 512:(ch2 + 1) * 512]), "dS", 16)
                        k.wait(PE, e_w2, e_hid)
                        for tc in range(NTC):
                            for kq in range(4):
                                kc2 = kg * 4 + kq
                                PE.matmul(pb[tc][:, 0:512],
                                          hid[:, kc2, tc * 128:(tc + 1) * 128],
                                          w2b[:, slot, kq, :],
                                          start=(kc2 == 0), stop=(kc2 == 15))
                        e_ye = PE.ev_last()
                    k.wait(DVE, e_ye)
                    for tc in range(NTC):
                        DVE.scalar_tensor_tensor(
                            yba[:, tc, ch2 * 512:(ch2 + 1) * 512], pb[tc][:, 0:512],
                            wcols[:, tc, e_i:e_i + 1],
                            yba[:, tc, ch2 * 512:(ch2 + 1) * 512], ALU.mult, ALU.add)
                    e_cmb = k.mark(DVE.engine_nop(), "dve")
                    k.wait(PE, e_cmb)
                if s == 0 and e_i == 0 and "d_yba1" in dbg:
                    k.wait(GP, e_cmb)
                    k.mark(GP.dma_start(out=dbg["d_yba1"][0:128, :], in_=yba[:, 0, :]), "dG", 16)
                    k.wait(DVE, k.ev("dG"))
            k.wait(GP, e_cmb, e_yb)
            k.mark(GP.dma_start(out=outv[:, s * NTC:(s + 1) * NTC, :],
                                in_=yba), "dG", 16)
            k.wait(DVE, k.ev("dG"))
    finally:
        es.close()


_BUILT = {}


def _get_program():
    key = "full"
    if key not in _BUILT:
        _BUILT[key] = build_program()
    return _BUILT[key]


def make_in_map(inputs, core, T=2048):
    g = lambda n: np.ascontiguousarray(np.asarray(inputs[n], dtype=np.float32))
    m = dict()
    m["xT"] = np.ascontiguousarray(g("L0_attn_x")[core].T)
    for nm, arr in host_consts().items():
        m[nm] = arr
    m["mix_r"] = g("L0_attn_x_r").ravel()
    m["mix_w"] = g("L0_attn_x_w").ravel()
    m["mix_k"] = g("L0_attn_x_k").ravel()
    m["mix_v"] = g("L0_attn_x_v").ravel()
    m["mix_a"] = g("L0_attn_x_a").ravel()
    m["mix_g"] = g("L0_attn_x_g").ravel()
    m["w0"] = g("L0_attn_w0").ravel()
    m["a0"] = g("L0_attn_a0").ravel()
    m["k_k"] = g("L0_attn_k_k").ravel()
    m["k_a"] = g("L0_attn_k_a").ravel()
    m["r_k"] = g("L0_attn_r_k").ravel()
    m["ln_g"] = g("L0_attn_ln_gamma").ravel()
    m["ln_b"] = g("L0_attn_ln_beta").ravel()
    m["w1"] = g("L0_attn_w1"); m["w2"] = g("L0_attn_w2")
    m["a1"] = g("L0_attn_a1"); m["a2"] = g("L0_attn_a2")
    m["g1"] = g("L0_attn_g1"); m["g2"] = g("L0_attn_g2")
    m["Wr"] = g("L0_attn_Wr"); m["Wk"] = g("L0_attn_Wk")
    m["Wv"] = g("L0_attn_Wv"); m["Wo"] = g("L0_attn_Wo")
    m["fg1"] = g("L0_ffn_g1"); m["fg2"] = g("L0_ffn_g2")
    m["Wq"] = g("L0_ffn_Wq"); m["Wdkv"] = g("L0_ffn_Wdkv")
    m["Wuk"] = g("L0_ffn_Wuk"); m["Wuv"] = g("L0_ffn_Wuv")
    m["Wof"] = g("L0_ffn_Wo"); m["gate_w"] = g("L0_ffn_gate_w")
    m["we1"] = g("L0_ffn_w1"); m["we2"] = g("L0_ffn_w2")
    return m


def kernel(**inputs):
    nc, _ = _get_program()
    in_maps = [make_in_map(inputs, c) for c in range(8)]
    res = run_bass_kernel_spmd(nc, in_maps, core_ids=list(range(8)))
    return np.stack([res.results[c]["out"] for c in range(8)], axis=0)


# version 07
# speedup vs baseline: 1.1013x; 1.0464x over previous
import numpy as np
import concourse.bass as bass
import concourse.mybir as mybir
from concourse.bass_utils import run_bass_kernel_spmd

F32 = mybir.dt.float32
AF = mybir.ActivationFunctionType
ALU = mybir.AluOpType
AX = mybir.AxisListType

C = 1024
H = 16
N = 64
LCH = 64
EPS_GN = 0.00064
EPS_LN = 1e-5
EPS_RMS = 1e-6
LATENT = 512
N_EXP = 8
EXP_DIM = 2048
SCL_W = -float(np.exp(-0.5))
CONST_VALS = [EPS_LN, EPS_GN, EPS_RMS, -4.0, 1e-12]


def host_consts():
    c = {}
    c["c_ident"] = np.eye(128, dtype=np.float32)
    c["c_i64b"] = np.tile(np.eye(64, dtype=np.float32), (2, 2))
    c["c_negI"] = (-1e9 * np.eye(128)).astype(np.float32)
    c["c_u01"] = np.tril(np.ones((128, 128), np.float32), -1)
    i_idx = (np.arange(128) % 64)[:, None]
    s_idx = np.tile(np.arange(64), 16)[None, :]
    c["c_mask_s"] = (s_idx < i_idx).astype(np.float32)
    c["c_maskT_s"] = (s_idx > i_idx).astype(np.float32)
    c["c_maskT_i"] = (s_idx >= i_idx).astype(np.float32)
    ind = np.zeros((128, 128), np.float32)
    for f in range(8):
        for p in range(128):
            ind[p, f * 16 + 2 * f + (1 if p >= 64 else 0)] = 1.0
    c["c_indgn"] = ind
    sel = np.zeros((16, 8 * 128), np.float32)
    for f in range(8):
        for m in range(128):
            sel[2 * f + (1 if m >= 64 else 0), f * 128 + m] = 1.0
    c["c_sel"] = sel
    c["c_iota8"] = np.tile(np.arange(8, dtype=np.float32)[None, :], (128, 1))
    c["c_iotar"] = 999.0 - c["c_iota8"]
    c["c_bias"] = np.tile(np.array(CONST_VALS, np.float32)[None, :], (128, 1))
    return c


_HC = host_consts()
CONST_NAMES = list(_HC.keys())

ATTN_VECS = ("mix_r", "mix_w", "mix_k", "mix_v", "mix_a", "mix_g", "w0", "a0",
             "k_k", "k_a", "r_k", "ln_g", "ln_b", "fg1", "fg2")


class K:
    def __init__(self, T=2048, SB=128, SD=512, SE=256):
        self.T, self.SB, self.SD, self.SE = T, SB, SD, SE
        assert T % SB == 0 and SB % LCH == 0 and T % SD == 0 and T % SE == 0
        self.nc = bass.Bass("TRN2", target_bir_lowering=False,
                            detect_race_conditions=False)
        self.sems = {}
        self.cnt = {}
        self.waited = {}

    def mark(self, inst, name, kk=1):
        inst.then_inc(self.sems[name], kk)
        self.cnt[name] = self.cnt.get(name, 0) + kk
        return (name, self.cnt[name])

    def ev(self, name):
        return (name, self.cnt.get(name, 0))

    def flush_dve(self, nc):
        e = self.mark(nc.vector.engine_nop(), "dve")
        self.wait(nc.scalar, e)
        e2 = self.mark(nc.scalar.nop(), "act")
        self.wait(nc.vector, e2)

    def wait(self, eng, *events):
        for event in events:
            if event is None:
                continue
            name, val = event
            if val <= 0:
                continue
            key = (id(eng), name)
            if self.waited.get(key, 0) >= val:
                continue
            eng.wait_ge(self.sems[name], val)
            self.waited[key] = val



class _PERec:

    def __init__(self, eng, k):
        self._e = eng
        self._k = k
        self.last = None
        self._last_ev = None
        self._last_ev_inst = None
        self._lastwin = None

    def ev_last(self):
        if self.last is None:
            return None
        if self._last_ev_inst is not self.last:
            self._last_ev = self._k.mark(self.last, "pe")
            self._last_ev_inst = self.last
        return self._last_ev

    def _guard(self, kbase, ksz):
        kwin = (kbase, ksz)
        if (self._lastwin is not None and ksz < 128 and self._lastwin[1] < 128
                and kwin != self._lastwin and self.last is not None):
            self._k.wait(self._e, self.ev_last())
        self._lastwin = kwin

    def matmul(self, out, lhsT, rhs, **kw):
        tp = kw.get("tile_position")
        kbase = tp[0] if tp is not None else 0
        self._guard(kbase, lhsT.shape[0])
        self.last = self._e.matmul(out, lhsT, rhs, **kw)
        return self.last

    def transpose(self, out, in_, ident, **kw):
        tp = kw.get("tile_position")
        kbase = tp[0] if tp is not None else 0
        self._guard(kbase, in_.shape[0])
        self.last = self._e.transpose(out, in_, ident, **kw)
        return self.last

    def __getattr__(self, nm):
        return getattr(self._e, nm)


def build_program(T=2048, SB=256, SD=512, SE=512, stages=("b", "d", "e"),
                  dbg_names=()):
    k = K(T=T, SB=SB, SD=SD, SE=SE)
    nc = k.nc
    ctx = nc.ctx

    din = {}

    def dram_in(name, shape):
        din[name] = nc.dram_tensor(name, list(shape), F32, kind="ExternalInput").ap()

    dram_in("xT", (C, T))
    for nm in CONST_NAMES:
        dram_in(nm, _HC[nm].shape)
    for nm, sh in [("w1", (C, 64)), ("w2", (64, C)), ("a1", (C, 64)), ("a2", (64, C)),
                   ("g1", (C, 128)), ("g2", (128, C)),
                   ("Wr", (C, C)), ("Wk", (C, C)), ("Wv", (C, C)), ("Wo", (C, C)),
                   ("Wq", (C, C)), ("Wdkv", (C, LATENT)), ("Wuk", (LATENT, C)),
                   ("Wuv", (LATENT, C)), ("Wof", (C, C)), ("gate_w", (C, N_EXP)),
                   ("we1", (N_EXP, C, EXP_DIM)), ("we2", (N_EXP, EXP_DIM, C))]:
        dram_in(nm, sh)
    for nm in ATTN_VECS:
        dram_in(nm, (C,))

    out = nc.dram_tensor("out", [T, C], F32, kind="ExternalOutput").ap()
    dbg = {}
    for nm, sh in dbg_names:
        dbg[nm] = nc.dram_tensor(nm, list(sh), F32, kind="ExternalOutput").ap()

    def scr(name, shape):
        return nc.dram_tensor(name, list(shape), F32, kind="Internal").ap()

    x2scr = scr("x2scr", (C, T))
    h1scr = scr("h1scr", (C, T))
    ybscr = scr("ybscr", (T, C))
    kscr = scr("kscr", (C, T))
    vscr = scr("vscr", (T // 128, 128, H * 65))

    for s in ("dS", "dG", "pe", "act", "dve", "gp"):
        k.sems[s] = ctx.enter_context(nc.semaphore(s))

    sbp = {}

    def S(name, shape):
        sbp[name] = ctx.enter_context(nc.sbuf_tensor(name, list(shape), F32)).ap()
        return sbp[name]

    for nm in CONST_NAMES:
        S("s" + nm, _HC[nm].shape)
    S("s_onescol", (128, 1))
    S("s_onesrow", (1, 128))
    S("s_vrow", (1, 1024))
    S("s_gamb", (128, 1024))
    S("s_betb", (128, 1024))
    for nm in ATTN_VECS:
        S("v_" + nm, (128, 8))
    S("v_ka1m", (128, 8))
    S("s_w1", (128, 8 * 64))
    S("s_w2", (64, 1024))
    S("s_a1", (128, 8 * 64))
    S("s_a2", (64, 1024))
    S("s_g1", (128, 8 * 128))
    S("s_g2", (128, 1024))
    S("s_gw", (128, 8 * 8))
    S("s_SS", (128, 8, 64))
    S("s_ln1last", (128, 8))
    S("s_e05n", (128, 1))

    for i, v in enumerate(CONST_VALS):
        nc.const_aps.register(F32, float(v), sbp["sc_bias"][:, i:i + 1])

    PSs = ctx.enter_context(nc.psum_tensor("PS", [128, 4096], F32)).ap()
    pb = [PSs[:, i * 512:(i + 1) * 512] for i in range(8)]
    pb.append(PSs)

    blk = ctx.enter_context(nc.Block())

    def body(sync):
        PE, ACT, DVE, GP = _PERec(nc.tensor, k), nc.scalar, nc.vector, nc.gpsimd
        for nm in CONST_NAMES:
            k.mark(sync.dma_start(out=sbp["s" + nm], in_=din[nm]), "dS", 16)
        for nm in ATTN_VECS:
            k.mark(sync.dma_start(out=sbp["v_" + nm],
                                  in_=din[nm].rearrange("(f p) -> p f", p=128),
                                  allow_slow_non_contiguous=True), "dS", 16)
        k.mark(sync.dma_start(out=sbp["s_w1"].rearrange("p (kc l) -> p kc l", l=64),
                              in_=din["w1"].rearrange("(kc p) l -> p kc l", p=128)), "dS", 16)
        k.mark(sync.dma_start(out=sbp["s_w2"], in_=din["w2"]), "dS", 16)
        k.mark(sync.dma_start(out=sbp["s_a1"].rearrange("p (kc l) -> p kc l", l=64),
                              in_=din["a1"].rearrange("(kc p) l -> p kc l", p=128)), "dS", 16)
        k.mark(sync.dma_start(out=sbp["s_a2"], in_=din["a2"]), "dS", 16)
        k.mark(sync.dma_start(out=sbp["s_g1"].rearrange("p (kc l) -> p kc l", l=128),
                              in_=din["g1"].rearrange("(kc p) l -> p kc l", p=128)), "dS", 16)
        k.mark(sync.dma_start(out=sbp["s_g2"], in_=din["g2"]), "dS", 16)
        ld = k.mark(sync.dma_start(out=sbp["s_gw"].rearrange("p (kc e) -> p kc e", e=8),
                                   in_=din["gate_w"].rearrange("(kc p) e -> p kc e", p=128)), "dS", 16)
        k.wait(DVE, ld)
        DVE.memset(sbp["s_onescol"], 1.0)
        DVE.memset(sbp["s_onesrow"], 1.0)
        DVE.tensor_scalar(sbp["v_ka1m"], sbp["v_k_a"], -1.0, 1.0, ALU.mult, ALU.add)
        DVE.memset(sbp["s_SS"], 0.0)
        DVE.memset(sbp["s_e05n"], -0.5)
        e = k.mark(DVE.memset(sbp["s_ln1last"], 0.0), "dve")
        k.wait(ACT, e)
        e = k.mark(ACT.activation(sbp["s_e05n"], sbp["s_e05n"], AF.Exp), "act")
        k.wait(DVE, e)
        e = k.mark(DVE.tensor_scalar_mul(sbp["s_e05n"], sbp["s_e05n"], -1.0), "dve")
        k.flush_dve(nc)
        for vec, dst in (("ln_g", "s_gamb"), ("ln_b", "s_betb")):
            k.wait(sync, k.ev("pe"))
            lv = k.mark(sync.dma_start(out=sbp["s_vrow"][0:1, :], in_=din[vec][None, :]), "dS", 16)
            k.wait(PE, lv, e)
            PE.matmul(pb[0][:, 0:512], sbp["s_onesrow"], sbp["s_vrow"][0:1, 0:512],
                      start=True, stop=True)
            PE.matmul(pb[1][:, 0:512], sbp["s_onesrow"],
                      sbp["s_vrow"][0:1, 512:1024], start=True, stop=True)
            ep = PE.ev_last()
            k.wait(DVE, ep)
            DVE.tensor_copy(sbp[dst][:, 0:512], pb[0][:, 0:512])
            e = k.mark(DVE.tensor_copy(sbp[dst][:, 512:1024], pb[1][:, 0:512]), "dve")

        if "b" in stages:
            phase_b(k, nc, din, dbg, sbp, pb, x2scr, sync)
        if "d" in stages:
            phase_d(k, nc, din, dbg, sbp, pb, x2scr, h1scr, ybscr, kscr, vscr, sync)
        if "e" in stages:
            phase_e(k, nc, din, dbg, sbp, pb, h1scr, ybscr, out, sync)

        GP.wait_ge(k.sems["dG"], k.cnt.get("dG", 0))
        sync.wait_ge(k.sems["dS"], k.cnt.get("dS", 0))
        PE.wait_ge(k.sems["dve"], k.cnt.get("dve", 0))
        PE.wait_ge(k.sems["act"], k.cnt.get("act", 0))

    @blk.sync
    def _(sync):
        body(sync)

    return nc, k


def phase_b(k, nc, din, dbg, sbp, pb, x2scr, sync):
    import os
    BSTEPS = int(os.environ.get("BSTEPS", "5"))
    T, SB = k.T, k.SB
    NSL = T // SB
    NCH = SB // LCH
    NTC = SB // 128
    PE, ACT, DVE, GP = _PERec(nc.tensor, k), nc.scalar, nc.vector, nc.gpsimd
    xT = din["xT"].rearrange("(f p) t -> p f t", p=128)
    x2v = x2scr.rearrange("(f p) t -> p f t", p=128)

    V = {nm: sbp["v_" + nm] for nm in ATTN_VECS}
    onescol, onesrow = sbp["s_onescol"], sbp["s_onesrow"]
    ident, i64b = sbp["sc_ident"], sbp["sc_i64b"]
    indgn, sel = sbp["sc_indgn"], sbp["sc_sel"]
    mask_s, maskT_s, maskT_i = sbp["sc_mask_s"], sbp["sc_maskT_s"], sbp["sc_maskT_i"]
    w1s, w2s, a1s, a2s = sbp["s_w1"], sbp["s_w2"], sbp["s_a1"], sbp["s_a2"]
    g1s, g2s = sbp["s_g1"], sbp["s_g2"]
    SS, ln1_last = sbp["s_SS"], sbp["s_ln1last"]
    gamb, betb = sbp["s_gamb"], sbp["s_betb"]
    ka1m = sbp["v_ka1m"]

    from contextlib import ExitStack
    es = ExitStack()

    def A(name, shape):
        return es.enter_context(nc.sbuf_tensor(name, list(shape), F32)).ap()

    pool1 = A("b_pool1", [128, 16 * SB])
    x = A("b_x", [128, 8, SB])
    ln = pool1[:, 0:8 * SB].rearrange("p (f t) -> p f t", t=SB)
    xx = pool1[:, 8 * SB:16 * SB].rearrange("p (f t) -> p f t", t=SB)
    mix = A("b_mix", [128, 8, SB])
    r_ = A("b_r", [128, 8, SB])
    kf = A("b_kf", [128, 8, SB])
    kkn = A("b_kkn", [128, 8, SB])
    bq = A("b_bq", [128, 8, SB])
    cc = A("b_cc", [128, 8, SB])
    vtm = A("b_vtm", [128, NTC, 1024])
    s1 = A("b_s1", [128, SB])
    sgb = A("b_sgb", [16, SB])
    wbuf = A("b_wbuf", [128, 2, 1024])
    lb = A("b_lb", [64, SB])
    smask = A("b_smask", [128, SB])
    rowm = A("b_rowm", [1, SB])
    rowr = A("b_rowr", [1, SB])
    rown = A("b_rown", [16, SB])
    E = A("b_E", [128, 8, LCH])
    Ei = A("b_Ei", [128, 8, LCH])
    Ep = A("b_Ep", [128, 8, LCH])
    EcL = A("b_EcL", [128, 8])
    rt = A("b_rt", [128, 8, LCH])
    at = A("b_at", [128, 8, LCH])
    bp_ = A("b_bp", [128, 8, LCH])
    kp = A("b_kp", [128, 8, LCH])
    ks = A("b_ks", [128, 8, LCH])
    bs = A("b_bs", [128, 8, LCH])
    Ap = A("b_Ap", [64, 1024])
    AkaT = A("b_AkaT", [64, 1024])
    ArbT = A("b_ArbT", [64, 1024])
    ArkT = A("b_ArkT", [64, 1024])
    U = A("b_U", [64, 1024])
    V2 = A("b_V2", [64, 1024])
    if SB >= 256:
        kst = pool1[0:64, 8 * SB:8 * SB + 1024]
        bst = pool1[0:64, 8 * SB + 1024:8 * SB + 2048]
    else:
        kst = A("b_kst", [64, 1024])
        bst = A("b_bst", [64, 1024])
    otm = A("b_otm", [128, NTC, 1024])
    gmu = A("b_gmu", [128, 32])[:, 0:16]
    gsc = A("b_gsc", [128, 32])[:, 0:16]
    grs = A("b_grs", [128, 32])[:, 0:16]
    sbT = A("b_sbT", [128, 32])[:, 0:16]
    ytm = pool1[:, 0:1024]
    zfm = pool1[:, 8 * SB:16 * SB].rearrange("p (f t) -> p f t", t=SB)
    if SB >= 256:
        mixflat = mix.rearrange("p a b -> p (a b)")
        Cl = [A("b_Cl%d" % i, [64, 1024]) for i in range(2)]
        Cl.append(mixflat[0:64, 0:1024])
        Cl.append(mixflat[0:64, 1024:2048])
    else:
        Cl = [A("b_Cl%d" % i, [64, 1024]) for i in range(4)]
    Cl.append(pool1[0:64, 0:1024])
    Cl.append(pool1[0:64, 1024:2048])
    try:
        sq = mix

        DVE.memset(smask, 1.0)
        e_mask = k.mark(DVE.memset(
            smask.rearrange("p (c l) -> p c l", l=LCH)[:, :, 0:1], 0.0), "dve")

        def stream_W(name, kc, slot, pe_guard="auto"):
            k.wait(sync, k.ev("pe") if pe_guard == "auto" else pe_guard)
            return k.mark(sync.dma_start(out=wbuf[:, slot, :],
                                         in_=din[name][kc * 128:(kc + 1) * 128, :]), "dS", 16)

        for s in range(NSL):
            t0 = s * SB
            sl = slice(t0, t0 + SB)
            k.wait(ACT, k.ev("pe"), k.ev("dve"))
            k.wait(DVE, k.ev("pe"), k.ev("act"))
            k.wait(PE, k.ev("act"), k.ev("dve"))
            LNS = int(os.environ.get("LNSTEP", "9"))
            e_x = k.mark(sync.dma_start(out=x, in_=xT[:, :, sl]), "dS", 16)
            if LNS < 2:
                continue
            k.wait(ACT, e_x)
            e_sq = k.mark(ACT.activation(sq, x, AF.Square), "act")
            if LNS < 3:
                continue
            k.wait(PE, e_x, e_sq, k.ev("dve"))
            for f in range(8):
                PE.matmul(pb[6][0:1, 0:SB], onescol, x[:, f, :], start=(f == 0), stop=False)
            for f in range(8):
                PE.matmul(pb[6][0:1, SB:2 * SB], onescol, sq[:, f, :],
                          start=False, stop=(f == 7))
            e = PE.ev_last()
            if LNS < 4:
                continue
            LSUB = int(os.environ.get("LNSUB", "9"))
            k.wait(DVE, e)
            DVE.tensor_scalar_mul(rowm, pb[6][0:1, 0:SB], 1.0 / C)
            e = k.mark(DVE.tensor_scalar_mul(rowr, pb[6][0:1, SB:2 * SB], 1.0 / C), "dve")
            if LSUB >= 2:
                e = k.mark(DVE.tensor_mul(pb[6][0:1, 0:SB], rowm, rowm), "dve")
            if LSUB >= 3:
                DVE.tensor_sub(rowr, rowr, pb[6][0:1, 0:SB])
                e = k.mark(DVE.tensor_scalar_max(rowr, rowr, 0.0), "dve")
            if LSUB >= 4:
                k.wait(ACT, e)
                e = k.mark(ACT.activation(rowr, rowr, AF.Sqrt, bias=EPS_LN), "act")
            if LSUB >= 5:
                k.wait(DVE, e)
                e = k.mark(DVE.reciprocal(rowr, rowr), "dve")
            if LNS < 5:
                continue
            k.wait(PE, e)
            PE.matmul(pb[6][:, 0:SB], onesrow, rowm, start=True, stop=True)
            PE.matmul(pb[7][:, 0:SB], onesrow, rowr, start=True, stop=True)
            e = PE.ev_last()
            if LNS < 6:
                continue
            k.wait(DVE, e)
            for f in range(8):
                DVE.tensor_sub(ln[:, f, :], x[:, f, :], pb[6][:, 0:SB])
            for f in range(8):
                DVE.tensor_mul(ln[:, f, :], ln[:, f, :], pb[7][:, 0:SB])
            for f in range(8):
                DVE.tensor_sub(xx[:, f, 1:SB], ln[:, f, 0:SB - 1], ln[:, f, 1:SB])
                DVE.scalar_tensor_tensor(xx[:, f, 0:1], ln[:, f, 0:1], -1.0,
                                         ln1_last[:, f:f + 1], ALU.mult, ALU.add)
            e_ln = k.mark(DVE.tensor_copy(ln1_last, ln[:, :, SB - 1]), "dve")

            if BSTEPS < 2:
                continue
            def mk_mix(mvec, pe_guard):
                k.wait(DVE, pe_guard)
                for f in range(8):
                    DVE.scalar_tensor_tensor(mix[:, f, :], xx[:, f, :],
                                             mvec[:, f:f + 1], ln[:, f, :],
                                             ALU.mult, ALU.add)
                return k.mark(DVE.engine_nop(), "dve")

            def proj_fm(wname, evmix, evac):
                evw = [stream_W(wname, 0, 0), None]
                e_pe_prev = None
                for kc in range(8):
                    if kc + 1 < 8:
                        evw[(kc + 1) % 2] = stream_W(wname, kc + 1, (kc + 1) % 2,
                                                     pe_guard=e_pe_prev)
                    k.wait(PE, evw[kc % 2], evmix)
                    W = wbuf[:, kc % 2, :]
                    for oc in range(8):
                        bank = pb[oc // 2]
                        col = (oc % 2) * SB
                        PE.matmul(bank[:, col:col + SB], W[:, oc * 128:(oc + 1) * 128],
                                  mix[:, kc, :], start=(kc == 0 and oc % 2 == 0),
                                  stop=(kc == 7 and oc % 2 == 1))
                    e_pe_prev = PE.ev_last()
                for oc in range(8):
                    evac(oc, pb[oc // 2][:, (oc % 2) * SB:(oc % 2) * SB + SB], e_pe_prev)
                return e_pe_prev

            evm = mk_mix(V["mix_r"], k.ev("pe"))

            def ev_r(oc, ap, epe):
                k.wait(DVE, epe)
                k.mark(DVE.tensor_copy(r_[:, oc, :], ap), "dve")
            e_projr = proj_fm("Wr", evm, ev_r)

            evm = mk_mix(V["mix_k"], e_projr)

            def ev_k(oc, ap, epe):
                k.wait(DVE, epe)
                k.mark(DVE.tensor_copy(kf[:, oc, :], ap), "dve")
            e_projk = proj_fm("Wk", evm, ev_k)

            evm = mk_mix(V["mix_v"], e_projk)
            evw = [stream_W("Wv", 0, 0), None]
            e_pe_prev = None
            for kc in range(8):
                if kc + 1 < 8:
                    evw[(kc + 1) % 2] = stream_W("Wv", kc + 1, (kc + 1) % 2, pe_guard=e_pe_prev)
                k.wait(PE, evw[kc % 2], evm)
                W = wbuf[:, kc % 2, :]
                for tc in range(NTC):
                    for hf in range(2):
                        PE.matmul(pb[2 * tc + hf][:, 0:512],
                                  mix[:, kc, tc * 128:(tc + 1) * 128],
                                  W[:, hf * 512:(hf + 1) * 512],
                                  start=(kc == 0), stop=(kc == 7))
                e_pe_prev = PE.ev_last()
            k.wait(DVE, e_pe_prev)
            for tc in range(NTC):
                DVE.tensor_copy(vtm[:, tc, 0:512], pb[2 * tc][:, 0:512])
                k.mark(DVE.tensor_copy(vtm[:, tc, 512:1024], pb[2 * tc + 1][:, 0:512]), "dve")

            if BSTEPS < 3:
                continue
            e_kkt = None
            k.wait(DVE, k.ev("pe"))
            for f in range(8):
                DVE.tensor_scalar_mul(kkn[:, f, :], kf[:, f, :], V["k_k"][:, f:f + 1])
            e_kkt = k.mark(DVE.engine_nop(), "dve")
            k.wait(ACT, e_kkt)
            e_sq2 = k.mark(ACT.activation(mix, kkn, AF.Square), "act")
            k.wait(PE, e_sq2)
            for f in range(8):
                PE.matmul(pb[4][0:16, 0:SB], indgn[:, f * 16:(f + 1) * 16],
                          mix[:, f, :], start=(f == 0), stop=(f == 7))
            e = PE.ev_last()
            k.wait(ACT, e)
            e = k.mark(ACT.activation(rown, pb[4][0:16, 0:SB], AF.Sqrt), "act")
            k.wait(DVE, e)
            DVE.tensor_scalar_max(rown, rown, 1e-12)
            e = k.mark(DVE.reciprocal(rown, rown), "dve")
            k.wait(PE, e)
            for f in range(8):
                k.wait(PE, k.ev("dve"))
                PE.matmul(pb[4 + f % 2][:, 0:SB], sel[:, f * 128:(f + 1) * 128],
                          rown, start=True, stop=True)
                e = PE.ev_last()
                k.wait(DVE, e)
                k.mark(DVE.tensor_mul(kkn[:, f, :], kkn[:, f, :], pb[4 + f % 2][:, 0:SB]), "dve")

            evm = mk_mix(V["mix_w"], k.ev("pe"))
            k.wait(PE, evm)
            for kc in range(8):
                PE.matmul(pb[0][0:64, 0:SB], w1s[:, kc * 64:(kc + 1) * 64],
                          mix[:, kc, :], start=(kc == 0), stop=(kc == 7))
            e = PE.ev_last()
            k.wait(ACT, e)
            e_tanh = k.mark(ACT.activation(lb, pb[0][0:64, 0:SB], AF.Tanh), "act")
            k.wait(PE, e_tanh)
            for ocp in range(4):
                bank = pb[1 + ocp]
                for j in range(2):
                    oc = 2 * ocp + j
                    PE.matmul(bank[:, j * SB:(j + 1) * SB],
                              w2s[:, oc * 128:(oc + 1) * 128], lb,
                              start=(j == 0), stop=(j == 1))
                e = PE.ev_last()
                k.wait(ACT, e)
                ACT.activation(cc[:, 2 * ocp, :], bank[:, 0:SB], AF.Sigmoid,
                               bias=V["w0"][:, 2 * ocp:2 * ocp + 1])
                e_cc = k.mark(ACT.activation(cc[:, 2 * ocp + 1, :], bank[:, SB:2 * SB],
                                             AF.Sigmoid,
                                             bias=V["w0"][:, 2 * ocp + 1:2 * ocp + 2]), "act")

            evm = mk_mix(V["mix_a"], k.ev("pe"))
            k.wait(PE, evm)
            for kc in range(8):
                PE.matmul(pb[0][0:64, 0:SB], a1s[:, kc * 64:(kc + 1) * 64],
                          mix[:, kc, :], start=(kc == 0), stop=(kc == 7))
            e = PE.ev_last()
            k.wait(DVE, e)
            e_za = k.mark(DVE.tensor_copy(lb, pb[0][0:64, 0:SB]), "dve")
            k.wait(PE, e_za)
            for ocp in range(4):
                bank = pb[1 + ocp]
                for j in range(2):
                    oc = 2 * ocp + j
                    PE.matmul(bank[:, j * SB:(j + 1) * SB],
                              a2s[:, oc * 128:(oc + 1) * 128], lb,
                              start=(j == 0), stop=(j == 1))
                e = PE.ev_last()
                k.wait(ACT, e)
                ACT.activation(bq[:, 2 * ocp, :], bank[:, 0:SB], AF.Sigmoid,
                               bias=V["a0"][:, 2 * ocp:2 * ocp + 1])
                e_ag = k.mark(ACT.activation(bq[:, 2 * ocp + 1, :], bank[:, SB:2 * SB],
                                             AF.Sigmoid,
                                             bias=V["a0"][:, 2 * ocp + 1:2 * ocp + 2]), "act")

            evm = mk_mix(V["mix_g"], k.ev("pe"))
            k.wait(PE, evm)
            for kc in range(8):
                PE.matmul(pb[0][:, 0:SB], g1s[:, kc * 128:(kc + 1) * 128],
                          mix[:, kc, :], start=(kc == 0), stop=(kc == 7))
            e_g1pe = PE.ev_last()
            k.wait(ACT, e_g1pe)
            e_s1 = k.mark(ACT.activation(s1, pb[0][:, 0:SB], AF.Sigmoid), "act")

            k.wait(DVE, e_ag, e_cc, e_g1pe)
            for f in range(8):
                DVE.tensor_scalar(mix[:, f, :], bq[:, f, :], V["k_a"][:, f:f + 1],
                                  None, ALU.mult)
                DVE.tensor_scalar_add(mix[:, f, :], mix[:, f, :], ka1m[:, f:f + 1])
            DVE.tensor_mul(kf, kf, mix)
            for f in range(8):
                DVE.tensor_scalar_mul(mix[:, f, :], r_[:, f, :], V["r_k"][:, f:f + 1])
            e = k.mark(DVE.tensor_mul(mix, mix, kf), "dve")
            k.wait(PE, e)
            k.wait(PE, k.ev("dve"))
            for f in range(8):
                PE.matmul(pb[4][0:16, 0:SB], indgn[:, f * 16:(f + 1) * 16],
                          mix[:, f, :], start=(f == 0), stop=(f == 7))
            e = PE.ev_last()
            k.wait(DVE, e)
            k.mark(DVE.tensor_copy(sgb, pb[4][0:16, 0:SB]), "dve")
            e_bq = k.mark(DVE.tensor_mul(bq, kkn, bq), "dve")
            k.wait(DVE, e_cc, e_mask)
            for f in range(8):
                DVE.tensor_tensor_scan(cc[:, f, :], smask, cc[:, f, :], 0.0,
                                       ALU.mult, ALU.add)
            for f in range(8):
                DVE.tensor_scalar_mul(cc[:, f, :], cc[:, f, :],
                                      sbp["s_e05n"][:, 0:1])
            e_scan = k.mark(DVE.engine_nop(), "dve")

            if BSTEPS < 4:
                continue
            psW0 = pb[8][0:64, 0:1024]
            psW1 = pb[8][0:64, 1024:2048]
            psO = pb[8][:, 2048:3072]
            psSa = pb[8][0:64, 3072:3584]
            psSb = pb[8][64:128, 3584:4096]

            prev_e_rs = None
            prev_e_ss = None
            for ch in range(NCH):
                tc = ch // 2
                rb = (ch % 2) * 64
                csl = slice(ch * LCH, (ch + 1) * LCH)
                k.wait(ACT, e_scan, e_bq, prev_e_rs, prev_e_ss)
                ACT.activation(E, cc[:, :, csl], AF.Exp)
                e1 = k.mark(ACT.activation(Ei, cc[:, :, csl], AF.Exp, scale=-1.0), "act")
                e2 = k.mark(ACT.activation(EcL, cc[:, :, ch * LCH + LCH - 1],
                                           AF.Exp), "act")
                k.wait(DVE, e1, e2)
                DVE.memset(Ep[:, :, 0:1], 1.0)
                DVE.tensor_copy(Ep[:, :, 1:LCH], E[:, :, 0:LCH - 1])
                DVE.tensor_mul(rt, r_[:, :, csl], E)
                DVE.scalar_tensor_tensor(at, Ep, -1.0, kkn[:, :, csl], ALU.mult, ALU.mult)
                DVE.tensor_mul(bp_, bq[:, :, csl], Ei)
                DVE.tensor_mul(kp, kf[:, :, csl], Ei)
                for f in range(8):
                    DVE.tensor_scalar_mul(ks[:, f, :], Ei[:, f, :], EcL[:, f:f + 1])
                DVE.tensor_mul(bs, bq[:, :, csl], ks)
                e_rs = k.mark(DVE.tensor_mul(ks, kf[:, :, csl], ks), "dve")
                prev_e_rs = e_rs
                CST = int(os.environ.get("CSTEP", "9"))
                if CST < 2:
                    continue

                HEADS = [(h, h // 2, (h % 2) * 64) for h in range(16)]

                k.wait(PE, k.ev("dve"))
                for cb in range(2):
                    PE.matmul(psW0[:, cb * 512:(cb + 1) * 512],
                              i64b[rb:rb + 64, 0:64],
                              vtm[rb:rb + 64, tc, cb * 512:(cb + 1) * 512],
                              start=True, stop=True, tile_position=(rb, 0))
                e = PE.ev_last()
                k.wait(DVE, e)
                e_v2 = k.mark(DVE.tensor_copy(V2, psW0), "dve")

                if CST < 3:
                    continue
                specs = [
                    (Ap, at, bp_, mask_s),
                    (Cl[0], bp_, at, maskT_s),
                    (AkaT, kp, at, maskT_s),
                    (ArbT, bp_, rt, maskT_i),
                    (ArkT, kp, rt, maskT_i),
                ]
                for idx, (dst, lh, rh, msk) in enumerate(specs):
                    bank = psW1 if idx % 2 == 0 else psW0
                    k.wait(PE, k.ev("dve"))
                    for h, f, nh in HEADS:
                        PE.matmul(bank[:, h * 64:(h + 1) * 64],
                                  lh[nh:nh + 64, f, :], rh[nh:nh + 64, f, :],
                                  start=(h % 8 == 0), stop=(h % 8 == 7),
                                  tile_position=(nh, 0))
                    e = PE.ev_last()
                    k.wait(DVE, e)
                    k.mark(DVE.tensor_mul(dst, bank, msk[0:64, :]), "dve")

                if CST < 4:
                    continue
                for lev in range(1, 6):
                    k.wait(PE, k.ev("dve"))
                    for h, f, nh in HEADS:
                        PE.matmul(psW1[:, h * 64:(h + 1) * 64],
                                  Ap[:, h * 64:(h + 1) * 64],
                                  Cl[lev - 1][:, h * 64:(h + 1) * 64],
                                  start=(h % 8 == 0), stop=(h % 8 == 7),
                                  tile_position=(0, 0))
                    e_c2 = PE.ev_last()
                    if lev < 5:
                        for h, f, nh in HEADS:
                            PE.matmul(psW0[:, h * 64:(h + 1) * 64],
                                      Cl[lev - 1][:, h * 64:(h + 1) * 64],
                                      Ap[:, h * 64:(h + 1) * 64],
                                      start=(h % 8 == 0), stop=(h % 8 == 7),
                                      tile_position=(0, 0))
                        e_a2 = PE.ev_last()
                        k.wait(DVE, e_a2)
                        DVE.tensor_copy(Ap, psW0)
                    k.wait(DVE, e_c2)
                    k.mark(DVE.tensor_copy(Cl[lev], psW1), "dve")

                if CST < 5:
                    continue
                k.wait(PE, k.ev("dve"), e_v2)
                for h, f, nh in HEADS:
                    PE.matmul(psW0[:, h * 64:(h + 1) * 64],
                              AkaT[:, h * 64:(h + 1) * 64],
                              V2[:, h * 64:(h + 1) * 64],
                              start=(h % 8 == 0), stop=False, tile_position=(0, 0))
                for h, f, nh in HEADS:
                    PE.matmul(psW0[:, h * 64:(h + 1) * 64],
                              at[nh:nh + 64, f, :], SS[nh:nh + 64, f, :],
                              start=False, stop=(h % 8 == 7), tile_position=(nh, 0))
                e = PE.ev_last()
                k.wait(DVE, e)
                e_u = k.mark(DVE.tensor_copy(U, psW0), "dve")

                for lev in (5, 4, 3, 2, 1, 0):
                    k.wait(PE, e_u)
                    for h, f, nh in HEADS:
                        PE.matmul(psW0[:, h * 64:(h + 1) * 64],
                                  Cl[lev][:, h * 64:(h + 1) * 64],
                                  U[:, h * 64:(h + 1) * 64],
                                  start=(h % 8 == 0), stop=(h % 8 == 7),
                                  tile_position=(0, 0))
                    e = PE.ev_last()
                    k.wait(DVE, e)
                    e_u = k.mark(DVE.tensor_add(U, U, psW0), "dve")

                if CST < 6:
                    continue
                k.wait(PE, e_u)
                for h, f, nh in HEADS:
                    PE.matmul(psO[rb:rb + 64, h * 64:(h + 1) * 64],
                              rt[nh:nh + 64, f, :], SS[nh:nh + 64, f, :],
                              start=(h % 8 == 0), stop=False, tile_position=(nh, rb))
                    PE.matmul(psO[rb:rb + 64, h * 64:(h + 1) * 64],
                              ArbT[:, h * 64:(h + 1) * 64],
                              U[:, h * 64:(h + 1) * 64],
                              start=False, stop=False, tile_position=(0, rb))
                    PE.matmul(psO[rb:rb + 64, h * 64:(h + 1) * 64],
                              ArkT[:, h * 64:(h + 1) * 64],
                              V2[:, h * 64:(h + 1) * 64],
                              start=False, stop=(h % 8 == 7), tile_position=(0, rb))
                e_o = PE.ev_last()

                if CST < 7:
                    continue
                for h, f, nh in HEADS:
                    PE.matmul(psW0[:, h * 64:(h + 1) * 64],
                              ks[nh:nh + 64, f, :], i64b[nh:nh + 64, 0:64],
                              start=(h % 8 == 0), stop=(h % 8 == 7),
                              tile_position=(nh, 0))
                e_k2 = PE.ev_last()
                for h, f, nh in HEADS:
                    PE.matmul(psW1[:, h * 64:(h + 1) * 64],
                              bs[nh:nh + 64, f, :], i64b[nh:nh + 64, 0:64],
                              start=(h % 8 == 0), stop=(h % 8 == 7),
                              tile_position=(nh, 0))
                e_b2 = PE.ev_last()
                k.wait(DVE, e_k2, e_b2)
                DVE.tensor_copy(kst, psW0)
                e_t = k.mark(DVE.tensor_copy(bst, psW1), "dve")

                k.wait(PE, e_t)
                for par, bankS in ((0, psSa), (1, psSb)):
                    hs = [h for h in range(16) if h % 2 == par]
                    for i_h, h in enumerate(hs):
                        f = h // 2
                        nh = par * 64
                        PE.matmul(bankS[:, f * 64:(f + 1) * 64],
                                  bst[:, h * 64:(h + 1) * 64],
                                  U[:, h * 64:(h + 1) * 64],
                                  start=(i_h == 0), stop=False,
                                  tile_position=(0, nh))
                        PE.matmul(bankS[:, f * 64:(f + 1) * 64],
                                  kst[:, h * 64:(h + 1) * 64],
                                  V2[:, h * 64:(h + 1) * 64],
                                  start=False, stop=(i_h == 7),
                                  tile_position=(0, nh))
                e_st = PE.ev_last()
                k.wait(DVE, e_st)
                for f in range(8):
                    DVE.tensor_scalar_mul(SS[:, f, :], SS[:, f, :], EcL[:, f:f + 1])
                SSf = SS.rearrange("p a b -> p (a b)")
                DVE.tensor_add(SSf[0:64, :], SSf[0:64, :], psSa)
                e_ss = k.mark(DVE.tensor_add(SSf[64:128, :], SSf[64:128, :], psSb), "dve")
                prev_e_ss = e_ss

                if CST < 8:
                    continue
                k.wait(DVE, e_o)
                k.mark(DVE.tensor_copy(otm[rb:rb + 64, tc, :], psO[rb:rb + 64, :]), "dve")
                if "d_SS" in dbg:
                    gch_ = s * NCH + ch
                    k.wait(GP, e_ss)
                    k.mark(GP.dma_start(out=dbg["d_SS"][gch_ * 128:(gch_ + 1) * 128, 0:512],
                                        in_=SS.rearrange("p a b -> p (a b)")), "dG", 16)
                    k.wait(DVE, k.ev("dG"))
                    k.wait(PE, k.ev("dG"))

                if s == 0 and ch == 0:
                    for dnm, dt_ in (("d_Ap", Ap), ("d_Cl0", Cl[0]), ("d_Cl5", Cl[5]),
                                     ("d_AkaT", AkaT), ("d_ArbT", ArbT),
                                     ("d_ArkT", ArkT), ("d_U", U), ("d_V2", V2),
                                     ("d_kst", kst), ("d_bst", bst)):
                        if dnm in dbg:
                            k.wait(GP, k.ev("dve"))
                            k.mark(GP.dma_start(out=dbg[dnm], in_=dt_), "dG", 16)
                    if "d_at" in dbg:
                        k.wait(GP, k.ev("dve"))
                        for dnm, dt_ in (("d_at", at), ("d_bp", bp_), ("d_kp", kp),
                                         ("d_rt", rt), ("d_ks", ks), ("d_bs", bs),
                                         ("d_E", E), ("d_Ei", Ei), ("d_Ep", Ep)):
                            k.mark(GP.dma_start(
                                out=dbg[dnm],
                                in_=dt_.rearrange("p a b -> p (a b)")), "dG", 16)
                    k.wait(DVE, k.ev("dG"))
                    k.wait(PE, k.ev("dG"))
                    k.wait(ACT, k.ev("dG"))

            if BSTEPS < 5:
                continue
            evW = [stream_W("Wo", 0, 0), None]
            for tc in range(NTC):
                k.wait(PE, k.ev("dve"))
                PE.transpose(pb[4][0:128, 0:16],
                             sgb[0:16, tc * 128:(tc + 1) * 128],
                             ident[0:16, 0:16])
                e = PE.ev_last()
                k.wait(DVE, e)
                e_sbt = k.mark(DVE.tensor_copy(sbT, pb[4][0:128, 0:16]), "dve")
                k.flush_dve(nc)
                if s == 0 and tc == 0 and "d_otmB" in dbg:
                    k.wait(GP, k.ev("dve"))
                    k.mark(GP.dma_start(out=dbg["d_otmB"][0:128, :], in_=otm[:, 0, :]), "dG", 16)
                    k.wait(DVE, k.ev("dG"))
                for h in range(16):
                    DVE.tensor_reduce(gmu[:, h:h + 1],
                                      otm[:, tc, h * 64:(h + 1) * 64], AX.X, ALU.add)
                k.flush_dve(nc)
                e = k.mark(DVE.tensor_scalar_mul(gmu, gmu, 1.0 / 64), "dve")
                k.flush_dve(nc)
                for h in range(16):
                    DVE.tensor_scalar(ytm[:, h * 64:(h + 1) * 64],
                                      otm[:, tc, h * 64:(h + 1) * 64],
                                      gmu[:, h:h + 1], None, ALU.subtract)
                e = k.mark(DVE.engine_nop(), "dve")
                k.wait(ACT, e)
                e = k.mark(ACT.activation(otm[:, tc, :], ytm, AF.Square), "act")
                k.wait(DVE, e)
                for h in range(16):
                    DVE.tensor_reduce(grs[:, h:h + 1],
                                      otm[:, tc, h * 64:(h + 1) * 64], AX.X, ALU.add)
                k.flush_dve(nc)
                e = k.mark(DVE.tensor_scalar_mul(grs, grs, 1.0 / 64), "dve")
                k.wait(ACT, e)
                e = k.mark(ACT.activation(grs, grs, AF.Sqrt, bias=EPS_GN), "act")
                k.wait(DVE, e)
                e = k.mark(DVE.reciprocal(grs, grs), "dve")
                k.flush_dve(nc)
                if s == 0 and "d_gmu" in dbg:
                    k.wait(GP, e)
                    k.mark(GP.dma_start(out=dbg["d_gmu"][tc * 128:(tc + 1) * 128, 0:16],
                                        in_=gmu), "dG", 16)
                    k.mark(GP.dma_start(out=dbg["d_grs"][tc * 128:(tc + 1) * 128, 0:16],
                                        in_=grs), "dG", 16)
                    k.mark(GP.dma_start(out=dbg["d_sbT"][tc * 128:(tc + 1) * 128, 0:16],
                                        in_=sbT), "dG", 16)
                    k.wait(DVE, k.ev("dG"))
                for h in range(16):
                    DVE.tensor_scalar(ytm[:, h * 64:(h + 1) * 64],
                                      ytm[:, h * 64:(h + 1) * 64],
                                      grs[:, h:h + 1], None, ALU.mult)
                if s == 0 and tc == 0 and "d_yh" in dbg:
                    e_ = k.mark(DVE.engine_nop(), "dve")
                    k.wait(GP, e_)
                    k.mark(GP.dma_start(out=dbg["d_yh"][0:128, :], in_=ytm), "dG", 16)
                    k.mark(GP.dma_start(out=dbg["d_gam"][0:128, :], in_=gamb), "dG", 16)
                    k.mark(GP.dma_start(out=dbg["d_bet"][0:128, :], in_=betb), "dG", 16)
                    k.wait(DVE, k.ev("dG"))
                DVE.tensor_mul(ytm, ytm, gamb)
                DVE.tensor_add(ytm, ytm, betb)
                k.wait(DVE, e_sbt)
                for h in range(16):
                    DVE.scalar_tensor_tensor(ytm[:, h * 64:(h + 1) * 64],
                                             vtm[:, tc, h * 64:(h + 1) * 64],
                                             sbT[:, h:h + 1],
                                             ytm[:, h * 64:(h + 1) * 64],
                                             ALU.mult, ALU.add)
                e = k.mark(DVE.engine_nop(), "dve")
                if s == 0 and "d_y" in dbg:
                    k.wait(GP, e)
                    k.mark(GP.dma_start(out=dbg["d_y"][tc * 128:(tc + 1) * 128, :],
                                        in_=ytm), "dG", 16)
                    k.wait(DVE, k.ev("dG"))
                k.wait(PE, e_s1, e)
                PE.matmul(pb[4][:, 0:512], s1[:, tc * 128:(tc + 1) * 128],
                          g2s[:, 0:512], start=True, stop=True)
                PE.matmul(pb[5][:, 0:512], s1[:, tc * 128:(tc + 1) * 128],
                          g2s[:, 512:1024], start=True, stop=True)
                e = PE.ev_last()
                k.wait(DVE, e)
                DVE.tensor_mul(ytm[:, 0:512], ytm[:, 0:512], pb[4][:, 0:512])
                e = k.mark(DVE.tensor_mul(ytm[:, 512:1024], ytm[:, 512:1024],
                                          pb[5][:, 0:512]), "dve")
                if s == 0 and "d_z" in dbg:
                    k.wait(GP, e)
                    k.mark(GP.dma_start(out=dbg["d_z"][tc * 128:(tc + 1) * 128, :],
                                        in_=ytm), "dG", 16)
                    k.wait(DVE, k.ev("dG"))
                    k.wait(PE, k.ev("dG"))
                k.wait(PE, e)
                for oc in range(8):
                    PE.transpose(pb[4 + oc // 4][:, (oc % 4) * 128:(oc % 4 + 1) * 128],
                                 ytm[:, oc * 128:(oc + 1) * 128], ident)
                e = PE.ev_last()
                k.wait(DVE, e)
                for oc in range(8):
                    k.mark(DVE.tensor_copy(zfm[:, oc, tc * 128:(tc + 1) * 128],
                                           pb[4 + oc // 4][:, (oc % 4) * 128:(oc % 4 + 1) * 128]),
                           "dve")

            e_pe_prev = None
            for kc in range(8):
                if kc + 1 < 8:
                    evW[(kc + 1) % 2] = stream_W("Wo", kc + 1, (kc + 1) % 2,
                                                 pe_guard=e_pe_prev)
                k.wait(PE, evW[kc % 2], k.ev("dve"))
                W = wbuf[:, kc % 2, :]
                for oc in range(8):
                    bank = pb[oc // 2]
                    col = (oc % 2) * SB
                    PE.matmul(bank[:, col:col + SB], W[:, oc * 128:(oc + 1) * 128],
                              zfm[:, kc, :], start=(kc == 0 and oc % 2 == 0),
                              stop=(kc == 7 and oc % 2 == 1))
                e_pe_prev = PE.ev_last()
            k.wait(DVE, e_pe_prev)
            for oc in range(8):
                DVE.tensor_add(x[:, oc, :], x[:, oc, :],
                               pb[oc // 2][:, (oc % 2) * SB:(oc % 2) * SB + SB])
            e = k.mark(DVE.engine_nop(), "dve")
            k.wait(GP, e)
            ew = k.mark(GP.dma_start(out=x2v[:, :, sl], in_=x), "dG", 16)
            if "d_x2" in dbg:
                dv = dbg["d_x2"].rearrange("(f p) t -> p f t", p=128)
                k.mark(GP.dma_start(out=dv[:, :, sl], in_=x), "dG", 16)
            k.wait(sync, k.ev("dG"))

            if "d_ln1" in dbg:
                dv = dbg["d_ln1"].rearrange("(f p) t -> p f t", p=128)
                k.mark(GP.dma_start(out=dv[:, :, sl], in_=ln), "dG", 16)
            if "d_r" in dbg:
                dv = dbg["d_r"].rearrange("(f p) t -> p f t", p=128)
                k.mark(GP.dma_start(out=dv[:, :, sl], in_=r_), "dG", 16)
            if "d_kf" in dbg:
                dv = dbg["d_kf"].rearrange("(f p) t -> p f t", p=128)
                k.mark(GP.dma_start(out=dv[:, :, sl], in_=kf), "dG", 16)
            if "d_kkn" in dbg:
                dv = dbg["d_kkn"].rearrange("(f p) t -> p f t", p=128)
                k.mark(GP.dma_start(out=dv[:, :, sl], in_=kkn), "dG", 16)
            if "d_cc" in dbg:
                dv = dbg["d_cc"].rearrange("(f p) t -> p f t", p=128)
                k.mark(GP.dma_start(out=dv[:, :, sl], in_=cc), "dG", 16)
            if "d_otm" in dbg:
                dv = dbg["d_otm"]
                for tc in range(NTC):
                    gt = s * NTC + tc
                    k.mark(GP.dma_start(
                        out=dv[gt * 128:(gt + 1) * 128, 0:1024],
                        in_=otm[:, tc, :]), "dG", 16)
    finally:
        es.close()


def phase_d(k, nc, din, dbg, sbp, pb, x2scr, h1scr, ybscr, kscr, vscr, sync):
    T, SD = k.T, k.SD
    SH = SD // 2
    NSD = T // SD
    NTC = SD // 128
    PE, ACT, DVE, GP = _PERec(nc.tensor, k), nc.scalar, nc.vector, nc.gpsimd
    x2v = x2scr.rearrange("(f p) t -> p f t", p=128)
    h1v = h1scr.rearrange("(f p) t -> p f t", p=128)
    kv = kscr.rearrange("(f p) t -> p f t", p=128)
    ybv = ybscr.rearrange("(tc p) c -> p tc c", p=128)
    onescol, onesrow = sbp["s_onescol"], sbp["s_onesrow"]
    ident = sbp["sc_ident"]
    negI, u01 = sbp["sc_negI"], sbp["sc_u01"]
    fg1 = sbp["v_fg1"]

    from contextlib import ExitStack
    es = ExitStack()

    def A(name, shape):
        return es.enter_context(nc.sbuf_tensor(name, list(shape), F32)).ap()

    x2 = A("d_x2s", [128, 8, SD])
    h0 = A("d_h0", [128, 8, SD])
    t1 = A("ds_t1", [128, 8, SD])
    cdk = A("d_cdk", [128, 4, SD])
    q = A("d_q", [128, 8, SD])
    att = A("d_att", [128, 8, SD])
    kb = A("d_kb", [128, 2, 8, 128])
    vb = A("d_vb", [128, 2, 1040])
    Pb = A("d_Pb", [128, 512])
    wbuf = A("d_wbuf", [128, 2, 1024])
    rowm = A("d_rowm", [1, SD])
    rowr = A("d_rowr", [1, SD])
    zrow = A("d_zrow", [65, SD])
    zbc = A("d_zbc", [64, SD])
    ones65 = A("d_ones65", [65, 64])
    ytm = A("d_ytm", [128, 1024])

    try:
        e = k.mark(DVE.memset(ones65, 1.0), "dve")
        for s in range(NSD):
            t0 = s * SD
            sl = slice(t0, t0 + SD)
            k.wait(ACT, k.ev("pe"), k.ev("dve"))
            k.wait(DVE, k.ev("pe"), k.ev("act"))
            k.wait(PE, k.ev("act"), k.ev("dve"))
            k.wait(sync, k.ev("dG"))
            e_x = k.mark(sync.dma_start(out=x2, in_=x2v[:, :, sl]), "dS", 16)
            k.wait(ACT, e_x)
            e_sq = k.mark(ACT.activation(h0, x2, AF.Square), "act")
            k.wait(PE, e_x, e_sq)
            for f in range(8):
                PE.matmul(pb[6][0:1, 0:SD], onescol, x2[:, f, :], start=(f == 0),
                          stop=(f == 7))
            for f in range(8):
                PE.matmul(pb[7][0:1, 0:SD], onescol, h0[:, f, :], start=(f == 0),
                          stop=(f == 7))
            e = PE.ev_last()
            k.wait(DVE, e)
            DVE.tensor_scalar_mul(rowm, pb[6][0:1, 0:SD], 1.0 / C)
            DVE.tensor_scalar_mul(rowr, pb[7][0:1, 0:SD], 1.0 / C)
            DVE.tensor_mul(pb[6][0:1, 0:SD], rowm, rowm)
            DVE.tensor_sub(rowr, rowr, pb[6][0:1, 0:SD])
            e = k.mark(DVE.tensor_scalar_max(rowr, rowr, 0.0), "dve")
            k.wait(ACT, e)
            e = k.mark(ACT.activation(rowr, rowr, AF.Sqrt, bias=EPS_LN), "act")
            k.wait(DVE, e)
            e = k.mark(DVE.reciprocal(rowr, rowr), "dve")
            k.wait(PE, e)
            PE.matmul(pb[6][:, 0:SD], onesrow, rowm, start=True, stop=True)
            PE.matmul(pb[7][:, 0:SD], onesrow, rowr, start=True, stop=True)
            e = PE.ev_last()
            k.wait(DVE, e)
            for f in range(8):
                DVE.tensor_sub(h0[:, f, :], x2[:, f, :], pb[6][:, 0:SD])
            for f in range(8):
                DVE.tensor_mul(h0[:, f, :], h0[:, f, :], pb[7][:, 0:SD])
            e_h0 = k.mark(DVE.engine_nop(), "dve")
            k.wait(ACT, e_h0)
            e_sq = k.mark(ACT.activation(t1, h0, AF.Square), "act")
            k.wait(PE, e_sq)
            for f in range(8):
                PE.matmul(pb[6][0:1, 0:SD], onescol, t1[:, f, :], start=(f == 0),
                          stop=(f == 7))
            e = PE.ev_last()
            k.wait(DVE, e)
            e = k.mark(DVE.tensor_scalar(rowr, pb[6][0:1, 0:SD], 1.0 / C, EPS_RMS,
                                         ALU.mult, ALU.add), "dve")
            k.wait(ACT, e)
            e = k.mark(ACT.activation(rowr, rowr, AF.Sqrt), "act")
            k.wait(DVE, e)
            e = k.mark(DVE.reciprocal(rowr, rowr), "dve")
            k.wait(PE, e)
            PE.matmul(pb[6][:, 0:SD], onesrow, rowr, start=True, stop=True)
            e = PE.ev_last()
            k.wait(DVE, e)
            for f in range(8):
                DVE.scalar_tensor_tensor(t1[:, f, :], h0[:, f, :], fg1[:, f:f + 1],
                                         pb[6][:, 0:SD], ALU.mult, ALU.mult)
            e_t1 = k.mark(DVE.engine_nop(), "dve")

            def stream_W(name, kc, slot):
                ncols = din[name].shape[1]
                k.wait(sync, k.ev("pe"))
                return k.mark(sync.dma_start(out=wbuf[:, slot, 0:ncols],
                                             in_=din[name][kc * 128:(kc + 1) * 128, :]),
                              "dS", 16)

            evw = [stream_W("Wdkv", 0, 0), None]
            e_pe_prev = None
            for kc in range(8):
                if kc + 1 < 8:
                    evw[(kc + 1) % 2] = stream_W("Wdkv", kc + 1, (kc + 1) % 2)
                k.wait(PE, evw[kc % 2], e_t1)
                W = wbuf[:, kc % 2, 0:512]
                for oc in range(4):
                    PE.matmul(pb[oc][:, 0:SD], W[:, oc * 128:(oc + 1) * 128],
                              t1[:, kc, :], start=(kc == 0), stop=(kc == 7))
                e_pe_prev = PE.ev_last()
            k.wait(DVE, e_pe_prev)
            for oc in range(4):
                k.mark(DVE.tensor_copy(cdk[:, oc, :], pb[oc][:, 0:SD]), "dve")
            evw = [stream_W("Wq", 0, 0), None]
            for kc in range(8):
                if kc + 1 < 8:
                    evw[(kc + 1) % 2] = stream_W("Wq", kc + 1, (kc + 1) % 2)
                k.wait(PE, evw[kc % 2], k.ev("dve"))
                W = wbuf[:, kc % 2, :]
                for oc in range(8):
                    PE.matmul(pb[oc][:, 0:SD],
                              W[:, oc * 128:(oc + 1) * 128],
                              t1[:, kc, :], start=(kc == 0), stop=(kc == 7))
                e_pe_prev = PE.ev_last()
            k.wait(ACT, e_pe_prev)
            for oc in range(8):
                ACT.activation(q[:, oc, :], pb[oc][:, 0:SD],
                               AF.Identity, scale=0.125)
            e_q = k.mark(ACT.nop(), "act")

            if s == 0 and "d_t1" in dbg:
                k.wait(GP, e_t1, e_q)
                dv = dbg["d_t1"].rearrange("(f p) t -> p f t", p=128)
                k.mark(GP.dma_start(out=dv[:, :, sl], in_=t1), "dG", 16)
                dv = dbg["d_qd"].rearrange("(f p) t -> p f t", p=128)
                k.mark(GP.dma_start(out=dv[:, :, sl], in_=q), "dG", 16)
                k.wait(DVE, k.ev("dG"))
                k.wait(PE, k.ev("dG"))
            evw = [stream_W("Wuk", 0, 0), None]
            for kc in range(4):
                if kc + 1 < 4:
                    evw[(kc + 1) % 2] = stream_W("Wuk", kc + 1, (kc + 1) % 2)
                k.wait(PE, evw[kc % 2], k.ev("act"))
                W = wbuf[:, kc % 2, :]
                for oc in range(8):
                    PE.matmul(pb[oc][:, 0:SD], W[:, oc * 128:(oc + 1) * 128],
                              cdk[:, kc, :], start=(kc == 0), stop=(kc == 3))
                e_pe_prev = PE.ev_last()
            k.wait(DVE, e_pe_prev)
            for oc in range(8):
                DVE.tensor_copy(att[:, oc, :], pb[oc][:, 0:SD])
            e_kev = k.mark(DVE.engine_nop(), "dve")
            k.wait(GP, e_kev)
            e_kw = k.mark(GP.dma_start(out=kv[:, :, sl], in_=att), "dG", 16)

            evw = [stream_W("Wuv", 0, 0), None]
            for kc in range(4):
                if kc + 1 < 4:
                    evw[(kc + 1) % 2] = stream_W("Wuv", kc + 1, (kc + 1) % 2)
                k.wait(PE, evw[kc % 2], k.ev("dve"))
                W = wbuf[:, kc % 2, :]
                for tc in range(NTC):
                    for hf in range(2):
                        PE.matmul(pb[2 * tc + hf][:, 0:512],
                                  cdk[:, kc, tc * 128:(tc + 1) * 128],
                                  W[:, hf * 512:(hf + 1) * 512],
                                  start=(kc == 0), stop=(kc == 3))
                e_pe_prev = PE.ev_last()
            k.wait(DVE, e_pe_prev)
            for tc in range(NTC):
                vbt = vb[:, 0, :] if tc % 2 == 0 else vb[:, 1, :]
                vb3 = vbt.rearrange("p (h n) -> p h n", n=65)
                DVE.memset(vb3[:, :, 64:65], 1.0)
                DVE.tensor_copy(vb3[:, 0:8, 0:64],
                                pb[2 * tc][:, 0:512].rearrange("p (h n) -> p h n", n=64))
                e = k.mark(DVE.tensor_copy(vb3[:, 8:16, 0:64],
                                           pb[2 * tc + 1][:, 0:512].rearrange("p (h n) -> p h n", n=64)),
                           "dve")
                k.wait(GP, e)
                e_vw = k.mark(GP.dma_start(out=vscr[s * NTC + tc, :, :], in_=vbt), "dG", 16)
                k.wait(DVE, k.ev("dG"))

            nkt = (s + 1) * NTC
            k.wait(sync, k.ev("dG"))
            for h in range(16):
                f = h // 2
                nh = (h % 2) * 64
                for kt in range(nkt):
                    slot = kt % 2
                    k.wait(sync, k.ev("pe"))
                    e_kt = k.mark(sync.dma_start(out=kb[:, slot, :, :],
                                                 in_=kv[:, :, kt * 128:(kt + 1) * 128]), "dS", 16)
                    e_vt = k.mark(sync.dma_start(out=vb[:, slot, :],
                                                 in_=vscr[kt, :, :]), "dS", 16)
                    qoff = max(0, kt * 128 - t0)
                    diag = kt * 128 >= t0
                    k.wait(PE, e_kt, e_q, k.ev("act"))
                    PE.matmul(pb[4][:, qoff:SD], kb[nh:nh + 64, slot, f, :],
                              q[nh:nh + 64, f, qoff:SD], start=True, stop=not diag,
                              tile_position=(nh, 0))
                    if diag:
                        PE.matmul(pb[4][:, qoff:qoff + 128], negI, u01,
                                  start=False, stop=True, tile_position=(0, 0))
                    e_sc = PE.ev_last()
                    k.wait(ACT, e_sc)
                    e_p = k.mark(ACT.activation(Pb[:, qoff:SD], pb[4][:, qoff:SD],
                                                AF.Exp, bias=-4.0), "act")
                    k.wait(PE, e_p, e_vt)
                    PE.matmul(pb[5][0:65, qoff:SD],
                              vb[:, slot, 65 * h:65 * h + 65],
                              Pb[:, qoff:SD], start=(kt == 0), stop=(kt == nkt - 1),
                              tile_position=(0, 0))
                    e_pv = PE.ev_last()
                k.wait(DVE, e_pv)
                e_z = k.mark(DVE.reciprocal(zrow[64:65, 0:SD], pb[5][64:65, 0:SD]), "dve")
                k.wait(PE, e_z)
                PE.matmul(pb[6][0:64, 0:SD], ones65[64:65, 0:64],
                          zrow[64:65, 0:SD], start=True, stop=True,
                          tile_position=(64, 0))
                e_zb = PE.ev_last()
                k.wait(DVE, e_zb)
                DVE.tensor_copy(zbc, pb[6][0:64, 0:SD])
                e_att = k.mark(DVE.tensor_tensor(att[nh:nh + 64, f, :],
                                                 pb[5][0:64, 0:SD], zbc,
                                                 ALU.mult), "dve")
                k.wait(PE, e_att)

            if s == 0 and "d_attn" in dbg:
                k.wait(GP, k.ev("dve"))
                dv = dbg["d_attn"].rearrange("(f p) t -> p f t", p=128)
                k.mark(GP.dma_start(out=dv[:, :, sl], in_=att), "dG", 16)
                k.wait(DVE, k.ev("dG"))
                k.wait(PE, k.ev("dG"))
            evw = [stream_W("Wof", 0, 0), None]
            for kc in range(8):
                if kc + 1 < 8:
                    evw[(kc + 1) % 2] = stream_W("Wof", kc + 1, (kc + 1) % 2)
                k.wait(PE, evw[kc % 2], k.ev("dve"))
                W = wbuf[:, kc % 2, :]
                for oc in range(8):
                    bank = pb[4 + oc // 2]
                    col = (oc % 2) * SH
                    PE.matmul(bank[:, col:col + SH], W[:, oc * 128:(oc + 1) * 128],
                              att[:, kc, 0:SH], start=(kc == 0 and oc % 2 == 0),
                              stop=(kc == 7 and oc % 2 == 1))
                for oc in range(8):
                    bank = pb[oc // 2]
                    col = (oc % 2) * SH
                    PE.matmul(bank[:, col:col + SH], W[:, oc * 128:(oc + 1) * 128],
                              att[:, kc, SH:2 * SH], start=(kc == 0 and oc % 2 == 0),
                              stop=(kc == 7 and oc % 2 == 1))
                e_pe_prev = PE.ev_last()
            k.wait(DVE, e_pe_prev)
            for oc in range(8):
                DVE.tensor_add(h0[:, oc, 0:SH], h0[:, oc, 0:SH],
                               pb[4 + oc // 2][:, (oc % 2) * SH:(oc % 2) * SH + SH])
                DVE.tensor_add(h0[:, oc, SH:2 * SH], h0[:, oc, SH:2 * SH],
                               pb[oc // 2][:, (oc % 2) * SH:(oc % 2) * SH + SH])
            e_h1 = k.mark(DVE.tensor_add(x2, x2, h0), "dve")
            k.wait(GP, e_h1)
            e_h1w = k.mark(GP.dma_start(out=h1v[:, :, sl], in_=h0), "dG", 16)
            if "d_h1" in dbg:
                dv = dbg["d_h1"].rearrange("(f p) t -> p f t", p=128)
                k.mark(GP.dma_start(out=dv[:, :, sl], in_=h0), "dG", 16)
            for tc in range(NTC):
                k.wait(PE, e_h1, k.ev("dve"))
                for oc in range(8):
                    PE.transpose(pb[4 + oc // 4][:, (oc % 4) * 128:(oc % 4 + 1) * 128],
                                 x2[:, oc, tc * 128:(tc + 1) * 128], ident)
                e = PE.ev_last()
                k.wait(DVE, e)
                for oc in range(8):
                    DVE.tensor_copy(ytm[:, oc * 128:(oc + 1) * 128],
                                    pb[4 + oc // 4][:, (oc % 4) * 128:(oc % 4 + 1) * 128])
                e = k.mark(DVE.engine_nop(), "dve")
                k.wait(GP, e)
                k.mark(GP.dma_start(out=ybv[:, s * NTC + tc, :], in_=ytm), "dG", 16)
                if "d_yb" in dbg:
                    k.mark(GP.dma_start(
                        out=dbg["d_yb"][(s * NTC + tc) * 128:(s * NTC + tc + 1) * 128, :],
                        in_=ytm), "dG", 16)
                k.wait(DVE, k.ev("dG"))
                k.wait(PE, k.ev("dG"))
    finally:
        es.close()


def phase_e(k, nc, din, dbg, sbp, pb, h1scr, ybscr, out, sync):
    T, SE = k.T, k.SE
    NSE = T // SE
    NTC = SE // 128
    PE, ACT, DVE, GP = _PERec(nc.tensor, k), nc.scalar, nc.vector, nc.gpsimd
    h1v = h1scr.rearrange("(f p) t -> p f t", p=128)
    ybv = ybscr.rearrange("(tc p) c -> p tc c", p=128)
    outv = out.rearrange("(tc p) c -> p tc c", p=128)
    onescol, onesrow = sbp["s_onescol"], sbp["s_onesrow"]
    fg2 = sbp["v_fg2"]
    gws = sbp["s_gw"]
    iotar = sbp["sc_iotar"]

    from contextlib import ExitStack
    es = ExitStack()

    def A(name, shape):
        return es.enter_context(nc.sbuf_tensor(name, list(shape), F32)).ap()

    h1 = A("e_h1", [128, 8, SE])
    t2 = A("e_t2", [128, 8, SE])
    yba = A("e_yba", [128, NTC, 1024])
    hid = A("e_hid", [128, 16, SE])
    w1b = A("e_w1b", [128, 2, 8, 256])
    w2b = A("e_w2b", [128, 2, 4, 512])
    rowm = A("e_rowm", [1, SE])
    v1c = A("e_v1c", [128, NTC])
    v2c = A("e_v2c", [128, NTC])
    i1c = A("e_i1c", [128, NTC])
    i2c = A("e_i2c", [128, NTC])
    g1c = A("e_g1c", [128, NTC])
    g2c = A("e_g2c", [128, NTC])
    eqs = A("e_eqs", [128, NTC * 8])
    wcols = A("e_wcols", [128, NTC, 8])

    try:
        for s in range(NSE):
            t0 = s * SE
            sl = slice(t0, t0 + SE)
            k.wait(ACT, k.ev("pe"), k.ev("dve"))
            k.wait(DVE, k.ev("pe"), k.ev("act"))
            k.wait(PE, k.ev("act"), k.ev("dve"))
            k.wait(sync, k.ev("dG"))
            e_h = k.mark(sync.dma_start(out=h1, in_=h1v[:, :, sl]), "dS", 16)
            e_yb = k.mark(sync.dma_start(out=yba,
                                         in_=ybv[:, s * NTC:(s + 1) * NTC, :]), "dS", 16)
            k.wait(ACT, e_h)
            e_sq = k.mark(ACT.activation(t2, h1, AF.Square), "act")
            k.wait(PE, e_sq, e_h)
            for f in range(8):
                PE.matmul(pb[6][0:1, 0:SE], onescol, t2[:, f, :], start=(f == 0),
                          stop=(f == 7))
            e = PE.ev_last()
            k.wait(DVE, e)
            e = k.mark(DVE.tensor_scalar(rowm, pb[6][0:1, 0:SE], 1.0 / C, EPS_RMS,
                                         ALU.mult, ALU.add), "dve")
            k.wait(ACT, e)
            e = k.mark(ACT.activation(rowm, rowm, AF.Sqrt), "act")
            k.wait(DVE, e)
            e = k.mark(DVE.reciprocal(rowm, rowm), "dve")
            k.wait(PE, e)
            PE.matmul(pb[6][:, 0:SE], onesrow, rowm, start=True, stop=True)
            e = PE.ev_last()
            k.wait(DVE, e)
            for f in range(8):
                DVE.scalar_tensor_tensor(t2[:, f, :], h1[:, f, :], fg2[:, f:f + 1],
                                         pb[6][:, 0:SE], ALU.mult, ALU.mult)
            e_t2 = k.mark(DVE.engine_nop(), "dve")

            k.wait(PE, e_t2)
            for tc in range(NTC):
                for kc in range(8):
                    PE.matmul(pb[7][:, tc * 8:(tc + 1) * 8],
                              t2[:, kc, tc * 128:(tc + 1) * 128],
                              gws[:, kc * 8:(kc + 1) * 8],
                              start=(kc == 0), stop=(kc == 7))
            e = PE.ev_last()
            k.wait(DVE, e)
            Ls = [pb[7][:, tc * 8:(tc + 1) * 8] for tc in range(NTC)]
            eqt = [eqs[:, tc * 8:(tc + 1) * 8] for tc in range(NTC)]
            for tc in range(NTC):
                DVE.tensor_reduce(v1c[:, tc:tc + 1], Ls[tc], AX.X, ALU.max)
            k.flush_dve(nc)
            for tc in range(NTC):
                DVE.tensor_scalar(eqt[tc], Ls[tc], v1c[:, tc:tc + 1], None, ALU.is_equal)
            k.flush_dve(nc)
            for tc in range(NTC):
                DVE.tensor_mul(eqt[tc], eqt[tc], iotar)
            k.flush_dve(nc)
            for tc in range(NTC):
                DVE.tensor_reduce(i1c[:, tc:tc + 1], eqt[tc], AX.X, ALU.max)
            k.flush_dve(nc)
            for tc in range(NTC):
                DVE.tensor_scalar(eqt[tc], iotar, i1c[:, tc:tc + 1], None, ALU.is_equal)
            k.flush_dve(nc)
            for tc in range(NTC):
                DVE.scalar_tensor_tensor(eqt[tc], eqt[tc], -1e30, Ls[tc],
                                         ALU.mult, ALU.add)
            k.flush_dve(nc)
            for tc in range(NTC):
                DVE.tensor_reduce(v2c[:, tc:tc + 1], eqt[tc], AX.X, ALU.max)
            k.flush_dve(nc)
            for tc in range(NTC):
                DVE.tensor_scalar(eqt[tc], eqt[tc], v2c[:, tc:tc + 1], None, ALU.is_equal)
            k.flush_dve(nc)
            for tc in range(NTC):
                DVE.tensor_mul(eqt[tc], eqt[tc], iotar)
            k.flush_dve(nc)
            for tc in range(NTC):
                DVE.tensor_reduce(i2c[:, tc:tc + 1], eqt[tc], AX.X, ALU.max)
            k.flush_dve(nc)
            e = k.mark(DVE.tensor_sub(g2c, v2c, v1c), "dve")
            k.wait(ACT, e)
            ACT.activation(g1c, g2c, AF.Sigmoid, scale=-1.0)
            e_gt = k.mark(ACT.activation(g2c, g2c, AF.Sigmoid), "act")
            k.wait(DVE, e_gt)
            wq = wcols.rearrange("p a b -> p (a b)")
            for tc in range(NTC):
                for e_i in range(N_EXP):
                    DVE.tensor_scalar(wq[:, tc * 8 + e_i:tc * 8 + e_i + 1],
                                      i1c[:, tc:tc + 1], float(999 - e_i), None,
                                      ALU.is_equal)
                    DVE.tensor_scalar(eqt[tc][:, e_i:e_i + 1],
                                      i2c[:, tc:tc + 1], float(999 - e_i), None,
                                      ALU.is_equal)
            k.flush_dve(nc)
            for tc in range(NTC):
                DVE.tensor_scalar(wq[:, tc * 8:(tc + 1) * 8],
                                  wq[:, tc * 8:(tc + 1) * 8], g1c[:, tc:tc + 1],
                                  None, ALU.mult)
                DVE.tensor_scalar(eqt[tc], eqt[tc], g2c[:, tc:tc + 1], None, ALU.mult)
            k.flush_dve(nc)
            for tc in range(NTC):
                DVE.tensor_add(wq[:, tc * 8:(tc + 1) * 8],
                               wq[:, tc * 8:(tc + 1) * 8], eqt[tc])
            k.flush_dve(nc)
            e_wc = k.mark(DVE.engine_nop(), "dve")

            if s == 0 and "d_yba0" in dbg:
                k.wait(GP, e_wc, e_yb)
                k.mark(GP.dma_start(out=dbg["d_yba0"][0:128, :], in_=yba[:, 0, :]), "dG", 16)
                k.mark(GP.dma_start(out=dbg["d_wc"][0:128, 0:N_EXP * NTC],
                                    in_=wcols.rearrange("p a b -> p (a b)")), "dG", 16)
                k.mark(GP.dma_start(out=dbg["d_t2"][0:128, :],
                                    in_=t2.rearrange("p a b -> p (a b)")[:, 0:1024]), "dG", 16)
                k.mark(GP.dma_start(out=dbg["d_tv"][0:128, 0:NTC],
                                    in_=v1c), "dG", 16)
                k.mark(GP.dma_start(out=dbg["d_tv"][0:128, NTC:2 * NTC],
                                    in_=v2c), "dG", 16)
                k.mark(GP.dma_start(out=dbg["d_tv"][0:128, 2 * NTC:3 * NTC],
                                    in_=i1c), "dG", 16)
                k.mark(GP.dma_start(out=dbg["d_tv"][0:128, 3 * NTC:4 * NTC],
                                    in_=i2c), "dG", 16)
                k.mark(GP.dma_start(out=dbg["d_tv"][0:128, 4 * NTC:5 * NTC],
                                    in_=g1c), "dG", 16)
                k.mark(GP.dma_start(out=dbg["d_tv"][0:128, 5 * NTC:6 * NTC],
                                    in_=g2c), "dG", 16)
                k.wait(DVE, k.ev("dG"))
            we1 = [din["we1"][e_i].rearrange("(kc p) hh -> p kc hh", p=128)
                   for e_i in range(N_EXP)]
            we2 = [din["we2"][e_i].rearrange("(kc p) cc -> p kc cc", p=128)
                   for e_i in range(N_EXP)]
            for e_i in range(N_EXP):
                e_hid = None
                for hq in range(8):
                    slot = hq % 2
                    k.wait(sync, k.ev("pe"))
                    e_w1 = k.mark(sync.dma_start(
                        out=w1b[:, slot, :, :],
                        in_=we1[e_i][:, :, hq * 256:(hq + 1) * 256]), "dS", 16)
                    k.wait(PE, e_w1, e_wc, k.ev("act"))
                    for oc2 in range(2):
                        for kc in range(8):
                            PE.matmul(pb[6 + oc2][:, 0:SE],
                                      w1b[:, slot, kc, oc2 * 128:(oc2 + 1) * 128],
                                      t2[:, kc, :],
                                      start=(kc == 0), stop=(kc == 7))
                    e = PE.ev_last()
                    k.wait(ACT, e)
                    ACT.activation(hid[:, 2 * hq, :], pb[6][:, 0:SE], AF.Silu)
                    e_hid = k.mark(ACT.activation(hid[:, 2 * hq + 1, :],
                                                  pb[7][:, 0:SE], AF.Silu), "act")
                for ch2 in range(2):
                    for kg in range(4):
                        slot = kg % 2
                        k.wait(sync, k.ev("pe"))
                        e_w2 = k.mark(sync.dma_start(
                            out=w2b[:, slot, :, :],
                            in_=we2[e_i][:, kg * 4:kg * 4 + 4,
                                         ch2 * 512:(ch2 + 1) * 512]), "dS", 16)
                        k.wait(PE, e_w2, e_hid)
                        for tc in range(NTC):
                            for kq in range(4):
                                kc2 = kg * 4 + kq
                                PE.matmul(pb[tc][:, 0:512],
                                          hid[:, kc2, tc * 128:(tc + 1) * 128],
                                          w2b[:, slot, kq, :],
                                          start=(kc2 == 0), stop=(kc2 == 15))
                        e_ye = PE.ev_last()
                    k.wait(DVE, e_ye)
                    for tc in range(NTC):
                        DVE.scalar_tensor_tensor(
                            yba[:, tc, ch2 * 512:(ch2 + 1) * 512], pb[tc][:, 0:512],
                            wcols[:, tc, e_i:e_i + 1],
                            yba[:, tc, ch2 * 512:(ch2 + 1) * 512], ALU.mult, ALU.add)
                    e_cmb = k.mark(DVE.engine_nop(), "dve")
                    k.wait(PE, e_cmb)
                if s == 0 and e_i == 0 and "d_yba1" in dbg:
                    k.wait(GP, e_cmb)
                    k.mark(GP.dma_start(out=dbg["d_yba1"][0:128, :], in_=yba[:, 0, :]), "dG", 16)
                    k.wait(DVE, k.ev("dG"))
            k.wait(GP, e_cmb, e_yb)
            k.mark(GP.dma_start(out=outv[:, s * NTC:(s + 1) * NTC, :],
                                in_=yba), "dG", 16)
            k.wait(DVE, k.ev("dG"))
    finally:
        es.close()


_BUILT = {}


def _get_program():
    key = "full"
    if key not in _BUILT:
        _BUILT[key] = build_program()
    return _BUILT[key]


def make_in_map(inputs, core, T=2048):
    g = lambda n: np.ascontiguousarray(np.asarray(inputs[n], dtype=np.float32))
    m = dict()
    m["xT"] = np.ascontiguousarray(g("L0_attn_x")[core].T)
    for nm, arr in host_consts().items():
        m[nm] = arr
    m["mix_r"] = g("L0_attn_x_r").ravel()
    m["mix_w"] = g("L0_attn_x_w").ravel()
    m["mix_k"] = g("L0_attn_x_k").ravel()
    m["mix_v"] = g("L0_attn_x_v").ravel()
    m["mix_a"] = g("L0_attn_x_a").ravel()
    m["mix_g"] = g("L0_attn_x_g").ravel()
    m["w0"] = g("L0_attn_w0").ravel()
    m["a0"] = g("L0_attn_a0").ravel()
    m["k_k"] = g("L0_attn_k_k").ravel()
    m["k_a"] = g("L0_attn_k_a").ravel()
    m["r_k"] = g("L0_attn_r_k").ravel()
    m["ln_g"] = g("L0_attn_ln_gamma").ravel()
    m["ln_b"] = g("L0_attn_ln_beta").ravel()
    m["w1"] = g("L0_attn_w1"); m["w2"] = g("L0_attn_w2")
    m["a1"] = g("L0_attn_a1"); m["a2"] = g("L0_attn_a2")
    m["g1"] = g("L0_attn_g1"); m["g2"] = g("L0_attn_g2")
    m["Wr"] = g("L0_attn_Wr"); m["Wk"] = g("L0_attn_Wk")
    m["Wv"] = g("L0_attn_Wv"); m["Wo"] = g("L0_attn_Wo")
    m["fg1"] = g("L0_ffn_g1"); m["fg2"] = g("L0_ffn_g2")
    m["Wq"] = g("L0_ffn_Wq"); m["Wdkv"] = g("L0_ffn_Wdkv")
    m["Wuk"] = g("L0_ffn_Wuk"); m["Wuv"] = g("L0_ffn_Wuv")
    m["Wof"] = g("L0_ffn_Wo"); m["gate_w"] = g("L0_ffn_gate_w")
    m["we1"] = g("L0_ffn_w1"); m["we2"] = g("L0_ffn_w2")
    return m


def kernel(**inputs):
    nc, _ = _get_program()
    in_maps = [make_in_map(inputs, c) for c in range(8)]
    res = run_bass_kernel_spmd(nc, in_maps, core_ids=list(range(8)))
    return np.stack([res.results[c]["out"] for c in range(8)], axis=0)


# version 09 (final)
# speedup vs baseline: 1.3952x; 1.2668x over previous
import numpy as np
import concourse.bass as bass
import concourse.mybir as mybir
from concourse.bass_utils import run_bass_kernel_spmd

F32 = mybir.dt.float32
AF = mybir.ActivationFunctionType
ALU = mybir.AluOpType
AX = mybir.AxisListType

C = 1024
H = 16
N = 64
LCH = 64
EPS_GN = 0.00064
EPS_LN = 1e-5
EPS_RMS = 1e-6
LATENT = 512
N_EXP = 8
EXP_DIM = 2048
SCL_W = -float(np.exp(-0.5))
CONST_VALS = [EPS_LN, EPS_GN, EPS_RMS, -4.0, 1e-12]


def host_consts():
    c = {}
    c["c_ident"] = np.eye(128, dtype=np.float32)
    c["c_i64b"] = np.tile(np.eye(64, dtype=np.float32), (2, 2))
    c["c_negI"] = (-1e9 * np.eye(128)).astype(np.float32)
    c["c_u01"] = np.tril(np.ones((128, 128), np.float32), -1)
    i_idx = (np.arange(128) % 64)[:, None]
    s_idx = np.tile(np.arange(64), 16)[None, :]
    c["c_mask_s"] = (s_idx < i_idx).astype(np.float32)
    c["c_maskT_s"] = (s_idx > i_idx).astype(np.float32)
    c["c_maskT_i"] = (s_idx >= i_idx).astype(np.float32)
    ind = np.zeros((128, 128), np.float32)
    for f in range(8):
        for p in range(128):
            ind[p, f * 16 + 2 * f + (1 if p >= 64 else 0)] = 1.0
    c["c_indgn"] = ind
    sel = np.zeros((16, 8 * 128), np.float32)
    for f in range(8):
        for m in range(128):
            sel[2 * f + (1 if m >= 64 else 0), f * 128 + m] = 1.0
    c["c_sel"] = sel
    c["c_iota8"] = np.tile(np.arange(8, dtype=np.float32)[None, :], (128, 1))
    c["c_iotar"] = 999.0 - c["c_iota8"]
    c["c_bias"] = np.tile(np.array(CONST_VALS, np.float32)[None, :], (128, 1))
    return c


_HC = host_consts()
CONST_NAMES = list(_HC.keys())

ATTN_VECS = ("mix_r", "mix_w", "mix_k", "mix_v", "mix_a", "mix_g", "w0", "a0",
             "k_k", "k_a", "r_k", "ln_g", "ln_b", "fg1", "fg2")


class K:
    def __init__(self, T=2048, SB=128, SD=512, SE=256):
        self.T, self.SB, self.SD, self.SE = T, SB, SD, SE
        assert T % SB == 0 and SB % LCH == 0 and T % SD == 0 and T % SE == 0
        self.nc = bass.Bass("TRN2", target_bir_lowering=False,
                            detect_race_conditions=False)
        self.sems = {}
        self.cnt = {}
        self.waited = {}

    def mark(self, inst, name, kk=1):
        inst.then_inc(self.sems[name], kk)
        self.cnt[name] = self.cnt.get(name, 0) + kk
        return (name, self.cnt[name])

    def ev(self, name):
        return (name, self.cnt.get(name, 0))

    def flush_dve(self, nc):
        e = self.mark(nc.vector.engine_nop(), "dve")
        self.wait(nc.scalar, e)
        e2 = self.mark(nc.scalar.nop(), "act")
        self.wait(nc.vector, e2)

    def wait(self, eng, *events):
        for event in events:
            if event is None:
                continue
            name, val = event
            if val <= 0:
                continue
            key = (id(eng), name)
            if self.waited.get(key, 0) >= val:
                continue
            eng.wait_ge(self.sems[name], val)
            self.waited[key] = val



class _PERec:

    def __init__(self, eng, k):
        self._e = eng
        self._k = k
        self.last = None
        self._last_ev = None
        self._last_ev_inst = None
        self._lastwin = None

    def ev_last(self):
        if self.last is None:
            return None
        if self._last_ev_inst is not self.last:
            self._last_ev = self._k.mark(self.last, "pe")
            self._last_ev_inst = self.last
        return self._last_ev

    def _guard(self, kbase, ksz):
        kwin = (kbase, ksz)
        if (self._lastwin is not None and ksz < 128 and self._lastwin[1] < 128
                and kwin != self._lastwin and self.last is not None):
            self._k.wait(self._e, self.ev_last())
        self._lastwin = kwin

    def matmul(self, out, lhsT, rhs, **kw):
        tp = kw.get("tile_position")
        kbase = tp[0] if tp is not None else 0
        self._guard(kbase, lhsT.shape[0])
        self.last = self._e.matmul(out, lhsT, rhs, **kw)
        return self.last

    def transpose(self, out, in_, ident, **kw):
        tp = kw.get("tile_position")
        kbase = tp[0] if tp is not None else 0
        self._guard(kbase, in_.shape[0])
        self.last = self._e.transpose(out, in_, ident, **kw)
        return self.last

    def __getattr__(self, nm):
        return getattr(self._e, nm)


def build_program(T=2048, SB=256, SD=512, SE=512, stages=("b", "d", "e"),
                  dbg_names=()):
    k = K(T=T, SB=SB, SD=SD, SE=SE)
    nc = k.nc
    ctx = nc.ctx

    din = {}

    def dram_in(name, shape):
        din[name] = nc.dram_tensor(name, list(shape), F32, kind="ExternalInput").ap()

    dram_in("xT", (C, T))
    for nm in CONST_NAMES:
        dram_in(nm, _HC[nm].shape)
    for nm, sh in [("w1", (C, 64)), ("w2", (64, C)), ("a1", (C, 64)), ("a2", (64, C)),
                   ("g1", (C, 128)), ("g2", (128, C)),
                   ("Wr", (C, C)), ("Wk", (C, C)), ("Wv", (C, C)), ("Wo", (C, C)),
                   ("Wq", (C, C)), ("Wdkv", (C, LATENT)), ("Wuk", (LATENT, C)),
                   ("Wuv", (LATENT, C)), ("Wof", (C, C)), ("gate_w", (C, N_EXP)),
                   ("we1", (N_EXP, C, EXP_DIM)), ("we2", (N_EXP, EXP_DIM, C))]:
        dram_in(nm, sh)
    for nm in ATTN_VECS:
        dram_in(nm, (C,))

    out = nc.dram_tensor("out", [T, C], F32, kind="ExternalOutput").ap()
    dbg = {}
    for nm, sh in dbg_names:
        dbg[nm] = nc.dram_tensor(nm, list(sh), F32, kind="ExternalOutput").ap()

    def scr(name, shape):
        return nc.dram_tensor(name, list(shape), F32, kind="Internal").ap()

    x2scr = scr("x2scr", (C, T))
    h1scr = scr("h1scr", (C, T))
    ybscr = scr("ybscr", (T, C))
    kscr = scr("kscr", (C, T))
    vscr = scr("vscr", (T // 128, 128, H * 65))

    for s in ("dS", "dG", "pe", "act", "dve", "gp"):
        k.sems[s] = ctx.enter_context(nc.semaphore(s))

    sbp = {}

    def S(name, shape):
        sbp[name] = ctx.enter_context(nc.sbuf_tensor(name, list(shape), F32)).ap()
        return sbp[name]

    for nm in CONST_NAMES:
        S("s" + nm, _HC[nm].shape)
    S("s_onescol", (128, 1))
    S("s_onesrow", (1, 128))
    S("s_vrow", (1, 1024))
    S("s_gamb", (128, 1024))
    S("s_betb", (128, 1024))
    for nm in ATTN_VECS:
        S("v_" + nm, (128, 8))
    S("v_ka1m", (128, 8))
    S("s_w1", (128, 8 * 64))
    S("s_w2", (64, 1024))
    S("s_a1", (128, 8 * 64))
    S("s_a2", (64, 1024))
    S("s_g1", (128, 8 * 128))
    S("s_g2", (128, 1024))
    S("s_gw", (128, 8 * 8))
    S("s_SS", (128, 8, 64))
    S("s_ln1last", (128, 8))
    S("s_e05n", (128, 1))

    for i, v in enumerate(CONST_VALS):
        nc.const_aps.register(F32, float(v), sbp["sc_bias"][:, i:i + 1])

    PSs = ctx.enter_context(nc.psum_tensor("PS", [128, 4096], F32)).ap()
    pb = [PSs[:, i * 512:(i + 1) * 512] for i in range(8)]
    pb.append(PSs)

    blk = ctx.enter_context(nc.Block())

    def body(sync):
        PE, ACT, DVE, GP = _PERec(nc.tensor, k), nc.scalar, nc.vector, nc.gpsimd
        for nm in CONST_NAMES:
            k.mark(sync.dma_start(out=sbp["s" + nm], in_=din[nm]), "dS", 16)
        for nm in ATTN_VECS:
            k.mark(sync.dma_start(out=sbp["v_" + nm],
                                  in_=din[nm].rearrange("(f p) -> p f", p=128),
                                  allow_slow_non_contiguous=True), "dS", 16)
        k.mark(sync.dma_start(out=sbp["s_w1"].rearrange("p (kc l) -> p kc l", l=64),
                              in_=din["w1"].rearrange("(kc p) l -> p kc l", p=128)), "dS", 16)
        k.mark(sync.dma_start(out=sbp["s_w2"], in_=din["w2"]), "dS", 16)
        k.mark(sync.dma_start(out=sbp["s_a1"].rearrange("p (kc l) -> p kc l", l=64),
                              in_=din["a1"].rearrange("(kc p) l -> p kc l", p=128)), "dS", 16)
        k.mark(sync.dma_start(out=sbp["s_a2"], in_=din["a2"]), "dS", 16)
        k.mark(sync.dma_start(out=sbp["s_g1"].rearrange("p (kc l) -> p kc l", l=128),
                              in_=din["g1"].rearrange("(kc p) l -> p kc l", p=128)), "dS", 16)
        k.mark(sync.dma_start(out=sbp["s_g2"], in_=din["g2"]), "dS", 16)
        ld = k.mark(sync.dma_start(out=sbp["s_gw"].rearrange("p (kc e) -> p kc e", e=8),
                                   in_=din["gate_w"].rearrange("(kc p) e -> p kc e", p=128)), "dS", 16)
        k.wait(DVE, ld)
        DVE.memset(sbp["s_onescol"], 1.0)
        DVE.memset(sbp["s_onesrow"], 1.0)
        DVE.tensor_scalar(sbp["v_ka1m"], sbp["v_k_a"], -1.0, 1.0, ALU.mult, ALU.add)
        DVE.memset(sbp["s_SS"], 0.0)
        DVE.memset(sbp["s_e05n"], -0.5)
        e = k.mark(DVE.memset(sbp["s_ln1last"], 0.0), "dve")
        k.wait(ACT, e)
        e = k.mark(ACT.activation(sbp["s_e05n"], sbp["s_e05n"], AF.Exp), "act")
        k.wait(DVE, e)
        e = k.mark(DVE.tensor_scalar_mul(sbp["s_e05n"], sbp["s_e05n"], -1.0), "dve")
        k.flush_dve(nc)
        for vec, dst in (("ln_g", "s_gamb"), ("ln_b", "s_betb")):
            k.wait(sync, k.ev("pe"))
            lv = k.mark(sync.dma_start(out=sbp["s_vrow"][0:1, :], in_=din[vec][None, :]), "dS", 16)
            k.wait(PE, lv, e)
            PE.matmul(pb[0][:, 0:512], sbp["s_onesrow"], sbp["s_vrow"][0:1, 0:512],
                      start=True, stop=True)
            PE.matmul(pb[1][:, 0:512], sbp["s_onesrow"],
                      sbp["s_vrow"][0:1, 512:1024], start=True, stop=True)
            ep = PE.ev_last()
            k.wait(DVE, ep)
            DVE.tensor_copy(sbp[dst][:, 0:512], pb[0][:, 0:512])
            e = k.mark(DVE.tensor_copy(sbp[dst][:, 512:1024], pb[1][:, 0:512]), "dve")

        if "b" in stages:
            phase_b(k, nc, din, dbg, sbp, pb, x2scr, sync)
        if "d" in stages:
            phase_d(k, nc, din, dbg, sbp, pb, x2scr, h1scr, ybscr, kscr, vscr, sync)
        if "e" in stages:
            phase_e(k, nc, din, dbg, sbp, pb, h1scr, ybscr, out, sync)

        GP.wait_ge(k.sems["dG"], k.cnt.get("dG", 0))
        sync.wait_ge(k.sems["dS"], k.cnt.get("dS", 0))
        PE.wait_ge(k.sems["dve"], k.cnt.get("dve", 0))
        PE.wait_ge(k.sems["act"], k.cnt.get("act", 0))

    @blk.sync
    def _(sync):
        body(sync)

    return nc, k


def phase_b(k, nc, din, dbg, sbp, pb, x2scr, sync):
    import os
    BSTEPS = int(os.environ.get("BSTEPS", "5"))
    T, SB = k.T, k.SB
    NSL = T // SB
    NCH = SB // LCH
    NTC = SB // 128
    PE, ACT, DVE, GP = _PERec(nc.tensor, k), nc.scalar, nc.vector, nc.gpsimd
    xT = din["xT"].rearrange("(f p) t -> p f t", p=128)
    x2v = x2scr.rearrange("(f p) t -> p f t", p=128)

    V = {nm: sbp["v_" + nm] for nm in ATTN_VECS}
    onescol, onesrow = sbp["s_onescol"], sbp["s_onesrow"]
    ident, i64b = sbp["sc_ident"], sbp["sc_i64b"]
    indgn, sel = sbp["sc_indgn"], sbp["sc_sel"]
    mask_s, maskT_s, maskT_i = sbp["sc_mask_s"], sbp["sc_maskT_s"], sbp["sc_maskT_i"]
    w1s, w2s, a1s, a2s = sbp["s_w1"], sbp["s_w2"], sbp["s_a1"], sbp["s_a2"]
    g1s, g2s = sbp["s_g1"], sbp["s_g2"]
    SS, ln1_last = sbp["s_SS"], sbp["s_ln1last"]
    gamb, betb = sbp["s_gamb"], sbp["s_betb"]
    ka1m = sbp["v_ka1m"]

    from contextlib import ExitStack
    es = ExitStack()

    def A(name, shape):
        return es.enter_context(nc.sbuf_tensor(name, list(shape), F32)).ap()

    pool1 = A("b_pool1", [128, 16 * SB])
    x = A("b_x", [128, 8, SB])
    ln = pool1[:, 0:8 * SB].rearrange("p (f t) -> p f t", t=SB)
    xx = pool1[:, 8 * SB:16 * SB].rearrange("p (f t) -> p f t", t=SB)
    mix = A("b_mix", [128, 8, SB])
    r_ = A("b_r", [128, 8, SB])
    kf = A("b_kf", [128, 8, SB])
    kkn = A("b_kkn", [128, 8, SB])
    bq = A("b_bq", [128, 8, SB])
    cc = A("b_cc", [128, 8, SB])
    vtm = A("b_vtm", [128, NTC, 1024])
    s1 = A("b_s1", [128, SB])
    sgb = A("b_sgb", [16, SB])
    wbuf = A("b_wbuf", [128, 2, 1024])
    lb = A("b_lb", [64, SB])
    smask = A("b_smask", [128, SB])
    rowm = A("b_rowm", [1, SB])
    rowr = A("b_rowr", [1, SB])
    rown = A("b_rown", [16, SB])
    E = A("b_E", [128, 8, LCH])
    Ei = A("b_Ei", [128, 8, LCH])
    Ep = A("b_Ep", [128, 8, LCH])
    EcL = A("b_EcL", [128, 8])
    rt = A("b_rt", [128, 8, LCH])
    at = A("b_at", [128, 8, LCH])
    bp_ = A("b_bp", [128, 8, LCH])
    kp = A("b_kp", [128, 8, LCH])
    ks = A("b_ks", [128, 8, LCH])
    bs = A("b_bs", [128, 8, LCH])
    Ap = A("b_Ap", [64, 1024])
    AkaT = A("b_AkaT", [64, 1024])
    ArbT = A("b_ArbT", [64, 1024])
    ArkT = A("b_ArkT", [64, 1024])
    U = A("b_U", [64, 1024])
    V2 = A("b_V2", [64, 1024])
    if SB >= 256:
        kst = pool1[0:64, 8 * SB:8 * SB + 1024]
        bst = pool1[0:64, 8 * SB + 1024:8 * SB + 2048]
    else:
        kst = A("b_kst", [64, 1024])
        bst = A("b_bst", [64, 1024])
    otm = A("b_otm", [128, NTC, 1024])
    gmu = A("b_gmu", [128, 32])[:, 0:16]
    gsc = A("b_gsc", [128, 32])[:, 0:16]
    grs = A("b_grs", [128, 32])[:, 0:16]
    sbT = A("b_sbT", [128, 32])[:, 0:16]
    ytm = pool1[:, 0:1024]
    zfm = pool1[:, 8 * SB:16 * SB].rearrange("p (f t) -> p f t", t=SB)
    if SB >= 256:
        mixflat = mix.rearrange("p a b -> p (a b)")
        Cl = [A("b_Cl%d" % i, [64, 1024]) for i in range(2)]
        Cl.append(mixflat[0:64, 0:1024])
        Cl.append(mixflat[0:64, 1024:2048])
    else:
        Cl = [A("b_Cl%d" % i, [64, 1024]) for i in range(4)]
    Cl.append(pool1[0:64, 0:1024])
    Cl.append(pool1[0:64, 1024:2048])
    try:
        sq = mix

        DVE.memset(smask, 1.0)
        e_mask = k.mark(DVE.memset(
            smask.rearrange("p (c l) -> p c l", l=LCH)[:, :, 0:1], 0.0), "dve")

        wuse = {}

        SWGB = int(os.environ.get("SWGB", "1"))

        def stream_W(name, kc, slot, pe_guard="auto"):
            g = (wuse.get(slot) if pe_guard == "auto" else pe_guard) \
                if SWGB else k.ev("pe")
            if g is not None:
                k.wait(sync, g)
            return k.mark(sync.dma_start(out=wbuf[:, slot, :],
                                         in_=din[name][kc * 128:(kc + 1) * 128, :]), "dS", 16)

        for s in range(NSL):
            t0 = s * SB
            sl = slice(t0, t0 + SB)
            k.wait(ACT, k.ev("pe"), k.ev("dve"))
            k.wait(DVE, k.ev("pe"), k.ev("act"))
            k.wait(PE, k.ev("act"), k.ev("dve"))
            LNS = int(os.environ.get("LNSTEP", "9"))
            e_x = k.mark(sync.dma_start(out=x, in_=xT[:, :, sl]), "dS", 16)
            if LNS < 2:
                continue
            k.wait(ACT, e_x)
            e_sq = k.mark(ACT.activation(sq, x, AF.Square), "act")
            if LNS < 3:
                continue
            k.wait(PE, e_x, e_sq, k.ev("dve"))
            for f in range(8):
                PE.matmul(pb[6][0:1, 0:SB], onescol, x[:, f, :], start=(f == 0), stop=False)
            for f in range(8):
                PE.matmul(pb[6][0:1, SB:2 * SB], onescol, sq[:, f, :],
                          start=False, stop=(f == 7))
            e = PE.ev_last()
            if LNS < 4:
                continue
            LSUB = int(os.environ.get("LNSUB", "9"))
            k.wait(DVE, e)
            DVE.tensor_scalar_mul(rowm, pb[6][0:1, 0:SB], 1.0 / C)
            e = k.mark(DVE.tensor_scalar_mul(rowr, pb[6][0:1, SB:2 * SB], 1.0 / C), "dve")
            if LSUB >= 2:
                e = k.mark(DVE.tensor_mul(pb[6][0:1, 0:SB], rowm, rowm), "dve")
            if LSUB >= 3:
                DVE.tensor_sub(rowr, rowr, pb[6][0:1, 0:SB])
                e = k.mark(DVE.tensor_scalar_max(rowr, rowr, 0.0), "dve")
            if LSUB >= 4:
                k.wait(ACT, e)
                e = k.mark(ACT.activation(rowr, rowr, AF.Sqrt, bias=EPS_LN), "act")
            if LSUB >= 5:
                k.wait(DVE, e)
                e = k.mark(DVE.reciprocal(rowr, rowr), "dve")
            if LNS < 5:
                continue
            k.wait(PE, e)
            PE.matmul(pb[6][:, 0:SB], onesrow, rowm, start=True, stop=True)
            PE.matmul(pb[7][:, 0:SB], onesrow, rowr, start=True, stop=True)
            e = PE.ev_last()
            if LNS < 6:
                continue
            k.wait(DVE, e)
            for f in range(8):
                DVE.tensor_sub(ln[:, f, :], x[:, f, :], pb[6][:, 0:SB])
            for f in range(8):
                DVE.tensor_mul(ln[:, f, :], ln[:, f, :], pb[7][:, 0:SB])
            for f in range(8):
                DVE.tensor_sub(xx[:, f, 1:SB], ln[:, f, 0:SB - 1], ln[:, f, 1:SB])
                DVE.scalar_tensor_tensor(xx[:, f, 0:1], ln[:, f, 0:1], -1.0,
                                         ln1_last[:, f:f + 1], ALU.mult, ALU.add)
            e_ln = k.mark(DVE.tensor_copy(ln1_last, ln[:, :, SB - 1]), "dve")

            if BSTEPS < 2:
                continue
            def mk_mix(mvec, pe_guard):
                k.wait(DVE, pe_guard)
                for f in range(8):
                    DVE.scalar_tensor_tensor(mix[:, f, :], xx[:, f, :],
                                             mvec[:, f:f + 1], ln[:, f, :],
                                             ALU.mult, ALU.add)
                return k.mark(DVE.engine_nop(), "dve")

            def proj_fm(wname, evmix, evac):
                evw = [stream_W(wname, 0, 0), None]
                e_pe_prev = None
                for kc in range(8):
                    if kc + 1 < 8:
                        evw[(kc + 1) % 2] = stream_W(wname, kc + 1, (kc + 1) % 2)
                    k.wait(PE, evw[kc % 2], evmix)
                    W = wbuf[:, kc % 2, :]
                    for oc in range(8):
                        bank = pb[oc // 2]
                        col = (oc % 2) * SB
                        PE.matmul(bank[:, col:col + SB], W[:, oc * 128:(oc + 1) * 128],
                                  mix[:, kc, :], start=(kc == 0 and oc % 2 == 0),
                                  stop=(kc == 7 and oc % 2 == 1))
                    e_pe_prev = PE.ev_last()
                    wuse[kc % 2] = e_pe_prev
                for oc in range(8):
                    evac(oc, pb[oc // 2][:, (oc % 2) * SB:(oc % 2) * SB + SB], e_pe_prev)
                return e_pe_prev

            evm = mk_mix(V["mix_r"], k.ev("pe"))

            def ev_r(oc, ap, epe):
                k.wait(DVE, epe)
                k.mark(DVE.tensor_copy(r_[:, oc, :], ap), "dve")
            e_projr = proj_fm("Wr", evm, ev_r)

            evm = mk_mix(V["mix_k"], e_projr)

            def ev_k(oc, ap, epe):
                k.wait(DVE, epe)
                k.mark(DVE.tensor_copy(kf[:, oc, :], ap), "dve")
            e_projk = proj_fm("Wk", evm, ev_k)

            evm = mk_mix(V["mix_v"], e_projk)
            evw = [stream_W("Wv", 0, 0), None]
            e_pe_prev = None
            for kc in range(8):
                if kc + 1 < 8:
                    evw[(kc + 1) % 2] = stream_W("Wv", kc + 1, (kc + 1) % 2)
                k.wait(PE, evw[kc % 2], evm)
                W = wbuf[:, kc % 2, :]
                for tc in range(NTC):
                    for hf in range(2):
                        PE.matmul(pb[2 * tc + hf][:, 0:512],
                                  mix[:, kc, tc * 128:(tc + 1) * 128],
                                  W[:, hf * 512:(hf + 1) * 512],
                                  start=(kc == 0), stop=(kc == 7))
                e_pe_prev = PE.ev_last()
                wuse[kc % 2] = e_pe_prev
            k.wait(DVE, e_pe_prev)
            for tc in range(NTC):
                DVE.tensor_copy(vtm[:, tc, 0:512], pb[2 * tc][:, 0:512])
                k.mark(DVE.tensor_copy(vtm[:, tc, 512:1024], pb[2 * tc + 1][:, 0:512]), "dve")

            if BSTEPS < 3:
                continue
            e_kkt = None
            k.wait(DVE, k.ev("pe"))
            for f in range(8):
                DVE.tensor_scalar_mul(kkn[:, f, :], kf[:, f, :], V["k_k"][:, f:f + 1])
            e_kkt = k.mark(DVE.engine_nop(), "dve")
            k.wait(ACT, e_kkt)
            e_sq2 = k.mark(ACT.activation(mix, kkn, AF.Square), "act")
            k.wait(PE, e_sq2)
            for f in range(8):
                PE.matmul(pb[4][0:16, 0:SB], indgn[:, f * 16:(f + 1) * 16],
                          mix[:, f, :], start=(f == 0), stop=(f == 7))
            e = PE.ev_last()
            k.wait(ACT, e)
            e = k.mark(ACT.activation(rown, pb[4][0:16, 0:SB], AF.Sqrt), "act")
            k.wait(DVE, e)
            DVE.tensor_scalar_max(rown, rown, 1e-12)
            e = k.mark(DVE.reciprocal(rown, rown), "dve")
            k.wait(PE, e)
            for f in range(8):
                k.wait(PE, k.ev("dve"))
                PE.matmul(pb[4 + f % 2][:, 0:SB], sel[:, f * 128:(f + 1) * 128],
                          rown, start=True, stop=True)
                e = PE.ev_last()
                k.wait(DVE, e)
                k.mark(DVE.tensor_mul(kkn[:, f, :], kkn[:, f, :], pb[4 + f % 2][:, 0:SB]), "dve")

            evm = mk_mix(V["mix_w"], k.ev("pe"))
            k.wait(PE, evm)
            for kc in range(8):
                PE.matmul(pb[0][0:64, 0:SB], w1s[:, kc * 64:(kc + 1) * 64],
                          mix[:, kc, :], start=(kc == 0), stop=(kc == 7))
            e = PE.ev_last()
            k.wait(ACT, e)
            e_tanh = k.mark(ACT.activation(lb, pb[0][0:64, 0:SB], AF.Tanh), "act")
            k.wait(PE, e_tanh)
            for ocp in range(4):
                bank = pb[1 + ocp]
                for j in range(2):
                    oc = 2 * ocp + j
                    PE.matmul(bank[:, j * SB:(j + 1) * SB],
                              w2s[:, oc * 128:(oc + 1) * 128], lb,
                              start=(j == 0), stop=(j == 1))
                e = PE.ev_last()
                k.wait(ACT, e)
                ACT.activation(cc[:, 2 * ocp, :], bank[:, 0:SB], AF.Sigmoid,
                               bias=V["w0"][:, 2 * ocp:2 * ocp + 1])
                e_cc = k.mark(ACT.activation(cc[:, 2 * ocp + 1, :], bank[:, SB:2 * SB],
                                             AF.Sigmoid,
                                             bias=V["w0"][:, 2 * ocp + 1:2 * ocp + 2]), "act")

            evm = mk_mix(V["mix_a"], k.ev("pe"))
            k.wait(PE, evm)
            for kc in range(8):
                PE.matmul(pb[0][0:64, 0:SB], a1s[:, kc * 64:(kc + 1) * 64],
                          mix[:, kc, :], start=(kc == 0), stop=(kc == 7))
            e = PE.ev_last()
            k.wait(DVE, e)
            e_za = k.mark(DVE.tensor_copy(lb, pb[0][0:64, 0:SB]), "dve")
            k.wait(PE, e_za)
            for ocp in range(4):
                bank = pb[1 + ocp]
                for j in range(2):
                    oc = 2 * ocp + j
                    PE.matmul(bank[:, j * SB:(j + 1) * SB],
                              a2s[:, oc * 128:(oc + 1) * 128], lb,
                              start=(j == 0), stop=(j == 1))
                e = PE.ev_last()
                k.wait(ACT, e)
                ACT.activation(bq[:, 2 * ocp, :], bank[:, 0:SB], AF.Sigmoid,
                               bias=V["a0"][:, 2 * ocp:2 * ocp + 1])
                e_ag = k.mark(ACT.activation(bq[:, 2 * ocp + 1, :], bank[:, SB:2 * SB],
                                             AF.Sigmoid,
                                             bias=V["a0"][:, 2 * ocp + 1:2 * ocp + 2]), "act")

            evm = mk_mix(V["mix_g"], k.ev("pe"))
            k.wait(PE, evm)
            for kc in range(8):
                PE.matmul(pb[0][:, 0:SB], g1s[:, kc * 128:(kc + 1) * 128],
                          mix[:, kc, :], start=(kc == 0), stop=(kc == 7))
            e_g1pe = PE.ev_last()
            k.wait(ACT, e_g1pe)
            e_s1 = k.mark(ACT.activation(s1, pb[0][:, 0:SB], AF.Sigmoid), "act")

            k.wait(DVE, e_ag, e_cc, e_g1pe)
            for f in range(8):
                DVE.tensor_scalar(mix[:, f, :], bq[:, f, :], V["k_a"][:, f:f + 1],
                                  None, ALU.mult)
                DVE.tensor_scalar_add(mix[:, f, :], mix[:, f, :], ka1m[:, f:f + 1])
            DVE.tensor_mul(kf, kf, mix)
            for f in range(8):
                DVE.tensor_scalar_mul(mix[:, f, :], r_[:, f, :], V["r_k"][:, f:f + 1])
            e = k.mark(DVE.tensor_mul(mix, mix, kf), "dve")
            k.wait(PE, e)
            k.wait(PE, k.ev("dve"))
            for f in range(8):
                PE.matmul(pb[4][0:16, 0:SB], indgn[:, f * 16:(f + 1) * 16],
                          mix[:, f, :], start=(f == 0), stop=(f == 7))
            e = PE.ev_last()
            k.wait(DVE, e)
            k.mark(DVE.tensor_copy(sgb, pb[4][0:16, 0:SB]), "dve")
            e_bq = k.mark(DVE.tensor_mul(bq, kkn, bq), "dve")
            k.wait(DVE, e_cc, e_mask)
            for f in range(8):
                DVE.tensor_tensor_scan(cc[:, f, :], smask, cc[:, f, :], 0.0,
                                       ALU.mult, ALU.add)
            for f in range(8):
                DVE.tensor_scalar_mul(cc[:, f, :], cc[:, f, :],
                                      sbp["s_e05n"][:, 0:1])
            e_scan = k.mark(DVE.engine_nop(), "dve")

            if BSTEPS < 4:
                continue
            psW0 = pb[8][0:64, 0:1024]
            psW1 = pb[8][0:64, 1024:2048]
            psO = pb[8][:, 2048:3072]
            psSa = pb[8][0:64, 3072:3584]
            psSb = pb[8][64:128, 3584:4096]

            prev_e_rs = None
            prev_e_ss = None
            for ch in range(NCH):
                tc = ch // 2
                rb = (ch % 2) * 64
                csl = slice(ch * LCH, (ch + 1) * LCH)
                k.wait(ACT, e_scan, e_bq, prev_e_rs, prev_e_ss)
                ACT.activation(E, cc[:, :, csl], AF.Exp)
                e1 = k.mark(ACT.activation(Ei, cc[:, :, csl], AF.Exp, scale=-1.0), "act")
                e2 = k.mark(ACT.activation(EcL, cc[:, :, ch * LCH + LCH - 1],
                                           AF.Exp), "act")
                k.wait(DVE, e1, e2)
                DVE.memset(Ep[:, :, 0:1], 1.0)
                DVE.tensor_copy(Ep[:, :, 1:LCH], E[:, :, 0:LCH - 1])
                DVE.tensor_mul(rt, r_[:, :, csl], E)
                DVE.scalar_tensor_tensor(at, Ep, -1.0, kkn[:, :, csl], ALU.mult, ALU.mult)
                DVE.tensor_mul(bp_, bq[:, :, csl], Ei)
                DVE.tensor_mul(kp, kf[:, :, csl], Ei)
                for f in range(8):
                    DVE.tensor_scalar_mul(ks[:, f, :], Ei[:, f, :], EcL[:, f:f + 1])
                DVE.tensor_mul(bs, bq[:, :, csl], ks)
                e_rs = k.mark(DVE.tensor_mul(ks, kf[:, :, csl], ks), "dve")
                prev_e_rs = e_rs
                CST = int(os.environ.get("CSTEP", "9"))
                if CST < 2:
                    continue

                HEADS = [(h, h // 2, (h % 2) * 64) for h in range(16)]

                k.wait(PE, k.ev("dve"))
                for cb in range(2):
                    PE.matmul(psW0[:, cb * 512:(cb + 1) * 512],
                              i64b[rb:rb + 64, 0:64],
                              vtm[rb:rb + 64, tc, cb * 512:(cb + 1) * 512],
                              start=True, stop=True, tile_position=(rb, 0))
                e = PE.ev_last()
                k.wait(DVE, e)
                e_v2 = k.mark(DVE.tensor_copy(V2, psW0), "dve")

                if CST < 3:
                    continue
                specs = [
                    (Ap, at, bp_, mask_s),
                    (Cl[0], bp_, at, maskT_s),
                    (AkaT, kp, at, maskT_s),
                    (ArbT, bp_, rt, maskT_i),
                    (ArkT, kp, rt, maskT_i),
                ]
                for idx, (dst, lh, rh, msk) in enumerate(specs):
                    bank = psW1 if idx % 2 == 0 else psW0
                    k.wait(PE, k.ev("dve"))
                    for h, f, nh in HEADS:
                        PE.matmul(bank[:, h * 64:(h + 1) * 64],
                                  lh[nh:nh + 64, f, :], rh[nh:nh + 64, f, :],
                                  start=(h % 8 == 0), stop=(h % 8 == 7),
                                  tile_position=(nh, 0))
                    e = PE.ev_last()
                    k.wait(DVE, e)
                    k.mark(DVE.tensor_mul(dst, bank, msk[0:64, :]), "dve")

                if CST < 4:
                    continue
                for lev in range(1, 6):
                    k.wait(PE, k.ev("dve"))
                    for h, f, nh in HEADS:
                        PE.matmul(psW1[:, h * 64:(h + 1) * 64],
                                  Ap[:, h * 64:(h + 1) * 64],
                                  Cl[lev - 1][:, h * 64:(h + 1) * 64],
                                  start=(h % 8 == 0), stop=(h % 8 == 7),
                                  tile_position=(0, 0))
                    e_c2 = PE.ev_last()
                    if lev < 5:
                        for h, f, nh in HEADS:
                            PE.matmul(psW0[:, h * 64:(h + 1) * 64],
                                      Cl[lev - 1][:, h * 64:(h + 1) * 64],
                                      Ap[:, h * 64:(h + 1) * 64],
                                      start=(h % 8 == 0), stop=(h % 8 == 7),
                                      tile_position=(0, 0))
                        e_a2 = PE.ev_last()
                        k.wait(DVE, e_a2)
                        DVE.tensor_copy(Ap, psW0)
                    k.wait(DVE, e_c2)
                    k.mark(DVE.tensor_copy(Cl[lev], psW1), "dve")

                if CST < 5:
                    continue
                k.wait(PE, k.ev("dve"), e_v2)
                for h, f, nh in HEADS:
                    PE.matmul(psW0[:, h * 64:(h + 1) * 64],
                              AkaT[:, h * 64:(h + 1) * 64],
                              V2[:, h * 64:(h + 1) * 64],
                              start=(h % 8 == 0), stop=False, tile_position=(0, 0))
                for h, f, nh in HEADS:
                    PE.matmul(psW0[:, h * 64:(h + 1) * 64],
                              at[nh:nh + 64, f, :], SS[nh:nh + 64, f, :],
                              start=False, stop=(h % 8 == 7), tile_position=(nh, 0))
                e = PE.ev_last()
                k.wait(DVE, e)
                e_u = k.mark(DVE.tensor_copy(U, psW0), "dve")

                for lev in (5, 4, 3, 2, 1, 0):
                    k.wait(PE, e_u)
                    for h, f, nh in HEADS:
                        PE.matmul(psW0[:, h * 64:(h + 1) * 64],
                                  Cl[lev][:, h * 64:(h + 1) * 64],
                                  U[:, h * 64:(h + 1) * 64],
                                  start=(h % 8 == 0), stop=(h % 8 == 7),
                                  tile_position=(0, 0))
                    e = PE.ev_last()
                    k.wait(DVE, e)
                    e_u = k.mark(DVE.tensor_add(U, U, psW0), "dve")

                if CST < 6:
                    continue
                k.wait(PE, e_u)
                for h, f, nh in HEADS:
                    PE.matmul(psO[rb:rb + 64, h * 64:(h + 1) * 64],
                              rt[nh:nh + 64, f, :], SS[nh:nh + 64, f, :],
                              start=(h % 8 == 0), stop=False, tile_position=(nh, rb))
                    PE.matmul(psO[rb:rb + 64, h * 64:(h + 1) * 64],
                              ArbT[:, h * 64:(h + 1) * 64],
                              U[:, h * 64:(h + 1) * 64],
                              start=False, stop=False, tile_position=(0, rb))
                    PE.matmul(psO[rb:rb + 64, h * 64:(h + 1) * 64],
                              ArkT[:, h * 64:(h + 1) * 64],
                              V2[:, h * 64:(h + 1) * 64],
                              start=False, stop=(h % 8 == 7), tile_position=(0, rb))
                e_o = PE.ev_last()

                if CST < 7:
                    continue
                for h, f, nh in HEADS:
                    PE.matmul(psW0[:, h * 64:(h + 1) * 64],
                              ks[nh:nh + 64, f, :], i64b[nh:nh + 64, 0:64],
                              start=(h % 8 == 0), stop=(h % 8 == 7),
                              tile_position=(nh, 0))
                e_k2 = PE.ev_last()
                for h, f, nh in HEADS:
                    PE.matmul(psW1[:, h * 64:(h + 1) * 64],
                              bs[nh:nh + 64, f, :], i64b[nh:nh + 64, 0:64],
                              start=(h % 8 == 0), stop=(h % 8 == 7),
                              tile_position=(nh, 0))
                e_b2 = PE.ev_last()
                k.wait(DVE, e_k2, e_b2)
                DVE.tensor_copy(kst, psW0)
                e_t = k.mark(DVE.tensor_copy(bst, psW1), "dve")

                k.wait(PE, e_t)
                for par, bankS in ((0, psSa), (1, psSb)):
                    hs = [h for h in range(16) if h % 2 == par]
                    for i_h, h in enumerate(hs):
                        f = h // 2
                        nh = par * 64
                        PE.matmul(bankS[:, f * 64:(f + 1) * 64],
                                  bst[:, h * 64:(h + 1) * 64],
                                  U[:, h * 64:(h + 1) * 64],
                                  start=(i_h == 0), stop=False,
                                  tile_position=(0, nh))
                        PE.matmul(bankS[:, f * 64:(f + 1) * 64],
                                  kst[:, h * 64:(h + 1) * 64],
                                  V2[:, h * 64:(h + 1) * 64],
                                  start=False, stop=(i_h == 7),
                                  tile_position=(0, nh))
                e_st = PE.ev_last()
                k.wait(DVE, e_st)
                for f in range(8):
                    DVE.tensor_scalar_mul(SS[:, f, :], SS[:, f, :], EcL[:, f:f + 1])
                SSf = SS.rearrange("p a b -> p (a b)")
                DVE.tensor_add(SSf[0:64, :], SSf[0:64, :], psSa)
                e_ss = k.mark(DVE.tensor_add(SSf[64:128, :], SSf[64:128, :], psSb), "dve")
                prev_e_ss = e_ss

                if CST < 8:
                    continue
                k.wait(DVE, e_o)
                k.mark(DVE.tensor_copy(otm[rb:rb + 64, tc, :], psO[rb:rb + 64, :]), "dve")
                if "d_SS" in dbg:
                    gch_ = s * NCH + ch
                    k.wait(GP, e_ss)
                    k.mark(GP.dma_start(out=dbg["d_SS"][gch_ * 128:(gch_ + 1) * 128, 0:512],
                                        in_=SS.rearrange("p a b -> p (a b)")), "dG", 16)
                    k.wait(DVE, k.ev("dG"))
                    k.wait(PE, k.ev("dG"))

                if s == 0 and ch == 0:
                    for dnm, dt_ in (("d_Ap", Ap), ("d_Cl0", Cl[0]), ("d_Cl5", Cl[5]),
                                     ("d_AkaT", AkaT), ("d_ArbT", ArbT),
                                     ("d_ArkT", ArkT), ("d_U", U), ("d_V2", V2),
                                     ("d_kst", kst), ("d_bst", bst)):
                        if dnm in dbg:
                            k.wait(GP, k.ev("dve"))
                            k.mark(GP.dma_start(out=dbg[dnm], in_=dt_), "dG", 16)
                    if "d_at" in dbg:
                        k.wait(GP, k.ev("dve"))
                        for dnm, dt_ in (("d_at", at), ("d_bp", bp_), ("d_kp", kp),
                                         ("d_rt", rt), ("d_ks", ks), ("d_bs", bs),
                                         ("d_E", E), ("d_Ei", Ei), ("d_Ep", Ep)):
                            k.mark(GP.dma_start(
                                out=dbg[dnm],
                                in_=dt_.rearrange("p a b -> p (a b)")), "dG", 16)
                    k.wait(DVE, k.ev("dG"))
                    k.wait(PE, k.ev("dG"))
                    k.wait(ACT, k.ev("dG"))

            if BSTEPS < 5:
                continue
            evW = [stream_W("Wo", 0, 0), None]
            for tc in range(NTC):
                k.wait(PE, k.ev("dve"))
                PE.transpose(pb[4][0:128, 0:16],
                             sgb[0:16, tc * 128:(tc + 1) * 128],
                             ident[0:16, 0:16])
                e = PE.ev_last()
                k.wait(DVE, e)
                e_sbt = k.mark(DVE.tensor_copy(sbT, pb[4][0:128, 0:16]), "dve")
                k.flush_dve(nc)
                if s == 0 and tc == 0 and "d_otmB" in dbg:
                    k.wait(GP, k.ev("dve"))
                    k.mark(GP.dma_start(out=dbg["d_otmB"][0:128, :], in_=otm[:, 0, :]), "dG", 16)
                    k.wait(DVE, k.ev("dG"))
                for h in range(16):
                    DVE.tensor_reduce(gmu[:, h:h + 1],
                                      otm[:, tc, h * 64:(h + 1) * 64], AX.X, ALU.add)
                k.flush_dve(nc)
                e = k.mark(DVE.tensor_scalar_mul(gmu, gmu, 1.0 / 64), "dve")
                k.flush_dve(nc)
                for h in range(16):
                    DVE.tensor_scalar(ytm[:, h * 64:(h + 1) * 64],
                                      otm[:, tc, h * 64:(h + 1) * 64],
                                      gmu[:, h:h + 1], None, ALU.subtract)
                e = k.mark(DVE.engine_nop(), "dve")
                k.wait(ACT, e)
                e = k.mark(ACT.activation(otm[:, tc, :], ytm, AF.Square), "act")
                k.wait(DVE, e)
                for h in range(16):
                    DVE.tensor_reduce(grs[:, h:h + 1],
                                      otm[:, tc, h * 64:(h + 1) * 64], AX.X, ALU.add)
                k.flush_dve(nc)
                e = k.mark(DVE.tensor_scalar_mul(grs, grs, 1.0 / 64), "dve")
                k.wait(ACT, e)
                e = k.mark(ACT.activation(grs, grs, AF.Sqrt, bias=EPS_GN), "act")
                k.wait(DVE, e)
                e = k.mark(DVE.reciprocal(grs, grs), "dve")
                k.flush_dve(nc)
                if s == 0 and "d_gmu" in dbg:
                    k.wait(GP, e)
                    k.mark(GP.dma_start(out=dbg["d_gmu"][tc * 128:(tc + 1) * 128, 0:16],
                                        in_=gmu), "dG", 16)
                    k.mark(GP.dma_start(out=dbg["d_grs"][tc * 128:(tc + 1) * 128, 0:16],
                                        in_=grs), "dG", 16)
                    k.mark(GP.dma_start(out=dbg["d_sbT"][tc * 128:(tc + 1) * 128, 0:16],
                                        in_=sbT), "dG", 16)
                    k.wait(DVE, k.ev("dG"))
                for h in range(16):
                    DVE.tensor_scalar(ytm[:, h * 64:(h + 1) * 64],
                                      ytm[:, h * 64:(h + 1) * 64],
                                      grs[:, h:h + 1], None, ALU.mult)
                if s == 0 and tc == 0 and "d_yh" in dbg:
                    e_ = k.mark(DVE.engine_nop(), "dve")
                    k.wait(GP, e_)
                    k.mark(GP.dma_start(out=dbg["d_yh"][0:128, :], in_=ytm), "dG", 16)
                    k.mark(GP.dma_start(out=dbg["d_gam"][0:128, :], in_=gamb), "dG", 16)
                    k.mark(GP.dma_start(out=dbg["d_bet"][0:128, :], in_=betb), "dG", 16)
                    k.wait(DVE, k.ev("dG"))
                DVE.tensor_mul(ytm, ytm, gamb)
                DVE.tensor_add(ytm, ytm, betb)
                k.wait(DVE, e_sbt)
                for h in range(16):
                    DVE.scalar_tensor_tensor(ytm[:, h * 64:(h + 1) * 64],
                                             vtm[:, tc, h * 64:(h + 1) * 64],
                                             sbT[:, h:h + 1],
                                             ytm[:, h * 64:(h + 1) * 64],
                                             ALU.mult, ALU.add)
                e = k.mark(DVE.engine_nop(), "dve")
                if s == 0 and "d_y" in dbg:
                    k.wait(GP, e)
                    k.mark(GP.dma_start(out=dbg["d_y"][tc * 128:(tc + 1) * 128, :],
                                        in_=ytm), "dG", 16)
                    k.wait(DVE, k.ev("dG"))
                k.wait(PE, e_s1, e)
                PE.matmul(pb[4][:, 0:512], s1[:, tc * 128:(tc + 1) * 128],
                          g2s[:, 0:512], start=True, stop=True)
                PE.matmul(pb[5][:, 0:512], s1[:, tc * 128:(tc + 1) * 128],
                          g2s[:, 512:1024], start=True, stop=True)
                e = PE.ev_last()
                k.wait(DVE, e)
                DVE.tensor_mul(ytm[:, 0:512], ytm[:, 0:512], pb[4][:, 0:512])
                e = k.mark(DVE.tensor_mul(ytm[:, 512:1024], ytm[:, 512:1024],
                                          pb[5][:, 0:512]), "dve")
                if s == 0 and "d_z" in dbg:
                    k.wait(GP, e)
                    k.mark(GP.dma_start(out=dbg["d_z"][tc * 128:(tc + 1) * 128, :],
                                        in_=ytm), "dG", 16)
                    k.wait(DVE, k.ev("dG"))
                    k.wait(PE, k.ev("dG"))
                k.wait(PE, e)
                for oc in range(8):
                    PE.transpose(pb[4 + oc // 4][:, (oc % 4) * 128:(oc % 4 + 1) * 128],
                                 ytm[:, oc * 128:(oc + 1) * 128], ident)
                e = PE.ev_last()
                k.wait(DVE, e)
                for oc in range(8):
                    k.mark(DVE.tensor_copy(zfm[:, oc, tc * 128:(tc + 1) * 128],
                                           pb[4 + oc // 4][:, (oc % 4) * 128:(oc % 4 + 1) * 128]),
                           "dve")

            e_pe_prev = None
            for kc in range(8):
                if kc + 1 < 8:
                    evW[(kc + 1) % 2] = stream_W("Wo", kc + 1, (kc + 1) % 2)
                k.wait(PE, evW[kc % 2], k.ev("dve"))
                W = wbuf[:, kc % 2, :]
                for oc in range(8):
                    bank = pb[oc // 2]
                    col = (oc % 2) * SB
                    PE.matmul(bank[:, col:col + SB], W[:, oc * 128:(oc + 1) * 128],
                              zfm[:, kc, :], start=(kc == 0 and oc % 2 == 0),
                              stop=(kc == 7 and oc % 2 == 1))
                e_pe_prev = PE.ev_last()
                wuse[kc % 2] = e_pe_prev
            k.wait(DVE, e_pe_prev)
            for oc in range(8):
                DVE.tensor_add(x[:, oc, :], x[:, oc, :],
                               pb[oc // 2][:, (oc % 2) * SB:(oc % 2) * SB + SB])
            e = k.mark(DVE.engine_nop(), "dve")
            k.wait(GP, e)
            ew = k.mark(GP.dma_start(out=x2v[:, :, sl], in_=x), "dG", 16)
            if "d_x2" in dbg:
                dv = dbg["d_x2"].rearrange("(f p) t -> p f t", p=128)
                k.mark(GP.dma_start(out=dv[:, :, sl], in_=x), "dG", 16)
            k.wait(sync, k.ev("dG"))

            if "d_ln1" in dbg:
                dv = dbg["d_ln1"].rearrange("(f p) t -> p f t", p=128)
                k.mark(GP.dma_start(out=dv[:, :, sl], in_=ln), "dG", 16)
            if "d_r" in dbg:
                dv = dbg["d_r"].rearrange("(f p) t -> p f t", p=128)
                k.mark(GP.dma_start(out=dv[:, :, sl], in_=r_), "dG", 16)
            if "d_kf" in dbg:
                dv = dbg["d_kf"].rearrange("(f p) t -> p f t", p=128)
                k.mark(GP.dma_start(out=dv[:, :, sl], in_=kf), "dG", 16)
            if "d_kkn" in dbg:
                dv = dbg["d_kkn"].rearrange("(f p) t -> p f t", p=128)
                k.mark(GP.dma_start(out=dv[:, :, sl], in_=kkn), "dG", 16)
            if "d_cc" in dbg:
                dv = dbg["d_cc"].rearrange("(f p) t -> p f t", p=128)
                k.mark(GP.dma_start(out=dv[:, :, sl], in_=cc), "dG", 16)
            if "d_otm" in dbg:
                dv = dbg["d_otm"]
                for tc in range(NTC):
                    gt = s * NTC + tc
                    k.mark(GP.dma_start(
                        out=dv[gt * 128:(gt + 1) * 128, 0:1024],
                        in_=otm[:, tc, :]), "dG", 16)
    finally:
        es.close()


def phase_d(k, nc, din, dbg, sbp, pb, x2scr, h1scr, ybscr, kscr, vscr, sync):
    import os
    KVG = int(os.environ.get("KVG", "1"))
    T, SD = k.T, k.SD
    SH = SD // 2
    NSD = T // SD
    NTC = SD // 128
    PE, ACT, DVE, GP = _PERec(nc.tensor, k), nc.scalar, nc.vector, nc.gpsimd
    x2v = x2scr.rearrange("(f p) t -> p f t", p=128)
    h1v = h1scr.rearrange("(f p) t -> p f t", p=128)
    kv = kscr.rearrange("(f p) t -> p f t", p=128)
    ybv = ybscr.rearrange("(tc p) c -> p tc c", p=128)
    onescol, onesrow = sbp["s_onescol"], sbp["s_onesrow"]
    ident = sbp["sc_ident"]
    negI, u01 = sbp["sc_negI"], sbp["sc_u01"]
    fg1 = sbp["v_fg1"]

    from contextlib import ExitStack
    es = ExitStack()

    def A(name, shape):
        return es.enter_context(nc.sbuf_tensor(name, list(shape), F32)).ap()

    x2 = A("d_x2s", [128, 8, SD])
    h0 = A("d_h0", [128, 8, SD])
    t1 = A("ds_t1", [128, 8, SD])
    cdk = A("d_cdk", [128, 4, SD])
    q = A("d_q", [128, 8, SD])
    att = A("d_att", [128, 8, SD])
    kb = A("d_kb", [128, 2, 8, 128])
    vb = A("d_vb", [128, 2, 1040])
    Pb = A("d_Pb", [128, 512])
    wbuf = A("d_wbuf", [128, 2, 1024])
    rowm = A("d_rowm", [1, SD])
    rowr = A("d_rowr", [1, SD])
    zrow = A("d_zrow", [65, SD])
    zbc = A("d_zbc", [64, SD])
    ones65 = A("d_ones65", [65, 64])
    ytm = A("d_ytm", [128, 1024])

    try:
        e = k.mark(DVE.memset(ones65, 1.0), "dve")
        wuse = {0: k.ev("pe"), 1: k.ev("pe")}
        for s in range(NSD):
            t0 = s * SD
            sl = slice(t0, t0 + SD)
            k.wait(ACT, k.ev("pe"), k.ev("dve"))
            k.wait(DVE, k.ev("pe"), k.ev("act"))
            k.wait(PE, k.ev("act"), k.ev("dve"))
            k.wait(sync, k.ev("dG"))
            e_x = k.mark(sync.dma_start(out=x2, in_=x2v[:, :, sl]), "dS", 16)
            k.wait(ACT, e_x)
            e_sq = k.mark(ACT.activation(h0, x2, AF.Square), "act")
            k.wait(PE, e_x, e_sq)
            for f in range(8):
                PE.matmul(pb[6][0:1, 0:SD], onescol, x2[:, f, :], start=(f == 0),
                          stop=(f == 7))
            for f in range(8):
                PE.matmul(pb[7][0:1, 0:SD], onescol, h0[:, f, :], start=(f == 0),
                          stop=(f == 7))
            e = PE.ev_last()
            k.wait(DVE, e)
            DVE.tensor_scalar_mul(rowm, pb[6][0:1, 0:SD], 1.0 / C)
            DVE.tensor_scalar_mul(rowr, pb[7][0:1, 0:SD], 1.0 / C)
            DVE.tensor_mul(pb[6][0:1, 0:SD], rowm, rowm)
            DVE.tensor_sub(rowr, rowr, pb[6][0:1, 0:SD])
            e = k.mark(DVE.tensor_scalar_max(rowr, rowr, 0.0), "dve")
            k.wait(ACT, e)
            e = k.mark(ACT.activation(rowr, rowr, AF.Sqrt, bias=EPS_LN), "act")
            k.wait(DVE, e)
            e = k.mark(DVE.reciprocal(rowr, rowr), "dve")
            k.wait(PE, e)
            PE.matmul(pb[6][:, 0:SD], onesrow, rowm, start=True, stop=True)
            PE.matmul(pb[7][:, 0:SD], onesrow, rowr, start=True, stop=True)
            e = PE.ev_last()
            k.wait(DVE, e)
            for f in range(8):
                DVE.tensor_sub(h0[:, f, :], x2[:, f, :], pb[6][:, 0:SD])
            for f in range(8):
                DVE.tensor_mul(h0[:, f, :], h0[:, f, :], pb[7][:, 0:SD])
            e_h0 = k.mark(DVE.engine_nop(), "dve")
            k.wait(ACT, e_h0)
            e_sq = k.mark(ACT.activation(t1, h0, AF.Square), "act")
            k.wait(PE, e_sq)
            for f in range(8):
                PE.matmul(pb[6][0:1, 0:SD], onescol, t1[:, f, :], start=(f == 0),
                          stop=(f == 7))
            e = PE.ev_last()
            k.wait(DVE, e)
            e = k.mark(DVE.tensor_scalar(rowr, pb[6][0:1, 0:SD], 1.0 / C, EPS_RMS,
                                         ALU.mult, ALU.add), "dve")
            k.wait(ACT, e)
            e = k.mark(ACT.activation(rowr, rowr, AF.Sqrt), "act")
            k.wait(DVE, e)
            e = k.mark(DVE.reciprocal(rowr, rowr), "dve")
            k.wait(PE, e)
            PE.matmul(pb[6][:, 0:SD], onesrow, rowr, start=True, stop=True)
            e = PE.ev_last()
            k.wait(DVE, e)
            for f in range(8):
                DVE.scalar_tensor_tensor(t1[:, f, :], h0[:, f, :], fg1[:, f:f + 1],
                                         pb[6][:, 0:SD], ALU.mult, ALU.mult)
            e_t1 = k.mark(DVE.engine_nop(), "dve")

            SWGD = int(os.environ.get("SWGD", "27"))
            _bits = {"Wdkv": 1, "Wq": 2, "Wuk": 4, "Wuv": 8, "Wof": 16}

            def stream_W(name, kc, slot):
                ncols = din[name].shape[1]
                g = wuse.get(slot) if (SWGD & _bits[name]) else k.ev("pe")
                if g is not None:
                    k.wait(sync, g)
                return k.mark(sync.dma_start(out=wbuf[:, slot, 0:ncols],
                                             in_=din[name][kc * 128:(kc + 1) * 128, :]),
                              "dS", 16)

            evw = [stream_W("Wdkv", 0, 0), None]
            e_pe_prev = None
            for kc in range(8):
                if kc + 1 < 8:
                    evw[(kc + 1) % 2] = stream_W("Wdkv", kc + 1, (kc + 1) % 2)
                k.wait(PE, evw[kc % 2], e_t1)
                W = wbuf[:, kc % 2, 0:512]
                for oc in range(4):
                    PE.matmul(pb[oc][:, 0:SD], W[:, oc * 128:(oc + 1) * 128],
                              t1[:, kc, :], start=(kc == 0), stop=(kc == 7))
                e_pe_prev = PE.ev_last()
                wuse[kc % 2] = e_pe_prev
            k.wait(DVE, e_pe_prev)
            for oc in range(4):
                k.mark(DVE.tensor_copy(cdk[:, oc, :], pb[oc][:, 0:SD]), "dve")
            evw = [stream_W("Wq", 0, 0), None]
            for kc in range(8):
                if kc + 1 < 8:
                    evw[(kc + 1) % 2] = stream_W("Wq", kc + 1, (kc + 1) % 2)
                k.wait(PE, evw[kc % 2], k.ev("dve"))
                W = wbuf[:, kc % 2, :]
                for oc in range(8):
                    PE.matmul(pb[oc][:, 0:SD],
                              W[:, oc * 128:(oc + 1) * 128],
                              t1[:, kc, :], start=(kc == 0), stop=(kc == 7))
                e_pe_prev = PE.ev_last()
                wuse[kc % 2] = e_pe_prev
            k.wait(ACT, e_pe_prev)
            for oc in range(8):
                ACT.activation(q[:, oc, :], pb[oc][:, 0:SD],
                               AF.Identity, scale=0.125)
            e_q = k.mark(ACT.nop(), "act")

            if s == 0 and "d_t1" in dbg:
                k.wait(GP, e_t1, e_q)
                dv = dbg["d_t1"].rearrange("(f p) t -> p f t", p=128)
                k.mark(GP.dma_start(out=dv[:, :, sl], in_=t1), "dG", 16)
                dv = dbg["d_qd"].rearrange("(f p) t -> p f t", p=128)
                k.mark(GP.dma_start(out=dv[:, :, sl], in_=q), "dG", 16)
                k.wait(DVE, k.ev("dG"))
                k.wait(PE, k.ev("dG"))
            evw = [stream_W("Wuk", 0, 0), None]
            for kc in range(4):
                if kc + 1 < 4:
                    evw[(kc + 1) % 2] = stream_W("Wuk", kc + 1, (kc + 1) % 2)
                k.wait(PE, evw[kc % 2], k.ev("act"))
                W = wbuf[:, kc % 2, :]
                for oc in range(8):
                    PE.matmul(pb[oc][:, 0:SD], W[:, oc * 128:(oc + 1) * 128],
                              cdk[:, kc, :], start=(kc == 0), stop=(kc == 3))
                e_pe_prev = PE.ev_last()
                wuse[kc % 2] = e_pe_prev
            k.wait(DVE, e_pe_prev)
            for oc in range(8):
                DVE.tensor_copy(att[:, oc, :], pb[oc][:, 0:SD])
            e_kev = k.mark(DVE.engine_nop(), "dve")
            k.wait(GP, e_kev)
            e_kw = k.mark(GP.dma_start(out=kv[:, :, sl], in_=att), "dG", 16)

            evw = [stream_W("Wuv", 0, 0), None]
            for kc in range(4):
                if kc + 1 < 4:
                    evw[(kc + 1) % 2] = stream_W("Wuv", kc + 1, (kc + 1) % 2)
                k.wait(PE, evw[kc % 2], k.ev("dve"))
                W = wbuf[:, kc % 2, :]
                for tc in range(NTC):
                    for hf in range(2):
                        PE.matmul(pb[2 * tc + hf][:, 0:512],
                                  cdk[:, kc, tc * 128:(tc + 1) * 128],
                                  W[:, hf * 512:(hf + 1) * 512],
                                  start=(kc == 0), stop=(kc == 3))
                e_pe_prev = PE.ev_last()
                wuse[kc % 2] = e_pe_prev
            k.wait(DVE, e_pe_prev)
            for tc in range(NTC):
                vbt = vb[:, 0, :] if tc % 2 == 0 else vb[:, 1, :]
                vb3 = vbt.rearrange("p (h n) -> p h n", n=65)
                DVE.memset(vb3[:, :, 64:65], 1.0)
                DVE.tensor_copy(vb3[:, 0:8, 0:64],
                                pb[2 * tc][:, 0:512].rearrange("p (h n) -> p h n", n=64))
                e = k.mark(DVE.tensor_copy(vb3[:, 8:16, 0:64],
                                           pb[2 * tc + 1][:, 0:512].rearrange("p (h n) -> p h n", n=64)),
                           "dve")
                k.wait(GP, e)
                e_vw = k.mark(GP.dma_start(out=vscr[s * NTC + tc, :, :], in_=vbt), "dG", 16)
                k.wait(DVE, k.ev("dG"))

            nkt = (s + 1) * NTC
            k.wait(sync, k.ev("dG"))
            kvuse = {0: k.ev("pe"), 1: k.ev("pe")}
            for h in range(16):
                f = h // 2
                nh = (h % 2) * 64
                for kt in range(nkt):
                    slot = kt % 2
                    g = kvuse.get(slot) if KVG else k.ev("pe")
                    if g is not None:
                        k.wait(sync, g)
                    e_kt = k.mark(sync.dma_start(out=kb[:, slot, :, :],
                                                 in_=kv[:, :, kt * 128:(kt + 1) * 128]), "dS", 16)
                    e_vt = k.mark(sync.dma_start(out=vb[:, slot, :],
                                                 in_=vscr[kt, :, :]), "dS", 16)
                    qoff = max(0, kt * 128 - t0)
                    diag = kt * 128 >= t0
                    k.wait(PE, e_kt, e_q, k.ev("act"))
                    PE.matmul(pb[4][:, qoff:SD], kb[nh:nh + 64, slot, f, :],
                              q[nh:nh + 64, f, qoff:SD], start=True, stop=not diag,
                              tile_position=(nh, 0))
                    if diag:
                        PE.matmul(pb[4][:, qoff:qoff + 128], negI, u01,
                                  start=False, stop=True, tile_position=(0, 0))
                    e_sc = PE.ev_last()
                    k.wait(ACT, e_sc)
                    e_p = k.mark(ACT.activation(Pb[:, qoff:SD], pb[4][:, qoff:SD],
                                                AF.Exp, bias=-4.0), "act")
                    k.wait(PE, e_p, e_vt)
                    PE.matmul(pb[5][0:65, qoff:SD],
                              vb[:, slot, 65 * h:65 * h + 65],
                              Pb[:, qoff:SD], start=(kt == 0), stop=(kt == nkt - 1),
                              tile_position=(0, 0))
                    e_pv = PE.ev_last()
                    kvuse[slot] = e_pv
                k.wait(DVE, e_pv)
                e_z = k.mark(DVE.reciprocal(zrow[64:65, 0:SD], pb[5][64:65, 0:SD]), "dve")
                k.wait(PE, e_z)
                PE.matmul(pb[6][0:64, 0:SD], ones65[64:65, 0:64],
                          zrow[64:65, 0:SD], start=True, stop=True,
                          tile_position=(64, 0))
                e_zb = PE.ev_last()
                k.wait(DVE, e_zb)
                DVE.tensor_copy(zbc, pb[6][0:64, 0:SD])
                e_att = k.mark(DVE.tensor_tensor(att[nh:nh + 64, f, :],
                                                 pb[5][0:64, 0:SD], zbc,
                                                 ALU.mult), "dve")
                k.wait(PE, e_att)

            if s == 0 and "d_attn" in dbg:
                k.wait(GP, k.ev("dve"))
                dv = dbg["d_attn"].rearrange("(f p) t -> p f t", p=128)
                k.mark(GP.dma_start(out=dv[:, :, sl], in_=att), "dG", 16)
                k.wait(DVE, k.ev("dG"))
                k.wait(PE, k.ev("dG"))
            evw = [stream_W("Wof", 0, 0), None]
            for kc in range(8):
                if kc + 1 < 8:
                    evw[(kc + 1) % 2] = stream_W("Wof", kc + 1, (kc + 1) % 2)
                k.wait(PE, evw[kc % 2], k.ev("dve"))
                W = wbuf[:, kc % 2, :]
                for oc in range(8):
                    bank = pb[4 + oc // 2]
                    col = (oc % 2) * SH
                    PE.matmul(bank[:, col:col + SH], W[:, oc * 128:(oc + 1) * 128],
                              att[:, kc, 0:SH], start=(kc == 0 and oc % 2 == 0),
                              stop=(kc == 7 and oc % 2 == 1))
                for oc in range(8):
                    bank = pb[oc // 2]
                    col = (oc % 2) * SH
                    PE.matmul(bank[:, col:col + SH], W[:, oc * 128:(oc + 1) * 128],
                              att[:, kc, SH:2 * SH], start=(kc == 0 and oc % 2 == 0),
                              stop=(kc == 7 and oc % 2 == 1))
                e_pe_prev = PE.ev_last()
                wuse[kc % 2] = e_pe_prev
            k.wait(DVE, e_pe_prev)
            for oc in range(8):
                DVE.tensor_add(h0[:, oc, 0:SH], h0[:, oc, 0:SH],
                               pb[4 + oc // 2][:, (oc % 2) * SH:(oc % 2) * SH + SH])
                DVE.tensor_add(h0[:, oc, SH:2 * SH], h0[:, oc, SH:2 * SH],
                               pb[oc // 2][:, (oc % 2) * SH:(oc % 2) * SH + SH])
            e_h1 = k.mark(DVE.tensor_add(x2, x2, h0), "dve")
            k.wait(GP, e_h1)
            e_h1w = k.mark(GP.dma_start(out=h1v[:, :, sl], in_=h0), "dG", 16)
            if "d_h1" in dbg:
                dv = dbg["d_h1"].rearrange("(f p) t -> p f t", p=128)
                k.mark(GP.dma_start(out=dv[:, :, sl], in_=h0), "dG", 16)
            for tc in range(NTC):
                k.wait(PE, e_h1, k.ev("dve"))
                for oc in range(8):
                    PE.transpose(pb[4 + oc // 4][:, (oc % 4) * 128:(oc % 4 + 1) * 128],
                                 x2[:, oc, tc * 128:(tc + 1) * 128], ident)
                e = PE.ev_last()
                k.wait(DVE, e)
                for oc in range(8):
                    DVE.tensor_copy(ytm[:, oc * 128:(oc + 1) * 128],
                                    pb[4 + oc // 4][:, (oc % 4) * 128:(oc % 4 + 1) * 128])
                e = k.mark(DVE.engine_nop(), "dve")
                k.wait(GP, e)
                k.mark(GP.dma_start(out=ybv[:, s * NTC + tc, :], in_=ytm), "dG", 16)
                if "d_yb" in dbg:
                    k.mark(GP.dma_start(
                        out=dbg["d_yb"][(s * NTC + tc) * 128:(s * NTC + tc + 1) * 128, :],
                        in_=ytm), "dG", 16)
                k.wait(DVE, k.ev("dG"))
                k.wait(PE, k.ev("dG"))
    finally:
        es.close()


def phase_e(k, nc, din, dbg, sbp, pb, h1scr, ybscr, out, sync):
    import os
    PEPAIR = int(os.environ.get("PEPAIR", "1"))
    SWGE = int(os.environ.get("SWGE", "1"))
    T, SE = k.T, k.SE
    NSE = T // SE
    NTC = SE // 128
    PE, ACT, DVE, GP = _PERec(nc.tensor, k), nc.scalar, nc.vector, nc.gpsimd
    h1v = h1scr.rearrange("(f p) t -> p f t", p=128)
    ybv = ybscr.rearrange("(tc p) c -> p tc c", p=128)
    outv = out.rearrange("(tc p) c -> p tc c", p=128)
    onescol, onesrow = sbp["s_onescol"], sbp["s_onesrow"]
    fg2 = sbp["v_fg2"]
    gws = sbp["s_gw"]
    iotar = sbp["sc_iotar"]

    from contextlib import ExitStack
    es = ExitStack()

    def A(name, shape):
        return es.enter_context(nc.sbuf_tensor(name, list(shape), F32)).ap()

    h1 = A("e_h1", [128, 8, SE])
    t2 = A("e_t2", [128, 8, SE])
    yba = A("e_yba", [128, NTC, 1024])
    hid = A("e_hid", [128, 16, SE])
    w1b = A("e_w1b", [128, 2, 8, 256])
    w2b = A("e_w2b", [128, 2, 4, 512])
    rowm = A("e_rowm", [1, SE])
    v1c = A("e_v1c", [128, NTC])
    v2c = A("e_v2c", [128, NTC])
    i1c = A("e_i1c", [128, NTC])
    i2c = A("e_i2c", [128, NTC])
    g1c = A("e_g1c", [128, NTC])
    g2c = A("e_g2c", [128, NTC])
    eqs = A("e_eqs", [128, NTC * 8])
    wcols = A("e_wcols", [128, NTC, 8])

    try:
        for s in range(NSE):
            t0 = s * SE
            sl = slice(t0, t0 + SE)
            k.wait(ACT, k.ev("pe"), k.ev("dve"))
            k.wait(DVE, k.ev("pe"), k.ev("act"))
            k.wait(PE, k.ev("act"), k.ev("dve"))
            k.wait(sync, k.ev("dG"))
            e_h = k.mark(sync.dma_start(out=h1, in_=h1v[:, :, sl]), "dS", 16)
            e_yb = k.mark(sync.dma_start(out=yba,
                                         in_=ybv[:, s * NTC:(s + 1) * NTC, :]), "dS", 16)
            k.wait(ACT, e_h)
            e_sq = k.mark(ACT.activation(t2, h1, AF.Square), "act")
            k.wait(PE, e_sq, e_h)
            for f in range(8):
                PE.matmul(pb[6][0:1, 0:SE], onescol, t2[:, f, :], start=(f == 0),
                          stop=(f == 7))
            e = PE.ev_last()
            k.wait(DVE, e)
            e = k.mark(DVE.tensor_scalar(rowm, pb[6][0:1, 0:SE], 1.0 / C, EPS_RMS,
                                         ALU.mult, ALU.add), "dve")
            k.wait(ACT, e)
            e = k.mark(ACT.activation(rowm, rowm, AF.Sqrt), "act")
            k.wait(DVE, e)
            e = k.mark(DVE.reciprocal(rowm, rowm), "dve")
            k.wait(PE, e)
            PE.matmul(pb[6][:, 0:SE], onesrow, rowm, start=True, stop=True)
            e = PE.ev_last()
            k.wait(DVE, e)
            for f in range(8):
                DVE.scalar_tensor_tensor(t2[:, f, :], h1[:, f, :], fg2[:, f:f + 1],
                                         pb[6][:, 0:SE], ALU.mult, ALU.mult)
            e_t2 = k.mark(DVE.engine_nop(), "dve")

            k.wait(PE, e_t2)
            for tc in range(NTC):
                for kc in range(8):
                    PE.matmul(pb[7][:, tc * 8:(tc + 1) * 8],
                              t2[:, kc, tc * 128:(tc + 1) * 128],
                              gws[:, kc * 8:(kc + 1) * 8],
                              start=(kc == 0), stop=(kc == 7))
            e = PE.ev_last()
            k.wait(DVE, e)
            Ls = [pb[7][:, tc * 8:(tc + 1) * 8] for tc in range(NTC)]
            eqt = [eqs[:, tc * 8:(tc + 1) * 8] for tc in range(NTC)]
            for tc in range(NTC):
                DVE.tensor_reduce(v1c[:, tc:tc + 1], Ls[tc], AX.X, ALU.max)
            k.flush_dve(nc)
            for tc in range(NTC):
                DVE.tensor_scalar(eqt[tc], Ls[tc], v1c[:, tc:tc + 1], None, ALU.is_equal)
            k.flush_dve(nc)
            for tc in range(NTC):
                DVE.tensor_mul(eqt[tc], eqt[tc], iotar)
            k.flush_dve(nc)
            for tc in range(NTC):
                DVE.tensor_reduce(i1c[:, tc:tc + 1], eqt[tc], AX.X, ALU.max)
            k.flush_dve(nc)
            for tc in range(NTC):
                DVE.tensor_scalar(eqt[tc], iotar, i1c[:, tc:tc + 1], None, ALU.is_equal)
            k.flush_dve(nc)
            for tc in range(NTC):
                DVE.scalar_tensor_tensor(eqt[tc], eqt[tc], -1e30, Ls[tc],
                                         ALU.mult, ALU.add)
            k.flush_dve(nc)
            for tc in range(NTC):
                DVE.tensor_reduce(v2c[:, tc:tc + 1], eqt[tc], AX.X, ALU.max)
            k.flush_dve(nc)
            for tc in range(NTC):
                DVE.tensor_scalar(eqt[tc], eqt[tc], v2c[:, tc:tc + 1], None, ALU.is_equal)
            k.flush_dve(nc)
            for tc in range(NTC):
                DVE.tensor_mul(eqt[tc], eqt[tc], iotar)
            k.flush_dve(nc)
            for tc in range(NTC):
                DVE.tensor_reduce(i2c[:, tc:tc + 1], eqt[tc], AX.X, ALU.max)
            k.flush_dve(nc)
            e = k.mark(DVE.tensor_sub(g2c, v2c, v1c), "dve")
            k.wait(ACT, e)
            ACT.activation(g1c, g2c, AF.Sigmoid, scale=-1.0)
            e_gt = k.mark(ACT.activation(g2c, g2c, AF.Sigmoid), "act")
            k.wait(DVE, e_gt)
            wq = wcols.rearrange("p a b -> p (a b)")
            for tc in range(NTC):
                for e_i in range(N_EXP):
                    DVE.tensor_scalar(wq[:, tc * 8 + e_i:tc * 8 + e_i + 1],
                                      i1c[:, tc:tc + 1], float(999 - e_i), None,
                                      ALU.is_equal)
                    DVE.tensor_scalar(eqt[tc][:, e_i:e_i + 1],
                                      i2c[:, tc:tc + 1], float(999 - e_i), None,
                                      ALU.is_equal)
            k.flush_dve(nc)
            for tc in range(NTC):
                DVE.tensor_scalar(wq[:, tc * 8:(tc + 1) * 8],
                                  wq[:, tc * 8:(tc + 1) * 8], g1c[:, tc:tc + 1],
                                  None, ALU.mult)
                DVE.tensor_scalar(eqt[tc], eqt[tc], g2c[:, tc:tc + 1], None, ALU.mult)
            k.flush_dve(nc)
            for tc in range(NTC):
                DVE.tensor_add(wq[:, tc * 8:(tc + 1) * 8],
                               wq[:, tc * 8:(tc + 1) * 8], eqt[tc])
            k.flush_dve(nc)
            e_wc = k.mark(DVE.engine_nop(), "dve")

            if s == 0 and "d_yba0" in dbg:
                k.wait(GP, e_wc, e_yb)
                k.mark(GP.dma_start(out=dbg["d_yba0"][0:128, :], in_=yba[:, 0, :]), "dG", 16)
                k.mark(GP.dma_start(out=dbg["d_wc"][0:128, 0:N_EXP * NTC],
                                    in_=wcols.rearrange("p a b -> p (a b)")), "dG", 16)
                k.mark(GP.dma_start(out=dbg["d_t2"][0:128, :],
                                    in_=t2.rearrange("p a b -> p (a b)")[:, 0:1024]), "dG", 16)
                k.mark(GP.dma_start(out=dbg["d_tv"][0:128, 0:NTC],
                                    in_=v1c), "dG", 16)
                k.mark(GP.dma_start(out=dbg["d_tv"][0:128, NTC:2 * NTC],
                                    in_=v2c), "dG", 16)
                k.mark(GP.dma_start(out=dbg["d_tv"][0:128, 2 * NTC:3 * NTC],
                                    in_=i1c), "dG", 16)
                k.mark(GP.dma_start(out=dbg["d_tv"][0:128, 3 * NTC:4 * NTC],
                                    in_=i2c), "dG", 16)
                k.mark(GP.dma_start(out=dbg["d_tv"][0:128, 4 * NTC:5 * NTC],
                                    in_=g1c), "dG", 16)
                k.mark(GP.dma_start(out=dbg["d_tv"][0:128, 5 * NTC:6 * NTC],
                                    in_=g2c), "dG", 16)
                k.wait(DVE, k.ev("dG"))
            we1 = [din["we1"][e_i].rearrange("(kc p) hh -> p kc hh", p=128)
                   for e_i in range(N_EXP)]
            we2 = [din["we2"][e_i].rearrange("(kc p) cc -> p kc cc", p=128)
                   for e_i in range(N_EXP)]
            if s == 0:
                w1use = {0: k.ev("pe"), 1: k.ev("pe")}
                w2use = {0: k.ev("pe"), 1: k.ev("pe")}
                siluev = {0: None, 1: None}
                e_cmb = None
            for e_i in range(N_EXP):
                e_hid = None
                for hq in range(8):
                    slot = hq % 2
                    bp0 = (6 if hq % 2 == 0 else 4) if PEPAIR else 6
                    g = w1use.get(slot) if SWGE else k.ev("pe")
                    if g is not None:
                        k.wait(sync, g)
                    e_w1 = k.mark(sync.dma_start(
                        out=w1b[:, slot, :, :],
                        in_=we1[e_i][:, :, hq * 256:(hq + 1) * 256]), "dS", 16)
                    k.wait(PE, e_w1, e_wc,
                           siluev[hq % 2] if PEPAIR else k.ev("act"))
                    for oc2 in range(2):
                        for kc in range(8):
                            PE.matmul(pb[bp0 + oc2][:, 0:SE],
                                      w1b[:, slot, kc, oc2 * 128:(oc2 + 1) * 128],
                                      t2[:, kc, :],
                                      start=(kc == 0), stop=(kc == 7))
                    e = PE.ev_last()
                    w1use[slot] = e
                    k.wait(ACT, e)
                    ACT.activation(hid[:, 2 * hq, :], pb[bp0][:, 0:SE], AF.Silu)
                    e_hid = k.mark(ACT.activation(hid[:, 2 * hq + 1, :],
                                                  pb[bp0 + 1][:, 0:SE], AF.Silu), "act")
                    siluev[hq % 2] = e_hid
                for ch2 in range(2):
                    for kg in range(4):
                        slot = kg % 2
                        g = w2use.get(slot) if SWGE else k.ev("pe")
                        if g is not None:
                            k.wait(sync, g)
                        e_w2 = k.mark(sync.dma_start(
                            out=w2b[:, slot, :, :],
                            in_=we2[e_i][:, kg * 4:kg * 4 + 4,
                                         ch2 * 512:(ch2 + 1) * 512]), "dS", 16)
                        k.wait(PE, e_w2, e_hid, e_cmb)
                        for tc in range(NTC):
                            for kq in range(4):
                                kc2 = kg * 4 + kq
                                PE.matmul(pb[tc][:, 0:512],
                                          hid[:, kc2, tc * 128:(tc + 1) * 128],
                                          w2b[:, slot, kq, :],
                                          start=(kc2 == 0), stop=(kc2 == 15))
                        e_ye = PE.ev_last()
                        w2use[slot] = e_ye
                    k.wait(DVE, e_ye)
                    for tc in range(NTC):
                        DVE.scalar_tensor_tensor(
                            yba[:, tc, ch2 * 512:(ch2 + 1) * 512], pb[tc][:, 0:512],
                            wcols[:, tc, e_i:e_i + 1],
                            yba[:, tc, ch2 * 512:(ch2 + 1) * 512], ALU.mult, ALU.add)
                    e_cmb = k.mark(DVE.engine_nop(), "dve")
                if s == 0 and e_i == 0 and "d_yba1" in dbg:
                    k.wait(GP, e_cmb)
                    k.mark(GP.dma_start(out=dbg["d_yba1"][0:128, :], in_=yba[:, 0, :]), "dG", 16)
                    k.wait(DVE, k.ev("dG"))
            k.wait(GP, e_cmb, e_yb)
            k.mark(GP.dma_start(out=outv[:, s * NTC:(s + 1) * NTC, :],
                                in_=yba), "dG", 16)
            k.wait(DVE, k.ev("dG"))
    finally:
        es.close()


_BUILT = {}


def _get_program():
    key = "full"
    if key not in _BUILT:
        _BUILT[key] = build_program()
    return _BUILT[key]


def make_in_map(inputs, core, T=2048):
    g = lambda n: np.ascontiguousarray(np.asarray(inputs[n], dtype=np.float32))
    m = dict()
    m["xT"] = np.ascontiguousarray(g("L0_attn_x")[core].T)
    for nm, arr in host_consts().items():
        m[nm] = arr
    m["mix_r"] = g("L0_attn_x_r").ravel()
    m["mix_w"] = g("L0_attn_x_w").ravel()
    m["mix_k"] = g("L0_attn_x_k").ravel()
    m["mix_v"] = g("L0_attn_x_v").ravel()
    m["mix_a"] = g("L0_attn_x_a").ravel()
    m["mix_g"] = g("L0_attn_x_g").ravel()
    m["w0"] = g("L0_attn_w0").ravel()
    m["a0"] = g("L0_attn_a0").ravel()
    m["k_k"] = g("L0_attn_k_k").ravel()
    m["k_a"] = g("L0_attn_k_a").ravel()
    m["r_k"] = g("L0_attn_r_k").ravel()
    m["ln_g"] = g("L0_attn_ln_gamma").ravel()
    m["ln_b"] = g("L0_attn_ln_beta").ravel()
    m["w1"] = g("L0_attn_w1"); m["w2"] = g("L0_attn_w2")
    m["a1"] = g("L0_attn_a1"); m["a2"] = g("L0_attn_a2")
    m["g1"] = g("L0_attn_g1"); m["g2"] = g("L0_attn_g2")
    m["Wr"] = g("L0_attn_Wr"); m["Wk"] = g("L0_attn_Wk")
    m["Wv"] = g("L0_attn_Wv"); m["Wo"] = g("L0_attn_Wo")
    m["fg1"] = g("L0_ffn_g1"); m["fg2"] = g("L0_ffn_g2")
    m["Wq"] = g("L0_ffn_Wq"); m["Wdkv"] = g("L0_ffn_Wdkv")
    m["Wuk"] = g("L0_ffn_Wuk"); m["Wuv"] = g("L0_ffn_Wuv")
    m["Wof"] = g("L0_ffn_Wo"); m["gate_w"] = g("L0_ffn_gate_w")
    m["we1"] = g("L0_ffn_w1"); m["we2"] = g("L0_ffn_w2")
    return m


def kernel(**inputs):
    nc, _ = _get_program()
    in_maps = [make_in_map(inputs, c) for c in range(8)]
    res = run_bass_kernel_spmd(nc, in_maps, core_ids=list(range(8)))
    return np.stack([res.results[c]["out"] for c in range(8)], axis=0)
